# version 16
# baseline (speedup 1.0000x reference)
import numpy as np
import concourse.bass as bass
import concourse.mybir as mybir
from concourse.bass_utils import run_bass_kernel_spmd
from contextlib import ExitStack

F32 = mybir.dt.float32
BF16 = mybir.dt.bfloat16
F16 = mybir.dt.float16
AF = mybir.ActivationFunctionType
ALU = mybir.AluOpType
AX = mybir.AxisListType.X

NEG_PUSH = -1000.0
BIG_F = 1.0e9


def chunks(total, size):
    out, o = [], 0
    while o < total:
        w = min(size, total - o)
        out.append((o, w))
        o += w
    return out


class Cfg:
    def __init__(self, H=4096, I=11008, TOK=1024, TG=512, E=8, R=16,
                 B_ITERS=8, FIXUP=True, N_CORES=8):
        self.H, self.I, self.TOK, self.TG, self.E, self.R = H, I, TOK, TG, E, R
        self.B_ITERS, self.FIXUP, self.N_CORES = B_ITERS, FIXUP, N_CORES
        self.HC = H // 128
        self.GROUPS = TOK // TG
        self.SUBS = TG // 128
        self.K = I // 2
        self.IGS = chunks(I, 512)
        self.NBLK = I // 128
        self.HCH = chunks(H, 512)
        self.ER = E * R
        self.F0W = E + self.ER
        self.CBS = chunks(self.HC, 4)
        self.F0CB = chunks(self.HC, 8)
        self.BBS = chunks(self.NBLK, 4)
        self.LNH = chunks(H, 1024)



def pack_weights(cfg, w_gate, w_up, w_down, W_base, b_base, W_router, b_router,
                 lora_A, lora_B):
    H, I = cfg.H, cfg.I

    def pack_ck(wT, colchunks, cbs):
        w3 = wT.reshape(cfg.HC, 128, wT.shape[1])
        parts = []
        for (i0, iw) in colchunks:
            for (c0, nc) in cbs:
                t = w3[c0:c0 + nc, :, i0:i0 + iw]
                parts.append(np.ascontiguousarray(t.transpose(1, 0, 2)).ravel())
        return np.concatenate(parts)

    wgT = np.ascontiguousarray(w_gate.T)
    wuT = np.ascontiguousarray(w_up.T)
    wgp = pack_ck(wgT, cfg.IGS, cfg.CBS)
    wup = pack_ck(wuT, cfg.IGS, cfg.CBS)

    wdT = np.ascontiguousarray(w_down.T)
    w3 = wdT.reshape(cfg.NBLK, 128, H)
    parts = []
    for (h0, hw) in cfg.HCH:
        for (b0, nb) in cfg.BBS:
            t = w3[b0:b0 + nb, :, h0:h0 + hw]
            parts.append(np.ascontiguousarray(t.transpose(1, 0, 2)).ravel())
    wdp = np.concatenate(parts)

    A_r = np.ascontiguousarray(lora_A.transpose(1, 0, 2)).reshape(H, cfg.ER)
    wra = np.concatenate([np.ascontiguousarray(W_router.T), A_r], axis=1)
    wrap_ = pack_ck(np.ascontiguousarray(wra), [(0, cfg.F0W)], cfg.F0CB)

    wbp = pack_ck(np.ascontiguousarray(W_base.T), cfg.HCH, cfg.CBS)

    bflat = np.ascontiguousarray(lora_B.reshape(cfg.ER, H))
    brow_wra = np.concatenate([b_router, np.zeros(cfg.ER, np.float32)])[None, :]
    brow_wb = np.ascontiguousarray(b_base[None, :])
    ident = np.eye(128, dtype=np.float32)
    iota9 = np.tile(np.arange(9, dtype=np.float32)[None, :], (128, 1))
    iotaE = np.tile(np.arange(1, cfg.E + 1, dtype=np.float32)[None, :],
                    (128, cfg.SUBS))
    onesrow = np.ones((1, 128), dtype=np.float32)
    return dict(wgp=wgp, wup=wup, wdp=wdp, wrap=wrap_, wbp=wbp, bflat=bflat,
                brow_wra=np.ascontiguousarray(brow_wra, np.float32),
                brow_wb=brow_wb.astype(np.float32),
                ident=ident, iota9=iota9, iotaE=iotaE, onesrow=onesrow)


def _off_ck(cfg, colchunks, cbs, ci, cbi):
    off = 0
    for i2, (i0, iw) in enumerate(colchunks):
        for c2, (c0, nc) in enumerate(cbs):
            if i2 == ci and c2 == cbi:
                return off
            off += 128 * nc * iw
    raise KeyError



class Em:
    ENGS = ("T", "V", "S", "G", "X")
    DSEMS = {"AR0": "G", "AR1": "G", "AR2": "G", "AR3": "G", "CG": "G",
             "XZ": "X", "XZ2": "X", "XQ": "X", "OB0": "X", "OB1": "X",
             "CX": "X"}

    def __init__(self):
        self.prog = {e: [] for e in self.ENGS}
        self.cnt = {e: 0 for e in self.ENGS}
        for s in self.DSEMS:
            self.cnt[s] = 0
        self.waited = {e: {} for e in self.ENGS}
        self.lastw = {}
        self.readers = {}

    def _emit_waits(self, eng, evts):
        need = {}
        for (se, v) in evts:
            if self.DSEMS.get(se) == eng:
                continue
            if v > self.waited[eng].get(se, 0):
                need[se] = max(need.get(se, 0), v)
        for se, v in need.items():
            self.waited[eng][se] = v
            self.prog[eng].append(("w", se, v))

    def _collect(self, reads, writes):
        evts = []
        for r in reads:
            ev = self.lastw.get(r)
            if ev is not None:
                evts.append(ev)
        for w in writes:
            ev = self.lastw.get(w)
            if ev is not None:
                evts.append(ev)
            evts.extend(self.readers.get(w, ()))
        return evts

    def _register(self, ev, reads, writes):
        for r in reads:
            self.readers.setdefault(r, []).append(ev)
        for w in writes:
            self.lastw[w] = ev
            self.readers[w] = []

    def op(self, eng, fn, reads=(), writes=(), inc=1):
        self._emit_waits(eng, self._collect(reads, writes))
        self.cnt[eng] += inc
        self.prog[eng].append(("i", fn, inc, eng))
        ev = (eng, self.cnt[eng])
        self._register(ev, reads, writes)
        return ev

    def dma(self, sem, fn, reads=(), writes=()):
        qeng = self.DSEMS[sem]
        self._emit_waits(qeng, self._collect(reads, writes))
        self.cnt[sem] += 16
        self.prog[qeng].append(("i", fn, 16, sem))
        ev = (sem, self.cnt[sem])
        self._register(ev, reads, writes)
        return ev

    def wait_only(self, eng, reads=(), writes=()):
        self._emit_waits(eng, self._collect(reads, writes))

    def raw_inc(self, eng, fn, reads=(), writes=(), inc=1):
        self.cnt[eng] += inc
        self.prog[eng].append(("i", fn, inc, eng))
        ev = (eng, self.cnt[eng])
        self._register(ev, reads, writes)
        return ev

    def force_event(self, region, ev):
        self.lastw[region] = ev
        self.readers[region] = []

    def replay(self, eng_name, eng_obj, sems):
        for item in self.prog[eng_name]:
            if item[0] == "w":
                eng_obj.wait_ge(sems[item[1]], item[2])
            else:
                inst = item[1](eng_obj)
                if item[2]:
                    inst.then_inc(sems[item[3]], item[2])



def build(cfg):
    nc = bass.Bass("TRN2", target_bir_lowering=False)
    H, I, E, R, ER, F0W = cfg.H, cfg.I, cfg.E, cfg.R, cfg.ER, cfg.F0W
    SUBS, GROUPS, K = cfg.SUBS, cfg.GROUPS, cfg.K
    nIG = len(cfg.IGS)
    nh = len(cfg.LNH)

    def din(name, shape):
        return nc.dram_tensor(name, list(shape), F32, kind="ExternalInput").ap()

    x_h = din("x", (cfg.TOK, H))
    wgp_h = din("wgp", (H * I,))
    wup_h = din("wup", (H * I,))
    wdp_h = din("wdp", (I * H,))
    wrap_h = din("wrap", (H * F0W,))
    wbp_h = din("wbp", (H * H,))
    bflat_h = din("bflat", (ER, H))
    browa_h = din("brow_wra", (1, F0W))
    browb_h = din("brow_wb", (1, H))
    ident_h = din("ident", (128, 128))
    iota9_h = din("iota9", (128, 9))
    iotaE_h = din("iotaE", (128, E * SUBS))
    ones_h = din("onesrow", (1, 128))
    out_h = nc.dram_tensor("out", [cfg.TOK, H], F32, kind="ExternalOutput").ap()

    em = Em()
    ctx = ExitStack()
    with ctx:
        def sbuf(name, shape, dt):
            return ctx.enter_context(nc.sbuf_tensor(name, list(shape), dt))

        s_buf = sbuf("s_buf", (128, SUBS * I), F16)
        xnT = sbuf("xnT", (128, cfg.HC * 128 * SUBS), F16)
        g_sb = sbuf("g_sb", (128, SUBS * H), F16)
        xz = sbuf("xz", (128, 1024), F32)
        ar = [sbuf("ar%d" % i_, (128, 2048), F16) for i_ in range(4)]
        bflat = sbuf("bflat_sb", (128, H), F16)
        mch_a = sbuf("mch_a", (128, 512), F16)
        mch_b = sbuf("mch_b", (128, 512), F16)
        mscr = sbuf("mscr", (128, 512), F32)
        ascr = mch_a
        bscr = mch_b
        qscr = sbuf("qscr", (128, 512), F32)
        ob = [mscr, qscr]
        cm8 = sbuf("cm8", (128, 8 * nIG), F32)
        fm8 = sbuf("fm8", (128, 8), F32)
        bv9 = sbuf("bv9", (128, 9), F32)
        oh9 = sbuf("oh9", (128, 9), F32)
        st_lo = sbuf("st_lo", (128, SUBS), F32)
        st_hi = sbuf("st_hi", (128, SUBS), F32)
        st_mid = sbuf("st_mid", (128, SUBS), F32)
        st_nmid = sbuf("st_nmid", (128, SUBS), F32)
        st_cp = sbuf("st_cp", (128, SUBS), F32)
        st_cn = sbuf("st_cn", (128, SUBS), F32)
        st_cnt = sbuf("st_cnt", (128, SUBS), F32)
        st_msk = sbuf("st_msk", (128, SUBS), mybir.dt.uint8)
        st_mski = sbuf("st_mski", (128, SUBS), mybir.dt.uint8)
        st_flo = sbuf("st_flo", (128, SUBS), F32)
        st_th = sbuf("st_th", (128, SUBS), F32)
        st_nth = sbuf("st_nth", (128, SUBS), F32)
        st_tmp = sbuf("st_tmp", (128, SUBS), F32)
        st_d = sbuf("st_d", (128, SUBS), F32)
        st_hi0 = sbuf("st_hi0", (128, SUBS), F32)
        st_fhi = sbuf("st_fhi", (128, SUBS), F32)
        st_w = sbuf("st_w", (128, SUBS), F32)
        st_num = sbuf("st_num", (128, SUBS), F32)
        st_den = sbuf("st_den", (128, SUBS), F32)
        st_trA = sbuf("st_trA", (128, SUBS), F32)
        st_trB = sbuf("st_trB", (128, SUBS), F32)
        st_trC = sbuf("st_trC", (128, SUBS), F32)
        st_tr1 = sbuf("st_tr1", (128, SUBS), F32)
        st_tr2 = sbuf("st_tr2", (128, SUBS), F32)
        st_floi = sbuf("st_floi", (128, SUBS), F32)
        st_fhii = sbuf("st_fhii", (128, SUBS), F32)
        st_pmsk = sbuf("st_pmsk", (128, SUBS), F32)
        st_pmski = sbuf("st_pmski", (128, SUBS), F32)
        st_msk2 = sbuf("st_msk2", (128, SUBS), mybir.dt.uint8)
        rmscr = sbuf("rmscr", (128, SUBS * nIG), F32)
        snkV = sbuf("snkV", (128, 1), F16)
        snkS = sbuf("snkS", (128, 1), F16)
        ln_sm = sbuf("ln_sm", (128, nh), F32)
        ln_sq = sbuf("ln_sq", (128, nh), F32)
        ln_mean = sbuf("ln_mean", (128, 1), F32)
        ln_var = sbuf("ln_var", (128, 1), F32)
        ln_rstd = sbuf("ln_rstd", (128, SUBS), F32)
        ln_nb = sbuf("ln_nb", (128, SUBS), F32)
        lg = sbuf("lg", (128, SUBS * E), F32)
        lgs = sbuf("lgs", (128, SUBS * E), F32)
        zc = sbuf("zc", (128, SUBS * E), F32)
        zc2 = sbuf("zc2", (128, SUBS * E), F32)
        sup = sbuf("sup", (128, SUBS * E), F32)
        supn = sbuf("supn", (128, SUBS * E), F32)
        kz = sbuf("kz", (128, SUBS), F32)
        kzr = sbuf("kzr", (128, SUBS), F32)
        tau = sbuf("tau", (128, SUBS), F32)
        wrt = sbuf("wrt", (128, SUBS * E), F32)
        srt_t = sbuf("srt_t", (128, SUBS * 4), F32)
        h_sb = sbuf("h_sb", (128, SUBS * ER), F16)
        hw_scr = sbuf("hw_scr", (128, ER), F32)
        hwT = sbuf("hwT", (128, SUBS * ER), F16)
        identF = sbuf("identF", (128, 128), F32)
        iota9_s = sbuf("iota9_s", (128, 9), F32)
        iotaE_s = sbuf("iotaE_s", (128, E * SUBS), F32)
        ones_sb = sbuf("ones_sb", (1, 128), F16)
        brow_a = sbuf("brow_a", (1, F0W), F16)
        brow_b = sbuf("brow_b", (1, H), F16)
        xq = sbuf("xq", (128, SUBS * 512), F32)
        t1scr = xq
        eps_ap = sbuf("eps_ap", (128, 1), F32)
        nhalf_ap = sbuf("nhalf_ap", (128, 1), F32)

        PB = [ctx.enter_context(nc.psum_tensor(f"PB{i}", [128, 512], F32))
              for i in range(8)]

        ALLS = [("s", s_, i_) for s_ in range(SUBS) for i_ in range(nIG)]

        em.dma("CX", lambda e: e.dma_start(out=identF[:], in_=ident_h))
        em.dma("CX", lambda e: e.dma_start(out=iota9_s[:], in_=iota9_h))
        em.dma("CX", lambda e: e.dma_start(out=iotaE_s[:], in_=iotaE_h))
        em.dma("CG", lambda e: e.dma_start(out=ones_sb[:], in_=ones_h))
        em.dma("CG", lambda e: e.dma_start(out=brow_a[:], in_=browa_h))
        em.dma("CG", lambda e: e.dma_start(out=brow_b[:], in_=browb_h))
        em.dma("CG", lambda e: e.dma_start(out=bflat[:], in_=bflat_h))
        for reg in [("identF",), ("iota9",), ("iotaE",)]:
            em.force_event(reg, ("CX", em.cnt["CX"]))
        for reg in [("ones",), ("browa",), ("browb",), ("bflat",)]:
            em.force_event(reg, ("CG", em.cnt["CG"]))

        em.op("V", lambda e: e.memset(eps_ap[:], 1e-5), writes=[("epsap",)])
        em.op("V", lambda e: e.memset(nhalf_ap[:], -0.5), writes=[("nhalfap",)])

        wslot = {"k": 0}

        def stream_tile(src_1d, off, nelem, d1, d2):
            k = wslot["k"] % 4
            wslot["k"] += 1
            dst = ar[k][:, 0:d1 * d2].rearrange("p (a b) -> p a b", a=d1)
            src = src_1d[off:off + nelem].rearrange("(p a b) -> p a b",
                                                    p=128, a=d1)
            em.dma("AR%d" % k, lambda e, d=dst, s=src: e.dma_start(out=d, in_=s),
                   writes=[("AR", k)])
            return k

        def arslice(k, d1, d2, j):
            return ar[k][:, 0:d1 * d2].rearrange("p (a b) -> p a b", a=d1)[:, j, :]

        def _ln_sub_emitters(grp2, bank_base=6, bufs=None):
            tok02 = grp2 * cfg.TG
            if bufs is None:
                bufs = [(xz, (("XZ",),), "XZ"), (xz, (("XZ",),), "XZ")]
            fns = []
            for sub in range(SUBS):
                def fn_a(sub=sub, **kw):
                    r0 = tok02 + sub * 128
                    for hi_, (h0, hw_) in enumerate(cfg.LNH):
                        buf, bkeys, bsem = bufs[hi_ % len(bufs)]
                        em.dma(bsem, lambda e, h0=h0, hw_=hw_, r0=r0, buf=buf:
                               e.dma_start(out=buf[:, 0:hw_],
                                           in_=x_h[r0:r0 + 128, h0:h0 + hw_]),
                               writes=list(bkeys))
                        em.op("V", lambda e, hw_=hw_, c=hi_, buf=buf:
                              e.reduce_sum(ln_sm[:, c:c + 1], buf[:, 0:hw_], AX),
                              reads=list(bkeys), writes=[("lnsm", hi_)])
                        em.op("S", lambda e, hw_=hw_, c=hi_, buf=buf:
                              e.activation(snkS[:, 0:1].broadcast_to([128, hw_]),
                                           buf[:, 0:hw_], AF.Square,
                                           accum_out=ln_sq[:, c:c + 1]),
                              reads=list(bkeys), writes=[("lnsq", hi_), ("snkS",)])
                    em.op("V", lambda e: e.reduce_sum(ln_mean[:, 0:1],
                                                      ln_sm[:, 0:nh], AX),
                          reads=[("lnsm", i_) for i_ in range(nh)],
                          writes=[("lnmean",)])
                    em.op("V", lambda e: e.tensor_scalar(
                        ln_mean[:, 0:1], ln_mean[:, 0:1], 1.0 / H, None, ALU.mult),
                        reads=[("lnmean",)], writes=[("lnmean",)])
                    em.op("V", lambda e: e.reduce_sum(ln_var[:, 0:1],
                                                      ln_sq[:, 0:nh], AX),
                          reads=[("lnsq", i_) for i_ in range(nh)],
                          writes=[("lnvar",)])
                    em.op("V", lambda e: e.tensor_scalar(
                        ln_var[:, 0:1], ln_var[:, 0:1], 1.0 / H, None, ALU.mult),
                        reads=[("lnvar",)], writes=[("lnvar",)])
                    em.op("V", lambda e: e.tensor_tensor(
                        st_tmp[:, 0:1], ln_mean[:, 0:1], ln_mean[:, 0:1], ALU.mult),
                        reads=[("lnmean",)], writes=[("sttmp",)])
                    em.op("V", lambda e: e.tensor_tensor(
                        ln_var[:, 0:1], ln_var[:, 0:1], st_tmp[:, 0:1], ALU.subtract),
                        reads=[("lnvar",), ("sttmp",)], writes=[("lnvar",)])
                    em.op("S", lambda e: e.activation(
                        ln_rstd[:, 0:1], ln_var[:, 0:1], AF.Ln, bias=eps_ap[:, 0:1]),
                        reads=[("lnvar",), ("epsap",)], writes=[("lnr0",)])
                    em.op("S", lambda e, sub=sub: e.activation(
                        ln_rstd[:, sub:sub + 1], ln_rstd[:, 0:1], AF.Exp,
                        scale=nhalf_ap[:, 0:1]),
                        reads=[("lnr0",), ("nhalfap",)],
                        writes=[("lnrstd", sub), ("lnr0",)])
                    em.op("V", lambda e, sub=sub: e.tensor_tensor(
                        ln_nb[:, sub:sub + 1], ln_mean[:, 0:1],
                        ln_rstd[:, sub:sub + 1], ALU.mult),
                        reads=[("lnmean",), ("lnrstd", sub)], writes=[("lnnb", sub)])
                    em.op("V", lambda e, sub=sub: e.tensor_scalar(
                        ln_nb[:, sub:sub + 1], ln_nb[:, sub:sub + 1], -1.0, None,
                        ALU.mult),
                        reads=[("lnnb", sub)], writes=[("lnnb", sub)])
                def fn_b(sub=sub, bank_base=bank_base, **kw):
                    bank_base = kw.get("bank_base", bank_base)
                    r0 = tok02 + sub * 128
                    tp_par = 0
                    for hi_, (h0, hw_) in enumerate(cfg.LNH):
                        buf, bkeys, bsem = bufs[hi_ % len(bufs)]
                        em.dma(bsem, lambda e, h0=h0, hw_=hw_, r0=r0, buf=buf:
                               e.dma_start(out=buf[:, 0:hw_],
                                           in_=x_h[r0:r0 + 128, h0:h0 + hw_]),
                               writes=list(bkeys))
                        em.op("V", lambda e, hw_=hw_, sub=sub, buf=buf:
                              e.tensor_scalar(
                            buf[:, 0:hw_], buf[:, 0:hw_], ln_rstd[:, sub:sub + 1],
                            ln_nb[:, sub:sub + 1], ALU.mult, op1=ALU.add),
                            reads=list(bkeys) + [("lnnb", sub), ("lnrstd", sub)],
                            writes=list(bkeys))
                        for (b0, nb) in chunks(hw_ // 128, 4):
                            bank = bank_base + (tp_par % 2)
                            tp_par += 1
                            em.wait_only("T", reads=list(bkeys) + [("identF",)],
                                         writes=[("PB", bank)])
                            for j in range(nb):
                                em.raw_inc("T", lambda e, b=b0 + j, j=j,
                                           bank=bank, buf=buf:
                                           e.transpose(
                                               PB[bank][:, j * 128:(j + 1) * 128],
                                               buf[:, b * 128:(b + 1) * 128],
                                               identF[:]),
                                           reads=list(bkeys) + [("identF",)],
                                           writes=[("PB", bank)],
                                           inc=(1 if j == nb - 1 else 0))
                            cbase = h0 // 128 + b0
                            em.op("V", lambda e, cbase=cbase, nb=nb, sub=sub,
                                  bank=bank:
                                  e.tensor_copy(
                                      xnT[:, (cbase * SUBS + sub) * 128:]
                                      .rearrange("p (a b) -> p a b", b=128)
                                      [:, 0:(nb - 1) * SUBS + 1:SUBS, :],
                                      PB[bank][:, 0:nb * 128]
                                      .rearrange("p (a b) -> p a b", b=128)),
                                  reads=[("PB", bank)], writes=[("xnT", sub)])

                fns.append(fn_a)
                fns.append(fn_b)
            return fns

        for grp in range(GROUPS):
            tok0 = grp * cfg.TG

            if grp == 0:
                if SUBS * 512 >= 2048:
                    g0bufs = [(xq[:, 0:1024], (("xq", 0), ("xq", 1)),
                               "XZ"),
                              (xq[:, 1024:2048], (("xq", 2), ("xq", 3)),
                               "XZ2")]
                else:
                    g0bufs = None
                for f_ in _ln_sub_emitters(0, bufs=g0bufs):
                    f_()
            XNT_R = [("xnT", s_) for s_ in range(SUBS)]

            for igi, (i0, iw) in enumerate(cfg.IGS):
                banks = [s_ + 4 * (igi % 2) for s_ in range(SUBS)]
                for cbi, (c0, nc8) in enumerate(cfg.CBS):
                    off = _off_ck(cfg, cfg.IGS, cfg.CBS, igi, cbi)
                    k = stream_tile(wgp_h, off, 128 * nc8 * iw, nc8, iw)
                    em.wait_only("T", reads=[("AR", k)] + XNT_R,
                                 writes=([("PB", b) for b in banks]
                                         if cbi == 0 else []))
                    last_cb = cbi == len(cfg.CBS) - 1
                    for c8 in range(nc8):
                        c = c0 + c8
                        for sub in range(SUBS):
                            st = (cbi == 0 and c8 == 0)
                            sp = (last_cb and c8 == nc8 - 1)
                            em.raw_inc(
                                "T",
                                lambda e, c=c, sub=sub, k=k, c8=c8, iw=iw,
                                       nc8=nc8, st=st, sp=sp, bank=banks[sub]:
                                e.matmul(PB[bank][:, 0:iw],
                                         xnT[:, (c * SUBS + sub) * 128:
                                             (c * SUBS + sub) * 128 + 128],
                                         arslice(k, nc8, iw, c8),
                                         start=st, stop=sp),
                                reads=[("AR", k)],
                                writes=([("PB", banks[sub])] if sp else []),
                                inc=(1 if (c8 == nc8 - 1 and sub == SUBS - 1)
                                     or sp else 0))
                for sub in range(SUBS):
                    bank = banks[sub]
                    em.op("S", lambda e, bank=bank, iw=iw, sub=sub, i0=i0:
                          e.activation(s_buf[:, sub * I + i0: sub * I + i0 + iw],
                                       PB[bank][:, 0:iw], AF.Silu),
                          reads=[("PB", bank)], writes=[("s", sub, igi)])
                    em.op("V", lambda e, sub=sub, i0=i0, iw=iw, igi=igi:
                          e.tensor_reduce(
                              rmscr[:, sub * nIG + igi: sub * nIG + igi + 1],
                              s_buf[:, sub * I + i0: sub * I + i0 + iw],
                              AX, ALU.max, apply_absolute_value=True),
                          reads=[("s", sub, igi)], writes=[("rm", sub, igi)])

            for fi, (c0, nc16) in enumerate(cfg.F0CB):
                off = _off_ck(cfg, [(0, F0W)], cfg.F0CB, 0, fi)
                k = stream_tile(wrap_h, off, 128 * nc16 * F0W, nc16, F0W)
                em.wait_only("T", reads=[("AR", k), ("browa",), ("ones",)] + XNT_R,
                             writes=([("PB", s_) for s_ in range(SUBS)]
                                     if fi == 0 else []))
                for c16 in range(nc16):
                    c = c0 + c16
                    for sub in range(SUBS):
                        st = (fi == 0 and c16 == 0)
                        em.raw_inc("T",
                                   lambda e, c=c, sub=sub, k=k, c16=c16,
                                          nc16=nc16, st=st:
                                   e.matmul(PB[sub][:, 0:F0W],
                                            xnT[:, (c * SUBS + sub) * 128:
                                                (c * SUBS + sub) * 128 + 128],
                                            arslice(k, nc16, F0W, c16),
                                            start=st, stop=False),
                                   reads=[("AR", k)], writes=[],
                                   inc=(1 if (c16 == nc16 - 1 and
                                              sub == SUBS - 1) else 0))
            for sub in range(SUBS):
                em.raw_inc("T", lambda e, sub=sub:
                           e.matmul(PB[sub][:, 0:F0W], ones_sb[:], brow_a[:],
                                    start=False, stop=True),
                           reads=[("browa",), ("ones",)],
                           writes=[("PB", sub)], inc=1)
                em.op("V", lambda e, sub=sub: e.tensor_copy(
                    lg[:, sub * E:(sub + 1) * E], PB[sub][:, 0:E]),
                    reads=[("PB", sub)], writes=[("lg", sub)])
                em.op("V", lambda e, sub=sub: e.tensor_copy(
                    h_sb[:, sub * ER:(sub + 1) * ER], PB[sub][:, E:E + ER]),
                    reads=[("PB", sub)], writes=[("hsb", sub)])

            _sparsemax(em, cfg, lg, lgs, zc, zc2, sup, supn, kz, kzr, tau, wrt,
                       srt_t, iotaE_s)

            for sub in range(SUBS):
                em.op("V", lambda e, sub=sub: e.tensor_tensor(
                    hw_scr[:, 0:ER].rearrange("p (a b) -> p a b", a=E),
                    h_sb[:, sub * ER:(sub + 1) * ER]
                    .rearrange("p (a b) -> p a b", a=E),
                    wrt[:, sub * E:(sub + 1) * E][:, :, None]
                    .broadcast_to([128, E, R]),
                    ALU.mult),
                    reads=[("hsb", sub), ("wrt",)], writes=[("hwscr",)])
                bank = 4 + (sub % 2)
                em.op("T", lambda e, bank=bank: e.transpose(
                    PB[bank][:, 0:128], hw_scr[:, 0:128], identF[:]),
                    reads=[("hwscr",), ("identF",)], writes=[("PB", bank)])
                em.op("V", lambda e, sub=sub, bank=bank: e.tensor_copy(
                    hwT[:, sub * ER:(sub + 1) * ER], PB[bank][:, 0:ER]),
                    reads=[("PB", bank)], writes=[("hwT", sub)])

            for sub in range(SUBS):
                em.op("V", lambda e, sub=sub: e.tensor_reduce(
                    st_hi[:, sub:sub + 1], rmscr[:, sub * nIG:(sub + 1) * nIG],
                    AX, ALU.max),
                    reads=[("rm", sub, i_) for i_ in range(nIG)],
                    writes=[("sthi",)])
            em.op("V", lambda e: e.tensor_copy(st_hi0[:], st_hi[:]),
                  reads=[("sthi",)], writes=[("sthi0",)])
            em.op("V", lambda e: e.memset(st_lo[:], 0.0), writes=[("stlo",)])
            em.op("V", lambda e: e.memset(st_flo[:], BIG_F), writes=[("stflo",)])

            base_rounds = _base_rounds(em, cfg, stream_tile, arslice, wbp_h, PB,
                                       xnT, ones_sb, brow_b, hwT, bflat, g_sb,
                                       XNT_R)
            def _tr_init(av, bv, cv):
                for c_, v_ in enumerate(av):
                    em.op("V", lambda e, c_=c_, v_=v_: e.memset(
                        st_trA[:, c_:c_ + 1], v_), writes=[("trA",)])
                for c_, v_ in enumerate(bv):
                    em.op("V", lambda e, c_=c_, v_=v_: e.memset(
                        st_trB[:, c_:c_ + 1], v_), writes=[("trB",)])
                for c_, v_ in enumerate(cv):
                    em.op("V", lambda e, c_=c_, v_=v_: e.memset(
                        st_trC[:, c_:c_ + 1], float(v_)), writes=[("trC",)])
            st = dict(lo=st_lo, hi=st_hi, mid=st_mid, nmid=st_nmid, cp=st_cp,
                      cn=st_cn, cnt=st_cnt, msk=st_msk, mski=st_mski,
                      flo=st_flo, fhi=st_fhi, hi0=st_hi0, w=st_w, num=st_num,
                      den=st_den, trA=st_trA, trB=st_trB, trC=st_trC,
                      tr1=st_tr1, tr2=st_tr2, floi=st_floi, fhii=st_fhii,
                      pmsk=st_pmsk, pmski=st_pmski, msk2=st_msk2,
                      th=st_th, _tr_init=_tr_init)
            bis_iters = _bisect_program(em, cfg, s_buf, st, snkV, snkS)
            tie_pick = bis_iters.pop()
            nb_, ni_ = len(base_rounds), len(bis_iters)
            bi = 0
            for ri, rfn in enumerate(base_rounds):
                rfn()
                if ri < 1:
                    continue
                want = int(np.ceil((ri - 1 + 1) * ni_ / max(nb_, 1)))
                while bi < min(want, ni_):
                    bis_iters[bi]()
                    bi += 1
            while bi < ni_:
                bis_iters[bi]()
                bi += 1

            if cfg.FIXUP:
                _fixup(em, cfg, s_buf, st_lo, st_hi, st_flo, st_th, st_d, ascr,
                       bscr, t1scr, qscr, cm8, fm8, bv9, oh9, iota9_s)
            else:
                em.op("V", lambda e: e.tensor_copy(st_th[:], st_hi[:]),
                      reads=[("sthi",)],
                      writes=[("stth", s_) for s_ in range(SUBS)])
            tie_pick()
            for sub in range(SUBS):
                em.op("V", lambda e, sub=sub: e.tensor_scalar(
                    st_nth[:, sub:sub + 1], st_th[:, sub:sub + 1], -1.0, None,
                    ALU.mult),
                    reads=[("stth", sub)], writes=[("stnth", sub)])

            def _ph2a_mask(igi, i0, iw):
                    for sub in range(SUBS):
                        em.op("V", lambda e, sub=sub, i0=i0, iw=iw: e.tensor_scalar(
                            mch_a[:, 0:iw],
                            s_buf[:, sub * I + i0: sub * I + i0 + iw],
                            st_th[:, sub:sub + 1], st_nth[:, sub:sub + 1],
                            ALU.min, op1=ALU.max),
                            reads=[("s", sub, igi), ("stth", sub),
                                   ("stnth", sub)],
                            writes=[("mcha",)])
                        em.op("V", lambda e, sub=sub, i0=i0, iw=iw: e.tensor_tensor(
                            mch_b[:, 0:iw],
                            s_buf[:, sub * I + i0: sub * I + i0 + iw],
                            mch_a[:, 0:iw], ALU.not_equal),
                            reads=[("s", sub, igi), ("mcha",)], writes=[("mchb",)])
                        em.op("V", lambda e, sub=sub, i0=i0, iw=iw: e.tensor_tensor(
                            mch_a[:, 0:iw], mch_b[:, 0:iw],
                            s_buf[:, sub * I + i0: sub * I + i0 + iw], ALU.mult),
                            reads=[("s", sub, igi), ("mchb",)], writes=[("mcha",)])
                        bank = sub + 4 * (igi % 2)
                        em.op("V", lambda e, sub=sub, iw=iw, bank=bank:
                              e.tensor_tensor(
                            mscr[:, 0:iw], mch_a[:, 0:iw], PB[bank][:, 0:iw],
                            ALU.mult),
                            reads=[("mcha",), ("PB", bank)], writes=[("OBK0",)])
                        nb = iw // 128
                        em.wait_only("T", reads=[("OBK0",), ("identF",)],
                                     writes=[("PB", bank)])
                        for j in range(nb):
                            em.raw_inc("T", lambda e, j=j, bank=bank:
                                       e.transpose(PB[bank][:, j * 128:(j + 1) * 128],
                                                   mscr[:, j * 128:(j + 1) * 128],
                                                   identF[:]),
                                       reads=[("OBK0",), ("identF",)],
                                       writes=[("PB", bank)],
                                       inc=(1 if j == nb - 1 else 0))
                        em.op("V", lambda e, sub=sub, i0=i0, iw=iw, bank=bank:
                              e.tensor_copy(
                                  s_buf[:, sub * I + i0: sub * I + i0 + iw],
                                  PB[bank][:, 0:iw]),
                              reads=[("PB", bank)], writes=[("s", sub, igi)])

            pend_mask = []
            for igi, (i0, iw) in enumerate(cfg.IGS):
                banks = [s_ + 4 * (igi % 2) for s_ in range(SUBS)]
                for cbi, (c0, nc8) in enumerate(cfg.CBS):
                    off = _off_ck(cfg, cfg.IGS, cfg.CBS, igi, cbi)
                    k = stream_tile(wup_h, off, 128 * nc8 * iw, nc8, iw)
                    em.wait_only("T", reads=[("AR", k)] + XNT_R,
                                 writes=([("PB", b) for b in banks]
                                         if cbi == 0 else []))
                    last_cb = cbi == len(cfg.CBS) - 1
                    for c8 in range(nc8):
                        c = c0 + c8
                        for sub in range(SUBS):
                            st = (cbi == 0 and c8 == 0)
                            sp = (last_cb and c8 == nc8 - 1)
                            em.raw_inc(
                                "T",
                                lambda e, c=c, sub=sub, k=k, c8=c8, iw=iw,
                                       nc8=nc8, st=st, sp=sp, bank=banks[sub]:
                                e.matmul(PB[bank][:, 0:iw],
                                         xnT[:, (c * SUBS + sub) * 128:
                                             (c * SUBS + sub) * 128 + 128],
                                         arslice(k, nc8, iw, c8),
                                         start=st, stop=sp),
                                reads=[("AR", k)],
                                writes=([("PB", banks[sub])] if sp else []),
                                inc=(1 if (c8 == nc8 - 1 and sub == SUBS - 1)
                                     or sp else 0))
                pend_mask.append((igi, i0, iw))
                if len(pend_mask) > 1:
                    _ph2a_mask(*pend_mask.pop(0))
            while pend_mask:
                _ph2a_mask(*pend_mask.pop(0))

            next_ln = _ln_sub_emitters(grp + 1) if grp + 1 < GROUPS else []
            nbb = len(cfg.BBS)
            for qi, (h0, hw_) in enumerate(cfg.HCH):
                banks = [s_ + 4 * (qi % 2) for s_ in range(SUBS)]
                for bsi, (b0, nb) in enumerate(cfg.BBS):
                    off = 0
                    for q2, (hh0, hhw) in enumerate(cfg.HCH):
                        for b2, (bb0, bnb) in enumerate(cfg.BBS):
                            if q2 == qi and b2 == bsi:
                                break
                            off += 128 * bnb * hhw
                        else:
                            continue
                        break
                    k = stream_tile(wdp_h, off, 128 * nb * hw_, nb, hw_)
                    em.wait_only("T", reads=[("AR", k)] +
                                 ([("s", s_, i_) for s_ in range(SUBS)
                                   for i_ in range(nIG)] if bsi == 0 else []),
                                 writes=([("PB", b) for b in banks]
                                         if bsi == 0 else []))
                    for jb in range(nb):
                        blk = b0 + jb
                        for sub in range(SUBS):
                            st = (bsi == 0 and jb == 0)
                            sp = (bsi == nbb - 1 and jb == nb - 1)
                            rds = [("AR", k)]
                            if sp:
                                rds = rds + [("s", sub, i_) for i_ in range(nIG)]
                            em.raw_inc(
                                "T",
                                lambda e, blk=blk, sub=sub, k=k, jb=jb, hw_=hw_,
                                       nb=nb, st=st, sp=sp, bank=banks[sub]:
                                e.matmul(PB[bank][:, 0:hw_],
                                         s_buf[:, sub * I + blk * 128:
                                               sub * I + blk * 128 + 128],
                                         arslice(k, nb, hw_, jb),
                                         start=st, stop=sp),
                                reads=rds,
                                writes=([("PB", banks[sub])] if sp else []),
                                inc=(1 if (jb == nb - 1 and sub == SUBS - 1)
                                     or sp else 0))
                em.dma("XQ", lambda e, h0=h0, hw_=hw_, t0=tok0:
                       e.dma_start(
                           out=xq[:, 0:SUBS * 512]
                           .rearrange("p (a b) -> p a b", a=SUBS)[:, :, 0:hw_],
                           in_=x_h[t0:t0 + SUBS * 128, h0:h0 + hw_]
                           .rearrange("(s p) h -> p s h", s=SUBS)),
                       writes=[("xq", s_) for s_ in range(SUBS)])
                for sub in range(SUBS):
                    bank = banks[sub]
                    r0 = tok0 + sub * 128
                    o = (qi * SUBS + sub) % 2
                    em.op("V", lambda e, sub=sub, bank=bank, h0=h0, hw_=hw_, o=o:
                          e.tensor_tensor(ob[o][:, 0:hw_], PB[bank][:, 0:hw_],
                                          g_sb[:, sub * H + h0:
                                               sub * H + h0 + hw_],
                                          ALU.mult),
                          reads=[("PB", bank), ("g", sub, qi)],
                          writes=[("OBK%d" % o,)])
                    em.op("V", lambda e, sub=sub, hw_=hw_, o=o:
                          e.tensor_tensor(ob[o][:, 0:hw_], ob[o][:, 0:hw_],
                                          xq[:, sub * 512: sub * 512 + hw_],
                                          ALU.add),
                          reads=[("OBK%d" % o,), ("xq", sub)],
                          writes=[("OBK%d" % o,)])
                    em.dma("OB%d" % o, lambda e, r0=r0, h0=h0, hw_=hw_, o=o:
                           e.dma_start(out=out_h[r0:r0 + 128, h0:h0 + hw_],
                                       in_=ob[o][:, 0:hw_]),
                           reads=[("OBK%d" % o,)])
                if grp + 1 < GROUPS and next_ln:
                    next_ln.pop(0)(bank_base=4 * (qi % 2))
            while grp + 1 < GROUPS and next_ln:
                next_ln.pop(0)(bank_base=6)

        em.prog["X"].append(("w", "OB0", em.cnt["OB0"]))
        em.prog["X"].append(("w", "OB1", em.cnt["OB1"]))

        sem_names = ["T", "V", "S"] + list(Em.DSEMS)
        with ExitStack() as sctx:
            sems = {n: sctx.enter_context(nc.semaphore("sem" + n))
                    for n in sem_names}
            block = sctx.enter_context(nc.Block())

            @block.tensor
            def _(e):
                em.replay("T", e, sems)

            @block.vector
            def _(e):
                em.replay("V", e, sems)

            @block.scalar
            def _(e):
                em.replay("S", e, sems)

            @block.gpsimd
            def _(e):
                em.replay("G", e, sems)

            @block.sync
            def _(e):
                em.replay("X", e, sems)

    return nc


def _base_rounds(em, cfg, stream_tile, arslice, wbp_h, PB, xnT, ones_sb,
                 brow_b, hwT, bflat, g_sb, XNT_R):
    SUBS, H, ER = cfg.SUBS, cfg.H, cfg.ER
    rounds = []
    for qi, (h0, hw_) in enumerate(cfg.HCH):
        def round_fn(qi=qi, h0=h0, hw_=hw_):
            banks = [s_ + 4 * (qi % 2) for s_ in range(SUBS)]
            for cbi, (c0, nc8) in enumerate(cfg.CBS):
                off = _off_ck(cfg, cfg.HCH, cfg.CBS, qi, cbi)
                k = stream_tile(wbp_h, off, 128 * nc8 * hw_, nc8, hw_)
                em.wait_only("T", reads=[("AR", k)] + XNT_R,
                             writes=([("PB", b) for b in banks]
                                     if cbi == 0 else []))
                for c8 in range(nc8):
                    c = c0 + c8
                    for sub in range(SUBS):
                        st = (cbi == 0 and c8 == 0)
                        em.raw_inc(
                            "T",
                            lambda e, c=c, sub=sub, k=k, c8=c8, nc8=nc8,
                                   hw_=hw_, st=st, bank=banks[sub]:
                            e.matmul(PB[bank][:, 0:hw_],
                                     xnT[:, (c * SUBS + sub) * 128:
                                         (c * SUBS + sub) * 128 + 128],
                                     arslice(k, nc8, hw_, c8),
                                     start=st, stop=False),
                            reads=[("AR", k)], writes=[],
                            inc=(1 if (c8 == nc8 - 1 and sub == SUBS - 1)
                                 else 0))
            em.wait_only("T", reads=[("browb",), ("ones",), ("bflat",)] +
                         [("hwT", s_) for s_ in range(SUBS)])
            for sub in range(SUBS):
                bank = banks[sub]
                em.raw_inc("T", lambda e, bank=bank, h0=h0, hw_=hw_:
                           e.matmul(PB[bank][:, 0:hw_], ones_sb[:],
                                    brow_b[:, h0:h0 + hw_],
                                    start=False, stop=False),
                           reads=[("browb",), ("ones",)], writes=[], inc=0)
                em.raw_inc("T", lambda e, bank=bank, sub=sub, h0=h0, hw_=hw_:
                           e.matmul(PB[bank][:, 0:hw_],
                                    hwT[:, sub * ER:(sub + 1) * ER],
                                    bflat[:, h0:h0 + hw_],
                                    start=False, stop=True),
                           reads=[("hwT", sub), ("bflat",)],
                           writes=[("PB", bank)], inc=1)
                em.op("S", lambda e, bank=bank, sub=sub, h0=h0, hw_=hw_:
                      e.activation(g_sb[:, sub * H + h0: sub * H + h0 + hw_],
                                   PB[bank][:, 0:hw_], AF.Silu),
                      reads=[("PB", bank)], writes=[("g", sub, qi)])
        rounds.append(round_fn)
    return rounds


def _bisect_program(em, cfg, s_buf, st, snkV, snkS):
    SUBS, I, K = cfg.SUBS, cfg.I, cfg.K
    nIG = len(cfg.IGS)
    NSUB = min(1024, I)
    KA = K * NSUB // I
    if SUBS == 4:
        v_passes = [(0, 0), (0, 1), (1, 0)]
        s_passes = [(1, 1), (2, 0), (2, 1), (3, 0), (3, 1)]
    else:
        v_passes = [(s_, sd) for s_ in range((SUBS + 1) // 2) for sd in (0, 1)]
        s_passes = [(s_, sd) for s_ in range((SUBS + 1) // 2, SUBS)
                    for sd in (0, 1)]
    vset = set(v_passes)
    trA_vals = [1.0 if (s_, 0) in vset else 0.5 for s_ in range(SUBS)]
    trB_vals = [1.0 if (s_, 1) in vset else 0.5 for s_ in range(SUBS)]
    trC_vals = [(0 if (s_, 0) in vset else 1) + (0 if (s_, 1) in vset else 1)
                for s_ in range(SUBS)]
    st["_tr_init"](trA_vals, trB_vals, trC_vals)
    trA, trB, trC = st["trA"], st["trB"], st["trC"]
    st_tr1, st_tr2 = st["tr1"], st["tr2"]
    A_ITERS = 12
    B_ITERS = cfg.B_ITERS
    lo, hi, mid, nmid = st["lo"], st["hi"], st["mid"], st["nmid"]
    cp, cn, cnt, msk, mski = st["cp"], st["cn"], st["cnt"], st["msk"], st["mski"]
    flo, fhi, hi0, w_, num, den = (st["flo"], st["fhi"], st["hi0"], st["w"],
                                   st["num"], st["den"])
    floi, fhii, pmsk = st["floi"], st["fhii"], st["pmsk"]

    def count_round(width, kthr, track_f, upd=True):
        em.op("V", lambda e: e.tensor_scalar(nmid[:], mid[:], -1.0, None,
                                             ALU.mult),
              reads=[("stmid",)], writes=[("stnmid",)])
        for (sub, side) in v_passes:
            srd = [("s", sub, i_) for i_ in range(nIG)]
            dst = cp if side == 0 else cn
            thr = mid if side == 0 else nmid
            opx = ALU.is_gt if side == 0 else ALU.is_lt
            tk = ("stmid",) if side == 0 else ("stnmid",)
            ck = ("stcp", sub) if side == 0 else ("stcn", sub)
            em.op("V", lambda e, sub=sub, width=width, dst=dst, thr=thr,
                  opx=opx: e.tensor_scalar(
                snkV[:, 0:1].broadcast_to([128, width]),
                s_buf[:, sub * I: sub * I + width],
                thr[:, sub:sub + 1], None, opx, op1=ALU.add,
                accum_out=dst[:, sub:sub + 1]),
                reads=[tk] + srd, writes=[("snkV",), ck])
        for (sub, side) in s_passes:
            srd = [("s", sub, i_) for i_ in range(nIG)]
            dst = cp if side == 0 else cn
            sc = 1.0 if side == 0 else -1.0
            ck = ("stcp", sub) if side == 0 else ("stcn", sub)
            em.op("S", lambda e, sub=sub, width=width, dst=dst, sc=sc:
                  e.activation(
                snkS[:, 0:1].broadcast_to([128, width]),
                s_buf[:, sub * I: sub * I + width], AF.Sign,
                bias=nmid[:, sub:sub + 1], scale=sc,
                accum_out=dst[:, sub:sub + 1]),
                reads=[("stnmid",)] + srd, writes=[("snkS",), ck])
        em.op("V", lambda e: e.tensor_tensor(st_tr1[:], cp[:], trA[:],
                                             ALU.mult),
              reads=[("stcp", s_) for s_ in range(SUBS)] + [("trA",)],
              writes=[("sttr1",)])
        em.op("V", lambda e: e.tensor_tensor(st_tr2[:], cn[:], trB[:],
                                             ALU.mult),
              reads=[("stcn", s_) for s_ in range(SUBS)] + [("trB",)],
              writes=[("sttr2",)])
        em.op("V", lambda e: e.tensor_tensor(cnt[:], st_tr1[:], st_tr2[:],
                                             ALU.add),
              reads=[("sttr1",), ("sttr2",)], writes=[("stcnt",)])
        em.op("V", lambda e, width=width: e.scalar_tensor_tensor(
            cnt[:], trC[:], 0.5 * width, cnt[:], ALU.mult, ALU.add),
            reads=[("stcnt",), ("trC",)], writes=[("stcnt",)])
        if not upd:
            return
        em.op("V", lambda e, kthr=kthr: e.tensor_scalar(
            msk[:], cnt[:], kthr, None, ALU.is_gt),
            reads=[("stcnt",)], writes=[("stmsk",)])
        em.op("V", lambda e, kthr=kthr: e.tensor_scalar(
            mski[:], cnt[:], kthr, None, ALU.is_le),
            reads=[("stcnt",)], writes=[("stmski",)])
        em.op("V", lambda e: e.select(lo[:], msk[:], mid[:], lo[:],
                                      add_drain=True),
              reads=[("stmsk",), ("stmid",), ("stlo",)], writes=[("stlo",)])
        em.op("V", lambda e: e.select(hi[:], mski[:], mid[:], hi[:],
                                      add_drain=True),
              reads=[("stmski",), ("stmid",), ("sthi",)], writes=[("sthi",)])
        if track_f:
            em.op("V", lambda e: e.select(flo[:], msk[:], cnt[:], flo[:],
                                          add_drain=True),
                  reads=[("stmsk",), ("stcnt",), ("stflo",)],
                  writes=[("stflo",)])
            em.op("V", lambda e: e.select(fhi[:], mski[:], cnt[:], fhi[:],
                                          add_drain=True),
                  reads=[("stmski",), ("stcnt",), ("stfhi",)],
                  writes=[("stfhi",)])
            em.op("V", lambda e: e.select(floi[:], msk[:], cnt[:], floi[:],
                                          add_drain=True),
                  reads=[("stmsk",), ("stcnt",), ("stfloi",)],
                  writes=[("stfloi",)])
            em.op("V", lambda e: e.select(fhii[:], mski[:], cnt[:], fhii[:],
                                          add_drain=True),
                  reads=[("stmski",), ("stcnt",), ("stfhii",)],
                  writes=[("stfhii",)])
            em.op("V", lambda e: e.tensor_tensor(st["tr1"][:], msk[:], pmsk[:],
                                                 ALU.mult),
                  reads=[("stmsk",), ("stpmsk",)], writes=[("sttr1",)])
            em.op("V", lambda e: e.tensor_scalar(
                st["msk2"][:], st["tr1"][:], 0.5, None, ALU.is_gt),
                reads=[("sttr1",)], writes=[("stmsk2",)])
            em.op("V", lambda e, K2=None: e.tensor_scalar(
                st["tr2"][:], fhii[:], 0.5, 0.5 * (cfg.K + 0.5), ALU.mult,
                op1=ALU.add),
                reads=[("stfhii",)], writes=[("sttr2",)])
            em.op("V", lambda e: e.select(fhii[:], st["msk2"][:], st["tr2"][:],
                                          fhii[:], add_drain=True),
                  reads=[("stmsk2",), ("sttr2",), ("stfhii",)],
                  writes=[("stfhii",)])
            em.op("V", lambda e: e.tensor_tensor(
                st["tr1"][:], mski[:], st["pmski"][:], ALU.mult),
                reads=[("stmski",), ("stpmski",)], writes=[("sttr1",)])
            em.op("V", lambda e: e.tensor_scalar(
                st["msk2"][:], st["tr1"][:], 0.5, None, ALU.is_gt),
                reads=[("sttr1",)], writes=[("stmsk2",)])
            em.op("V", lambda e: e.tensor_scalar(
                st["tr2"][:], floi[:], 0.5, 0.5 * (cfg.K + 0.5), ALU.mult,
                op1=ALU.add),
                reads=[("stfloi",)], writes=[("sttr2",)])
            em.op("V", lambda e: e.select(floi[:], st["msk2"][:], st["tr2"][:],
                                          floi[:], add_drain=True),
                  reads=[("stmsk2",), ("sttr2",), ("stfloi",)],
                  writes=[("stfloi",)])
            em.op("V", lambda e: e.tensor_copy(pmsk[:], msk[:]),
                  reads=[("stmsk",)], writes=[("stpmsk",)])
            em.op("V", lambda e: e.tensor_copy(st["pmski"][:], mski[:]),
                  reads=[("stmski",)], writes=[("stpmski",)])

    iters = []
    for it in range(A_ITERS):
        def a_iter(it=it):
            em.op("V", lambda e: e.tensor_tensor(mid[:], lo[:], hi[:], ALU.add),
                  reads=[("stlo",), ("sthi",)], writes=[("stmid",)])
            em.op("V", lambda e: e.tensor_scalar(mid[:], mid[:], 0.5, None,
                                                 ALU.mult),
                  reads=[("stmid",)], writes=[("stmid",)])
            count_round(NSUB, KA + 0.5, False)
        iters.append(a_iter)

    def b_init():
        em.op("V", lambda e: e.tensor_scalar(
            den[:], hi0[:], 0.012, None, ALU.mult),
            reads=[("sthi0",)], writes=[("stden",)])
        em.op("V", lambda e: e.tensor_tensor(lo[:], lo[:], den[:],
                                             ALU.subtract),
              reads=[("stlo",), ("stden",)], writes=[("stlo",)])
        em.op("V", lambda e: e.tensor_scalar(lo[:], lo[:], 0.0, None, ALU.max),
              reads=[("stlo",)], writes=[("stlo",)])
        em.op("V", lambda e: e.tensor_tensor(hi[:], hi[:], den[:], ALU.add),
              reads=[("sthi",), ("stden",)], writes=[("sthi",)])
        em.op("V", lambda e, v=float(cfg.K + 1300): e.memset(flo[:], v),
              writes=[("stflo",)])
        em.op("V", lambda e, v=float(cfg.K - 1300): e.memset(fhi[:], v),
              writes=[("stfhi",)])
        em.op("V", lambda e, v=float(cfg.K + 1300): e.memset(floi[:], v),
              writes=[("stfloi",)])
        em.op("V", lambda e, v=float(cfg.K - 1300): e.memset(fhii[:], v),
              writes=[("stfhii",)])
        em.op("V", lambda e: e.memset(pmsk[:], 0.0), writes=[("stpmsk",)])
        em.op("V", lambda e: e.memset(st["pmski"][:], 0.0),
              writes=[("stpmski",)])
    iters.append(b_init)

    for it in range(B_ITERS):
        def b_iter(it=it):
            if it == 0:
                em.op("V", lambda e: e.tensor_tensor(mid[:], lo[:], hi[:],
                                                     ALU.add),
                      reads=[("stlo",), ("sthi",)], writes=[("stmid",)])
                em.op("V", lambda e: e.tensor_scalar(mid[:], mid[:], 0.5, None,
                                                     ALU.mult),
                      reads=[("stmid",)], writes=[("stmid",)])
            else:
                em.op("V", lambda e: e.tensor_tensor(w_[:], hi[:], lo[:],
                                                     ALU.subtract),
                      reads=[("sthi",), ("stlo",)], writes=[("stw",)])
                em.op("V", lambda e: e.tensor_scalar(
                    num[:], floi[:], -(cfg.K + 0.5), None, ALU.add),
                    reads=[("stfloi",)], writes=[("stnum",)])
                em.op("V", lambda e: e.tensor_tensor(den[:], floi[:], fhii[:],
                                                     ALU.subtract),
                      reads=[("stfloi",), ("stfhii",)], writes=[("stden",)])
                em.op("V", lambda e: e.tensor_scalar(den[:], den[:], 1.0, None,
                                                     ALU.max),
                      reads=[("stden",)], writes=[("stden",)])
                em.op("V", lambda e: e.reciprocal(den[:], den[:]),
                      reads=[("stden",)], writes=[("stden",)])
                em.op("V", lambda e: e.tensor_tensor(num[:], num[:], den[:],
                                                     ALU.mult),
                      reads=[("stnum",), ("stden",)], writes=[("stnum",)])
                em.op("V", lambda e: e.tensor_scalar(
                    num[:], num[:], 1.0 / 64, 1.0 - 1.0 / 64, ALU.max,
                    op1=ALU.min),
                    reads=[("stnum",)], writes=[("stnum",)])
                em.op("V", lambda e: e.tensor_tensor(num[:], num[:], w_[:],
                                                     ALU.mult),
                      reads=[("stnum",), ("stw",)], writes=[("stnum",)])
                em.op("V", lambda e: e.tensor_tensor(mid[:], lo[:], num[:],
                                                     ALU.add),
                      reads=[("stlo",), ("stnum",)], writes=[("stmid",)])
            count_round(I, cfg.K + 0.5, True)
        iters.append(b_iter)

    def tie_pick():
        em.op("V", lambda e: e.tensor_copy(mid[:], st["th"][:]),
              reads=[("stth", s_) for s_ in range(SUBS)], writes=[("stmid",)])
        count_round(I, cfg.K + 0.5, False, upd=False)
        em.op("V", lambda e: e.tensor_scalar(
            num[:], cnt[:], float(-cfg.K), None, ALU.add),
            reads=[("stcnt",)], writes=[("stnum",)])
        em.op("V", lambda e: e.scalar_tensor_tensor(
            den[:], num[:], -1.0, num[:], ALU.mult, ALU.max),
            reads=[("stnum",)], writes=[("stden",)])
        em.op("V", lambda e: e.tensor_scalar(
            w_[:], st["flo"][:], float(-cfg.K), None, ALU.add),
            reads=[("stflo",)], writes=[("stw",)])
        em.op("V", lambda e: e.tensor_tensor(
            st["msk2"][:], den[:], w_[:], ALU.is_le),
            reads=[("stden",), ("stw",)], writes=[("stmsk2",)])
        em.op("V", lambda e: e.select(st["th"][:], st["msk2"][:], mid[:],
                                      lo[:], add_drain=True),
              reads=[("stmsk2",), ("stmid",), ("stlo",)],
              writes=[("stth", s_) for s_ in range(SUBS)])
    iters.append(tie_pick)
    return iters


def _fixup(em, cfg, s_buf, st_lo, st_hi, st_flo, st_th, st_d, ascr, bscr,
           t1scr, qscr, cm8, fm8, bv9, oh9, iota9_s):
    SUBS, I, K = cfg.SUBS, cfg.I, cfg.K
    nIG = len(cfg.IGS)
    for sub in range(SUBS):
        for igi, (i0, iw) in enumerate(cfg.IGS):
            em.op("S", lambda e, sub=sub, i0=i0, iw=iw: e.activation(
                ascr[:, 0:iw], s_buf[:, sub * I + i0: sub * I + i0 + iw],
                AF.Abs),
                reads=[("s", sub, igi)], writes=[("mcha",)])
            em.op("V", lambda e, sub=sub, iw=iw: e.tensor_scalar(
                bscr[:, 0:iw], ascr[:, 0:iw], st_lo[:, sub:sub + 1], 0.5,
                ALU.is_le, op1=ALU.subtract),
                reads=[("mcha",), ("stlo",)], writes=[("mchb",)])
            em.op("V", lambda e, sub=sub, iw=iw: e.scalar_tensor_tensor(
                qscr[:, 0:iw], ascr[:, 0:iw], st_hi[:, sub:sub + 1],
                bscr[:, 0:iw], ALU.subtract, ALU.mult),
                reads=[("mcha",), ("mchb",), ("sthi",)], writes=[("OBK1",)])
            em.op("V", lambda e, igi=igi, iw=iw: e.max(
                out=cm8[:, igi * 8:(igi + 1) * 8], in_=qscr[:, 0:iw]),
                reads=[("OBK1",)], writes=[("cm8", igi)])
        em.op("V", lambda e: e.max(out=fm8[:], in_=cm8[:]),
              reads=[("cm8", i_) for i_ in range(nIG)], writes=[("fm8",)])
        em.op("V", lambda e, sub=sub: e.tensor_copy(bv9[:, 0:1],
                                                    st_lo[:, sub:sub + 1]),
              reads=[("stlo",)], writes=[("bv9",)])
        em.op("V", lambda e, sub=sub: e.tensor_scalar(
            bv9[:, 1:9], fm8[:], -2.0, st_hi[:, sub:sub + 1], ALU.mult,
            op1=ALU.add),
            reads=[("fm8",), ("sthi",)], writes=[("bv9",)])
        em.op("V", lambda e, sub=sub: e.tensor_scalar(
            st_d[:, sub:sub + 1], st_flo[:, sub:sub + 1], float(-K), 8.0,
            ALU.add, op1=ALU.min),
            reads=[("stflo",)], writes=[("std", sub)])
        em.op("V", lambda e, sub=sub: e.tensor_scalar(
            st_d[:, sub:sub + 1], st_d[:, sub:sub + 1], 0.0, None, ALU.max),
            reads=[("std", sub)], writes=[("std", sub)])
        em.op("V", lambda e, sub=sub: e.scalar_tensor_tensor(
            oh9[:, 0:9], iota9_s[:, 0:9], st_d[:, sub:sub + 1], bv9[:, 0:9],
            ALU.is_equal, ALU.mult, accum_out=st_th[:, sub:sub + 1]),
            reads=[("iota9",), ("std", sub), ("bv9",)],
            writes=[("oh9",), ("stth", sub)])


def _sparsemax(em, cfg, lg, lgs, zc, zc2, sup, supn, kz, kzr, tau, wrt, srt_t,
               iotaE_s):
    SUBS, E = cfg.SUBS, cfg.E
    W = SUBS * E

    def v3(t):
        return t[:, 0:W].rearrange("p (a b) -> p a b", a=SUBS)

    em.op("V", lambda e: e.tensor_copy(lgs[:, 0:W], lg[:, 0:W]),
          reads=[("lg", s_) for s_ in range(SUBS)], writes=[("lgs",)])
    for rnd in range(E):
        if rnd % 2 == 0:
            a0, b0, npair = 0, 1, E // 2
        else:
            a0, b0, npair = 1, 2, (E - 1) // 2
        em.op("V", lambda e, a0=a0, b0=b0, npair=npair: e.tensor_tensor(
            srt_t[:, 0:SUBS * npair].rearrange("p (x y) -> p x y", x=SUBS),
            v3(lgs)[:, :, a0:a0 + 2 * npair - 1:2],
            v3(lgs)[:, :, b0:b0 + 2 * npair - 1:2], ALU.max),
            reads=[("lgs",)], writes=[("srtt",)])
        em.op("V", lambda e, a0=a0, b0=b0, npair=npair: e.tensor_tensor(
            v3(lgs)[:, :, b0:b0 + 2 * npair - 1:2],
            v3(lgs)[:, :, a0:a0 + 2 * npair - 1:2],
            v3(lgs)[:, :, b0:b0 + 2 * npair - 1:2], ALU.min),
            reads=[("lgs",)], writes=[("lgs",)])
        em.op("V", lambda e, a0=a0, npair=npair: e.tensor_copy(
            v3(lgs)[:, :, a0:a0 + 2 * npair - 1:2],
            srt_t[:, 0:SUBS * npair].rearrange("p (x y) -> p x y", x=SUBS)),
            reads=[("srtt",)], writes=[("lgs",)])
    em.op("V", lambda e: e.tensor_copy(zc[:, 0:W], lgs[:, 0:W]),
          reads=[("lgs",)], writes=[("zc",)])
    cur, oth, curk, othk = zc, zc2, "zc", "zc2"
    sh = 1
    while sh < E:
        em.op("V", lambda e, sh=sh, cur=cur, oth=oth: e.tensor_tensor(
            v3(oth)[:, :, sh:E], v3(cur)[:, :, sh:E], v3(cur)[:, :, 0:E - sh],
            ALU.add),
            reads=[(curk,)], writes=[(othk,)])
        em.op("V", lambda e, sh=sh, cur=cur, oth=oth: e.tensor_copy(
            v3(oth)[:, :, 0:sh], v3(cur)[:, :, 0:sh]),
            reads=[(curk,)], writes=[(othk,)])
        cur, oth, curk, othk = oth, cur, othk, curk
        sh *= 2
    em.op("V", lambda e: e.tensor_tensor(
        sup[:, 0:W], lgs[:, 0:W], iotaE_s[:, 0:W], ALU.mult),
        reads=[("lgs",), ("iotaE",)], writes=[("sup",)])
    em.op("V", lambda e: e.tensor_scalar(
        sup[:, 0:W], sup[:, 0:W], 1.0, None, ALU.add),
        reads=[("sup",)], writes=[("sup",)])
    em.op("V", lambda e, cur=cur: e.tensor_tensor(
        sup[:, 0:W], sup[:, 0:W], cur[:, 0:W], ALU.is_gt),
        reads=[("sup",), (curk,)], writes=[("sup",)])
    for sub in range(SUBS):
        em.op("V", lambda e, sub=sub: e.reduce_sum(
            kz[:, sub:sub + 1], sup[:, sub * E:(sub + 1) * E], AX),
            reads=[("sup",)], writes=[("kz",)])
    em.op("V", lambda e: e.memset(supn[:, 0:W], 0.0), writes=[("supn",)])
    em.op("V", lambda e: e.tensor_copy(
        v3(supn)[:, :, 0:E - 1], v3(sup)[:, :, 1:E]),
        reads=[("sup",)], writes=[("supn",)])
    em.op("V", lambda e: e.tensor_tensor(
        supn[:, 0:W], sup[:, 0:W], supn[:, 0:W], ALU.subtract),
        reads=[("sup",), ("supn",)], writes=[("supn",)])
    em.op("V", lambda e, cur=cur: e.tensor_tensor(
        supn[:, 0:W], supn[:, 0:W], cur[:, 0:W], ALU.mult),
        reads=[("supn",), (curk,)], writes=[("supn",)])
    for sub in range(SUBS):
        em.op("V", lambda e, sub=sub: e.reduce_sum(
            tau[:, sub:sub + 1], supn[:, sub * E:(sub + 1) * E], AX),
            reads=[("supn",)], writes=[("tau",)])
    em.op("V", lambda e: e.tensor_scalar(
        tau[:, 0:SUBS], tau[:, 0:SUBS], -1.0, None, ALU.add),
        reads=[("tau",)], writes=[("tau",)])
    em.op("V", lambda e: e.reciprocal(kzr[:, 0:SUBS], kz[:, 0:SUBS]),
          reads=[("kz",)], writes=[("kzr",)])
    em.op("V", lambda e: e.tensor_tensor(
        tau[:, 0:SUBS], tau[:, 0:SUBS], kzr[:, 0:SUBS], ALU.mult),
        reads=[("tau",), ("kzr",)], writes=[("tau",)])
    em.op("V", lambda e: e.tensor_tensor(
        v3(wrt), v3(lg),
        tau[:, 0:SUBS][:, :, None].broadcast_to([128, SUBS, E]),
        ALU.subtract),
        reads=[("lg", s_) for s_ in range(SUBS)] + [("tau",)],
        writes=[("wrt",)])
    em.op("V", lambda e: e.tensor_scalar(
        wrt[:, 0:W], wrt[:, 0:W], 0.0, None, ALU.max),
        reads=[("wrt",)], writes=[("wrt",)])



def kernel(L0_attn_x, L0_attn_w_gate, L0_attn_w_up, L0_attn_w_down,
           L0_ffn_W_base, L0_ffn_b_base, L0_ffn_W_router, L0_ffn_b_router,
           L0_ffn_lora_A, L0_ffn_lora_B):
    cfg = Cfg()
    xin = np.asarray(L0_attn_x)
    x = np.ascontiguousarray(xin.astype(np.float32, copy=False).reshape(-1, cfg.H))
    packed = pack_weights(
        cfg,
        np.asarray(L0_attn_w_gate, np.float32),
        np.asarray(L0_attn_w_up, np.float32),
        np.asarray(L0_attn_w_down, np.float32),
        np.asarray(L0_ffn_W_base, np.float32),
        np.asarray(L0_ffn_b_base, np.float32),
        np.asarray(L0_ffn_W_router, np.float32),
        np.asarray(L0_ffn_b_router, np.float32),
        np.asarray(L0_ffn_lora_A, np.float32),
        np.asarray(L0_ffn_lora_B, np.float32))
    nc = build(cfg)
    n = cfg.N_CORES
    per = x.shape[0] // n
    in_maps = []
    for c in range(n):
        m = dict(packed)
        m["x"] = np.ascontiguousarray(x[c * per:(c + 1) * per])
        in_maps.append(m)
    res = run_bass_kernel_spmd(nc, in_maps, core_ids=list(range(n)))
    out = np.concatenate([r["out"] for r in res.results], axis=0)
    return out.reshape(xin.shape).astype(np.float32)


# version 17
# speedup vs baseline: 1.1008x; 1.1008x over previous
import numpy as np
import concourse.bass as bass
import concourse.mybir as mybir
from concourse.bass_utils import run_bass_kernel_spmd
from contextlib import ExitStack

F32 = mybir.dt.float32
BF16 = mybir.dt.bfloat16
F16 = mybir.dt.float16
AF = mybir.ActivationFunctionType
ALU = mybir.AluOpType
AX = mybir.AxisListType.X

NEG_PUSH = -1000.0
BIG_F = 1.0e9


def chunks(total, size):
    out, o = [], 0
    while o < total:
        w = min(size, total - o)
        out.append((o, w))
        o += w
    return out


class Cfg:
    def __init__(self, H=4096, I=11008, TOK=1024, TG=512, E=8, R=16,
                 B_ITERS=8, FIXUP=True, N_CORES=8):
        self.H, self.I, self.TOK, self.TG, self.E, self.R = H, I, TOK, TG, E, R
        self.B_ITERS, self.FIXUP, self.N_CORES = B_ITERS, FIXUP, N_CORES
        self.HC = H // 128
        self.GROUPS = TOK // TG
        self.SUBS = TG // 128
        self.K = I // 2
        self.IGS = chunks(I, 512)
        self.NBLK = I // 128
        self.HCH = chunks(H, 512)
        self.ER = E * R
        self.F0W = E + self.ER
        self.CBS = chunks(self.HC, 4)
        self.F0CB = chunks(self.HC, 8)
        self.BBS = chunks(self.NBLK, 4)
        self.LNH = chunks(H, 1024)



def pack_weights(cfg, w_gate, w_up, w_down, W_base, b_base, W_router, b_router,
                 lora_A, lora_B):
    H, I = cfg.H, cfg.I

    def pack_ck(wT, colchunks, cbs):
        w3 = wT.reshape(cfg.HC, 128, wT.shape[1])
        parts = []
        for (i0, iw) in colchunks:
            for (c0, nc) in cbs:
                t = w3[c0:c0 + nc, :, i0:i0 + iw]
                parts.append(np.ascontiguousarray(t.transpose(1, 0, 2)).ravel())
        return np.concatenate(parts)

    wgT = np.ascontiguousarray(w_gate.T)
    wuT = np.ascontiguousarray(w_up.T)
    wgp = pack_ck(wgT, cfg.IGS, cfg.CBS)
    wup = pack_ck(wuT, cfg.IGS, cfg.CBS)

    wdT = np.ascontiguousarray(w_down.T)
    w3 = wdT.reshape(cfg.NBLK, 128, H)
    parts = []
    for (h0, hw) in cfg.HCH:
        for (b0, nb) in cfg.BBS:
            t = w3[b0:b0 + nb, :, h0:h0 + hw]
            parts.append(np.ascontiguousarray(t.transpose(1, 0, 2)).ravel())
    wdp = np.concatenate(parts)

    A_r = np.ascontiguousarray(lora_A.transpose(1, 0, 2)).reshape(H, cfg.ER)
    wra = np.concatenate([np.ascontiguousarray(W_router.T), A_r], axis=1)
    wrap_ = pack_ck(np.ascontiguousarray(wra), [(0, cfg.F0W)], cfg.F0CB)

    wbp = pack_ck(np.ascontiguousarray(W_base.T), cfg.HCH, cfg.CBS)

    bflat = np.ascontiguousarray(lora_B.reshape(cfg.ER, H))
    brow_wra = np.concatenate([b_router, np.zeros(cfg.ER, np.float32)])[None, :]
    brow_wb = np.ascontiguousarray(b_base[None, :])
    ident = np.eye(128, dtype=np.float32)
    iota9 = np.tile(np.arange(9, dtype=np.float32)[None, :], (128, 1))
    iotaE = np.tile(np.arange(1, cfg.E + 1, dtype=np.float32)[None, :],
                    (128, cfg.SUBS))
    onesrow = np.ones((1, 128), dtype=np.float32)
    return dict(wgp=wgp, wup=wup, wdp=wdp, wrap=wrap_, wbp=wbp, bflat=bflat,
                brow_wra=np.ascontiguousarray(brow_wra, np.float32),
                brow_wb=brow_wb.astype(np.float32),
                ident=ident, iota9=iota9, iotaE=iotaE, onesrow=onesrow)


def _off_ck(cfg, colchunks, cbs, ci, cbi):
    off = 0
    for i2, (i0, iw) in enumerate(colchunks):
        for c2, (c0, nc) in enumerate(cbs):
            if i2 == ci and c2 == cbi:
                return off
            off += 128 * nc * iw
    raise KeyError



class Em:
    ENGS = ("T", "V", "S", "G", "X")
    DSEMS = {"AR0": "G", "AR1": "G", "AR2": "G", "AR3": "G", "CG": "G",
             "XZ": "X", "XZ2": "X", "XQ": "X", "OB0": "X", "OB1": "X",
             "CX": "X"}

    def __init__(self):
        self.prog = {e: [] for e in self.ENGS}
        self.cnt = {e: 0 for e in self.ENGS}
        for s in self.DSEMS:
            self.cnt[s] = 0
        self.waited = {e: {} for e in self.ENGS}
        self.lastw = {}
        self.readers = {}

    def _emit_waits(self, eng, evts):
        need = {}
        for (se, v) in evts:
            if self.DSEMS.get(se) == eng:
                continue
            if v > self.waited[eng].get(se, 0):
                need[se] = max(need.get(se, 0), v)
        for se, v in need.items():
            self.waited[eng][se] = v
            self.prog[eng].append(("w", se, v))

    def _collect(self, reads, writes):
        evts = []
        for r in reads:
            ev = self.lastw.get(r)
            if ev is not None:
                evts.append(ev)
        for w in writes:
            ev = self.lastw.get(w)
            if ev is not None:
                evts.append(ev)
            evts.extend(self.readers.get(w, ()))
        return evts

    def _register(self, ev, reads, writes):
        for r in reads:
            self.readers.setdefault(r, []).append(ev)
        for w in writes:
            self.lastw[w] = ev
            self.readers[w] = []

    def op(self, eng, fn, reads=(), writes=(), inc=1):
        self._emit_waits(eng, self._collect(reads, writes))
        self.cnt[eng] += inc
        self.prog[eng].append(("i", fn, inc, eng))
        ev = (eng, self.cnt[eng])
        self._register(ev, reads, writes)
        return ev

    def dma(self, sem, fn, reads=(), writes=()):
        qeng = self.DSEMS[sem]
        self._emit_waits(qeng, self._collect(reads, writes))
        self.cnt[sem] += 16
        self.prog[qeng].append(("i", fn, 16, sem))
        ev = (sem, self.cnt[sem])
        self._register(ev, reads, writes)
        return ev

    def wait_only(self, eng, reads=(), writes=()):
        self._emit_waits(eng, self._collect(reads, writes))

    def raw_inc(self, eng, fn, reads=(), writes=(), inc=1):
        self.cnt[eng] += inc
        self.prog[eng].append(("i", fn, inc, eng))
        ev = (eng, self.cnt[eng])
        self._register(ev, reads, writes)
        return ev

    def force_event(self, region, ev):
        self.lastw[region] = ev
        self.readers[region] = []

    def replay(self, eng_name, eng_obj, sems):
        for item in self.prog[eng_name]:
            if item[0] == "w":
                eng_obj.wait_ge(sems[item[1]], item[2])
            else:
                inst = item[1](eng_obj)
                if item[2]:
                    inst.then_inc(sems[item[3]], item[2])



def build(cfg):
    nc = bass.Bass("TRN2", target_bir_lowering=False)
    H, I, E, R, ER, F0W = cfg.H, cfg.I, cfg.E, cfg.R, cfg.ER, cfg.F0W
    SUBS, GROUPS, K = cfg.SUBS, cfg.GROUPS, cfg.K
    nIG = len(cfg.IGS)
    nh = len(cfg.LNH)

    def din(name, shape):
        return nc.dram_tensor(name, list(shape), F32, kind="ExternalInput").ap()

    x_h = din("x", (cfg.TOK, H))
    wgp_h = din("wgp", (H * I,))
    wup_h = din("wup", (H * I,))
    wdp_h = din("wdp", (I * H,))
    wrap_h = din("wrap", (H * F0W,))
    wbp_h = din("wbp", (H * H,))
    bflat_h = din("bflat", (ER, H))
    browa_h = din("brow_wra", (1, F0W))
    browb_h = din("brow_wb", (1, H))
    ident_h = din("ident", (128, 128))
    iota9_h = din("iota9", (128, 9))
    iotaE_h = din("iotaE", (128, E * SUBS))
    ones_h = din("onesrow", (1, 128))
    out_h = nc.dram_tensor("out", [cfg.TOK, H], F32, kind="ExternalOutput").ap()

    em = Em()
    ctx = ExitStack()
    with ctx:
        def sbuf(name, shape, dt):
            return ctx.enter_context(nc.sbuf_tensor(name, list(shape), dt))

        s_buf = sbuf("s_buf", (128, SUBS * I), F16)
        xnT = sbuf("xnT", (128, cfg.HC * 128 * SUBS), F16)
        g_sb = sbuf("g_sb", (128, SUBS * H), F16)
        xz = sbuf("xz", (128, 1024), F32)
        ar = [sbuf("ar%d" % i_, (128, 2048), F16) for i_ in range(4)]
        bflat = sbuf("bflat_sb", (128, H), F16)
        mch_a = sbuf("mch_a", (128, 512), F16)
        mch_b = sbuf("mch_b", (128, 512), F16)
        mscr = sbuf("mscr", (128, 512), F32)
        ascr = mch_a
        bscr = mch_b
        qscr = sbuf("qscr", (128, 512), F32)
        ob = [mscr, qscr]
        cm8 = sbuf("cm8", (128, 8 * nIG), F32)
        fm8 = sbuf("fm8", (128, 8), F32)
        bv9 = sbuf("bv9", (128, 9), F32)
        oh9 = sbuf("oh9", (128, 9), F32)
        st_lo = sbuf("st_lo", (128, SUBS), F32)
        st_hi = sbuf("st_hi", (128, SUBS), F32)
        st_mid = sbuf("st_mid", (128, SUBS), F32)
        st_nmid = sbuf("st_nmid", (128, SUBS), F32)
        st_cp = sbuf("st_cp", (128, SUBS), F32)
        st_cn = sbuf("st_cn", (128, SUBS), F32)
        st_cnt = sbuf("st_cnt", (128, SUBS), F32)
        st_msk = sbuf("st_msk", (128, SUBS), mybir.dt.uint8)
        st_mski = sbuf("st_mski", (128, SUBS), mybir.dt.uint8)
        st_flo = sbuf("st_flo", (128, SUBS), F32)
        st_th = sbuf("st_th", (128, SUBS), F32)
        st_nth = sbuf("st_nth", (128, SUBS), F32)
        st_tmp = sbuf("st_tmp", (128, SUBS), F32)
        st_d = sbuf("st_d", (128, SUBS), F32)
        st_hi0 = sbuf("st_hi0", (128, SUBS), F32)
        st_fhi = sbuf("st_fhi", (128, SUBS), F32)
        st_w = sbuf("st_w", (128, SUBS), F32)
        st_num = sbuf("st_num", (128, SUBS), F32)
        st_den = sbuf("st_den", (128, SUBS), F32)
        st_trA = sbuf("st_trA", (128, SUBS), F32)
        st_trB = sbuf("st_trB", (128, SUBS), F32)
        st_trC = sbuf("st_trC", (128, SUBS), F32)
        st_tr1 = sbuf("st_tr1", (128, SUBS), F32)
        st_tr2 = sbuf("st_tr2", (128, SUBS), F32)
        st_floi = sbuf("st_floi", (128, SUBS), F32)
        st_fhii = sbuf("st_fhii", (128, SUBS), F32)
        st_pmsk = sbuf("st_pmsk", (128, SUBS), F32)
        st_pmski = sbuf("st_pmski", (128, SUBS), F32)
        st_msk2 = sbuf("st_msk2", (128, SUBS), mybir.dt.uint8)
        rmscr = sbuf("rmscr", (128, SUBS * nIG), F32)
        snkV = sbuf("snkV", (128, 1), F16)
        snkS = sbuf("snkS", (128, 1), F16)
        ln_sm = sbuf("ln_sm", (128, nh), F32)
        ln_sq = sbuf("ln_sq", (128, nh), F32)
        ln_mean = sbuf("ln_mean", (128, 1), F32)
        ln_var = sbuf("ln_var", (128, 1), F32)
        ln_rstd = sbuf("ln_rstd", (128, SUBS), F32)
        ln_nb = sbuf("ln_nb", (128, SUBS), F32)
        lg = sbuf("lg", (128, SUBS * E), F32)
        lgs = sbuf("lgs", (128, SUBS * E), F32)
        zc = sbuf("zc", (128, SUBS * E), F32)
        zc2 = sbuf("zc2", (128, SUBS * E), F32)
        sup = sbuf("sup", (128, SUBS * E), F32)
        supn = sbuf("supn", (128, SUBS * E), F32)
        kz = sbuf("kz", (128, SUBS), F32)
        kzr = sbuf("kzr", (128, SUBS), F32)
        tau = sbuf("tau", (128, SUBS), F32)
        wrt = sbuf("wrt", (128, SUBS * E), F32)
        srt_t = sbuf("srt_t", (128, SUBS * 4), F32)
        h_sb = sbuf("h_sb", (128, SUBS * ER), F16)
        hw_scr = sbuf("hw_scr", (128, ER), F32)
        hwT = sbuf("hwT", (128, SUBS * ER), F16)
        identF = sbuf("identF", (128, 128), F32)
        iota9_s = sbuf("iota9_s", (128, 9), F32)
        iotaE_s = sbuf("iotaE_s", (128, E * SUBS), F32)
        ones_sb = sbuf("ones_sb", (1, 128), F16)
        brow_a = sbuf("brow_a", (1, F0W), F16)
        brow_b = sbuf("brow_b", (1, H), F16)
        xq = sbuf("xq", (128, SUBS * 512), F32)
        t1scr = xq
        eps_ap = sbuf("eps_ap", (128, 1), F32)
        nhalf_ap = sbuf("nhalf_ap", (128, 1), F32)

        PB = [ctx.enter_context(nc.psum_tensor(f"PB{i}", [128, 512], F32))
              for i in range(8)]

        ALLS = [("s", s_, i_) for s_ in range(SUBS) for i_ in range(nIG)]

        def _tr_init(av, bv, cv):
            for c_, v_ in enumerate(av):
                em.op("V", lambda e, c_=c_, v_=v_: e.memset(
                    st_trA[:, c_:c_ + 1], v_), writes=[("trA",)])
            for c_, v_ in enumerate(bv):
                em.op("V", lambda e, c_=c_, v_=v_: e.memset(
                    st_trB[:, c_:c_ + 1], v_), writes=[("trB",)])
            for c_, v_ in enumerate(cv):
                em.op("V", lambda e, c_=c_, v_=v_: e.memset(
                    st_trC[:, c_:c_ + 1], float(v_)), writes=[("trC",)])
        bst = dict(lo=st_lo, hi=st_hi, mid=st_mid, nmid=st_nmid, cp=st_cp,
                  cn=st_cn, cnt=st_cnt, msk=st_msk, mski=st_mski,
                  flo=st_flo, fhi=st_fhi, hi0=st_hi0, w=st_w, num=st_num,
                  den=st_den, trA=st_trA, trB=st_trB, trC=st_trC,
                  tr1=st_tr1, tr2=st_tr2, floi=st_floi, fhii=st_fhii,
                  pmsk=st_pmsk, pmski=st_pmski, msk2=st_msk2,
                  th=st_th, _tr_init=_tr_init)

        em.dma("CX", lambda e: e.dma_start(out=identF[:], in_=ident_h))
        em.dma("CX", lambda e: e.dma_start(out=iota9_s[:], in_=iota9_h))
        em.dma("CX", lambda e: e.dma_start(out=iotaE_s[:], in_=iotaE_h))
        em.dma("CG", lambda e: e.dma_start(out=ones_sb[:], in_=ones_h))
        em.dma("CG", lambda e: e.dma_start(out=brow_a[:], in_=browa_h))
        em.dma("CG", lambda e: e.dma_start(out=brow_b[:], in_=browb_h))
        em.dma("CG", lambda e: e.dma_start(out=bflat[:], in_=bflat_h))
        for reg in [("identF",), ("iota9",), ("iotaE",)]:
            em.force_event(reg, ("CX", em.cnt["CX"]))
        for reg in [("ones",), ("browa",), ("browb",), ("bflat",)]:
            em.force_event(reg, ("CG", em.cnt["CG"]))

        em.op("V", lambda e: e.memset(eps_ap[:], 1e-5), writes=[("epsap",)])
        em.op("V", lambda e: e.memset(nhalf_ap[:], -0.5), writes=[("nhalfap",)])

        wslot = {"k": 0}

        def stream_tile(src_1d, off, nelem, d1, d2):
            k = wslot["k"] % 4
            wslot["k"] += 1
            dst = ar[k][:, 0:d1 * d2].rearrange("p (a b) -> p a b", a=d1)
            src = src_1d[off:off + nelem].rearrange("(p a b) -> p a b",
                                                    p=128, a=d1)
            em.dma("AR%d" % k, lambda e, d=dst, s=src: e.dma_start(out=d, in_=s),
                   writes=[("AR", k)])
            return k

        def arslice(k, d1, d2, j):
            return ar[k][:, 0:d1 * d2].rearrange("p (a b) -> p a b", a=d1)[:, j, :]

        def _ln_sub_emitters(grp2, bank_base=6, bufs=None):
            tok02 = grp2 * cfg.TG
            if bufs is None:
                bufs = [(xz, (("XZ",),), "XZ"), (xz, (("XZ",),), "XZ")]
            fns = []
            for sub in range(SUBS):
                def fn_a(sub=sub, **kw):
                    r0 = tok02 + sub * 128
                    for hi_, (h0, hw_) in enumerate(cfg.LNH):
                        buf, bkeys, bsem = bufs[hi_ % len(bufs)]
                        em.dma(bsem, lambda e, h0=h0, hw_=hw_, r0=r0, buf=buf:
                               e.dma_start(out=buf[:, 0:hw_],
                                           in_=x_h[r0:r0 + 128, h0:h0 + hw_]),
                               writes=list(bkeys))
                        em.op("V", lambda e, hw_=hw_, c=hi_, buf=buf:
                              e.reduce_sum(ln_sm[:, c:c + 1], buf[:, 0:hw_], AX),
                              reads=list(bkeys), writes=[("lnsm", hi_)])
                        em.op("S", lambda e, hw_=hw_, c=hi_, buf=buf:
                              e.activation(snkS[:, 0:1].broadcast_to([128, hw_]),
                                           buf[:, 0:hw_], AF.Square,
                                           accum_out=ln_sq[:, c:c + 1]),
                              reads=list(bkeys), writes=[("lnsq", hi_), ("snkS",)])
                    em.op("V", lambda e: e.reduce_sum(ln_mean[:, 0:1],
                                                      ln_sm[:, 0:nh], AX),
                          reads=[("lnsm", i_) for i_ in range(nh)],
                          writes=[("lnmean",)])
                    em.op("V", lambda e: e.tensor_scalar(
                        ln_mean[:, 0:1], ln_mean[:, 0:1], 1.0 / H, None, ALU.mult),
                        reads=[("lnmean",)], writes=[("lnmean",)])
                    em.op("V", lambda e: e.reduce_sum(ln_var[:, 0:1],
                                                      ln_sq[:, 0:nh], AX),
                          reads=[("lnsq", i_) for i_ in range(nh)],
                          writes=[("lnvar",)])
                    em.op("V", lambda e: e.tensor_scalar(
                        ln_var[:, 0:1], ln_var[:, 0:1], 1.0 / H, None, ALU.mult),
                        reads=[("lnvar",)], writes=[("lnvar",)])
                    em.op("V", lambda e: e.tensor_tensor(
                        st_tmp[:, 0:1], ln_mean[:, 0:1], ln_mean[:, 0:1], ALU.mult),
                        reads=[("lnmean",)], writes=[("sttmp",)])
                    em.op("V", lambda e: e.tensor_tensor(
                        ln_var[:, 0:1], ln_var[:, 0:1], st_tmp[:, 0:1], ALU.subtract),
                        reads=[("lnvar",), ("sttmp",)], writes=[("lnvar",)])
                    em.op("S", lambda e: e.activation(
                        ln_rstd[:, 0:1], ln_var[:, 0:1], AF.Ln, bias=eps_ap[:, 0:1]),
                        reads=[("lnvar",), ("epsap",)], writes=[("lnr0",)])
                    em.op("S", lambda e, sub=sub: e.activation(
                        ln_rstd[:, sub:sub + 1], ln_rstd[:, 0:1], AF.Exp,
                        scale=nhalf_ap[:, 0:1]),
                        reads=[("lnr0",), ("nhalfap",)],
                        writes=[("lnrstd", sub), ("lnr0",)])
                    em.op("V", lambda e, sub=sub: e.tensor_tensor(
                        ln_nb[:, sub:sub + 1], ln_mean[:, 0:1],
                        ln_rstd[:, sub:sub + 1], ALU.mult),
                        reads=[("lnmean",), ("lnrstd", sub)], writes=[("lnnb", sub)])
                    em.op("V", lambda e, sub=sub: e.tensor_scalar(
                        ln_nb[:, sub:sub + 1], ln_nb[:, sub:sub + 1], -1.0, None,
                        ALU.mult),
                        reads=[("lnnb", sub)], writes=[("lnnb", sub)])
                def fn_b(sub=sub, bank_base=bank_base, **kw):
                    bank_base = kw.get("bank_base", bank_base)
                    r0 = tok02 + sub * 128
                    tp_par = 0
                    for hi_, (h0, hw_) in enumerate(cfg.LNH):
                        buf, bkeys, bsem = bufs[hi_ % len(bufs)]
                        em.dma(bsem, lambda e, h0=h0, hw_=hw_, r0=r0, buf=buf:
                               e.dma_start(out=buf[:, 0:hw_],
                                           in_=x_h[r0:r0 + 128, h0:h0 + hw_]),
                               writes=list(bkeys))
                        em.op("V", lambda e, hw_=hw_, sub=sub, buf=buf:
                              e.tensor_scalar(
                            buf[:, 0:hw_], buf[:, 0:hw_], ln_rstd[:, sub:sub + 1],
                            ln_nb[:, sub:sub + 1], ALU.mult, op1=ALU.add),
                            reads=list(bkeys) + [("lnnb", sub), ("lnrstd", sub)],
                            writes=list(bkeys))
                        for (b0, nb) in chunks(hw_ // 128, 4):
                            bank = bank_base + (tp_par % 2)
                            tp_par += 1
                            em.wait_only("T", reads=list(bkeys) + [("identF",)],
                                         writes=[("PB", bank)])
                            for j in range(nb):
                                em.raw_inc("T", lambda e, b=b0 + j, j=j,
                                           bank=bank, buf=buf:
                                           e.transpose(
                                               PB[bank][:, j * 128:(j + 1) * 128],
                                               buf[:, b * 128:(b + 1) * 128],
                                               identF[:]),
                                           reads=list(bkeys) + [("identF",)],
                                           writes=[("PB", bank)],
                                           inc=(1 if j == nb - 1 else 0))
                            cbase = h0 // 128 + b0
                            em.op("V", lambda e, cbase=cbase, nb=nb, sub=sub,
                                  bank=bank:
                                  e.tensor_copy(
                                      xnT[:, (cbase * SUBS + sub) * 128:]
                                      .rearrange("p (a b) -> p a b", b=128)
                                      [:, 0:(nb - 1) * SUBS + 1:SUBS, :],
                                      PB[bank][:, 0:nb * 128]
                                      .rearrange("p (a b) -> p a b", b=128)),
                                  reads=[("PB", bank)], writes=[("xnT", sub)])

                fns.append(fn_a)
                fns.append(fn_b)
            return fns

        for grp in range(GROUPS):
            tok0 = grp * cfg.TG

            if grp == 0:
                if SUBS * 512 >= 2048:
                    g0bufs = [(xq[:, 0:1024], (("xq", 0), ("xq", 1)),
                               "XZ"),
                              (xq[:, 1024:2048], (("xq", 2), ("xq", 3)),
                               "XZ2")]
                else:
                    g0bufs = None
                for f_ in _ln_sub_emitters(0, bufs=g0bufs):
                    f_()
            XNT_R = [("xnT", s_) for s_ in range(SUBS)]

            a_iters, bis_iters, tie_pick = _bisect_program(em, cfg, s_buf,
                                                           bst, snkV, snkS)

            for igi, (i0, iw) in enumerate(cfg.IGS):
                banks = [s_ + 4 * (igi % 2) for s_ in range(SUBS)]
                for cbi, (c0, nc8) in enumerate(cfg.CBS):
                    off = _off_ck(cfg, cfg.IGS, cfg.CBS, igi, cbi)
                    k = stream_tile(wgp_h, off, 128 * nc8 * iw, nc8, iw)
                    em.wait_only("T", reads=[("AR", k)] + XNT_R,
                                 writes=([("PB", b) for b in banks]
                                         if cbi == 0 else []))
                    last_cb = cbi == len(cfg.CBS) - 1
                    for c8 in range(nc8):
                        c = c0 + c8
                        for sub in range(SUBS):
                            st = (cbi == 0 and c8 == 0)
                            sp = (last_cb and c8 == nc8 - 1)
                            em.raw_inc(
                                "T",
                                lambda e, c=c, sub=sub, k=k, c8=c8, iw=iw,
                                       nc8=nc8, st=st, sp=sp, bank=banks[sub]:
                                e.matmul(PB[bank][:, 0:iw],
                                         xnT[:, (c * SUBS + sub) * 128:
                                             (c * SUBS + sub) * 128 + 128],
                                         arslice(k, nc8, iw, c8),
                                         start=st, stop=sp),
                                reads=[("AR", k)],
                                writes=([("PB", banks[sub])] if sp else []),
                                inc=(1 if (c8 == nc8 - 1 and sub == SUBS - 1)
                                     or sp else 0))
                for sub in range(SUBS):
                    bank = banks[sub]
                    em.op("S", lambda e, bank=bank, iw=iw, sub=sub, i0=i0:
                          e.activation(s_buf[:, sub * I + i0: sub * I + i0 + iw],
                                       PB[bank][:, 0:iw], AF.Silu),
                          reads=[("PB", bank)], writes=[("s", sub, igi)])
                    if igi < 2:
                        em.op("V", lambda e, sub=sub, i0=i0, iw=iw, igi=igi:
                              e.tensor_reduce(
                                  rmscr[:, sub * nIG + igi:
                                        sub * nIG + igi + 1],
                                  s_buf[:, sub * I + i0: sub * I + i0 + iw],
                                  AX, ALU.max, apply_absolute_value=True),
                              reads=[("s", sub, igi)],
                              writes=[("rm", sub, igi)])
                if igi == 1:
                    for sub in range(SUBS):
                        em.op("V", lambda e, sub=sub: e.tensor_reduce(
                            st_hi[:, sub:sub + 1],
                            rmscr[:, sub * nIG: sub * nIG + 2], AX, ALU.max),
                            reads=[("rm", sub, 0), ("rm", sub, 1)],
                            writes=[("sthi",)])
                    em.op("V", lambda e: e.tensor_copy(st_hi0[:], st_hi[:]),
                          reads=[("sthi",)], writes=[("sthi0",)])
                    em.op("V", lambda e: e.memset(st_lo[:], 0.0),
                          writes=[("stlo",)])
                    em.op("V", lambda e: e.memset(st_flo[:], BIG_F),
                          writes=[("stflo",)])
                    for afn in a_iters:
                        afn()

            for fi, (c0, nc16) in enumerate(cfg.F0CB):
                off = _off_ck(cfg, [(0, F0W)], cfg.F0CB, 0, fi)
                k = stream_tile(wrap_h, off, 128 * nc16 * F0W, nc16, F0W)
                em.wait_only("T", reads=[("AR", k), ("browa",), ("ones",)] + XNT_R,
                             writes=([("PB", s_) for s_ in range(SUBS)]
                                     if fi == 0 else []))
                for c16 in range(nc16):
                    c = c0 + c16
                    for sub in range(SUBS):
                        st = (fi == 0 and c16 == 0)
                        em.raw_inc("T",
                                   lambda e, c=c, sub=sub, k=k, c16=c16,
                                          nc16=nc16, st=st:
                                   e.matmul(PB[sub][:, 0:F0W],
                                            xnT[:, (c * SUBS + sub) * 128:
                                                (c * SUBS + sub) * 128 + 128],
                                            arslice(k, nc16, F0W, c16),
                                            start=st, stop=False),
                                   reads=[("AR", k)], writes=[],
                                   inc=(1 if (c16 == nc16 - 1 and
                                              sub == SUBS - 1) else 0))
            for sub in range(SUBS):
                em.raw_inc("T", lambda e, sub=sub:
                           e.matmul(PB[sub][:, 0:F0W], ones_sb[:], brow_a[:],
                                    start=False, stop=True),
                           reads=[("browa",), ("ones",)],
                           writes=[("PB", sub)], inc=1)
                em.op("V", lambda e, sub=sub: e.tensor_copy(
                    lg[:, sub * E:(sub + 1) * E], PB[sub][:, 0:E]),
                    reads=[("PB", sub)], writes=[("lg", sub)])
                em.op("V", lambda e, sub=sub: e.tensor_copy(
                    h_sb[:, sub * ER:(sub + 1) * ER], PB[sub][:, E:E + ER]),
                    reads=[("PB", sub)], writes=[("hsb", sub)])

            _sparsemax(em, cfg, lg, lgs, zc, zc2, sup, supn, kz, kzr, tau, wrt,
                       srt_t, iotaE_s)

            for sub in range(SUBS):
                em.op("V", lambda e, sub=sub: e.tensor_tensor(
                    hw_scr[:, 0:ER].rearrange("p (a b) -> p a b", a=E),
                    h_sb[:, sub * ER:(sub + 1) * ER]
                    .rearrange("p (a b) -> p a b", a=E),
                    wrt[:, sub * E:(sub + 1) * E][:, :, None]
                    .broadcast_to([128, E, R]),
                    ALU.mult),
                    reads=[("hsb", sub), ("wrt",)], writes=[("hwscr",)])
                bank = 4 + (sub % 2)
                em.op("T", lambda e, bank=bank: e.transpose(
                    PB[bank][:, 0:128], hw_scr[:, 0:128], identF[:]),
                    reads=[("hwscr",), ("identF",)], writes=[("PB", bank)])
                em.op("V", lambda e, sub=sub, bank=bank: e.tensor_copy(
                    hwT[:, sub * ER:(sub + 1) * ER], PB[bank][:, 0:ER]),
                    reads=[("PB", bank)], writes=[("hwT", sub)])

            base_rounds = _base_rounds(em, cfg, stream_tile, arslice, wbp_h, PB,
                                       xnT, ones_sb, brow_b, hwT, bflat, g_sb,
                                       XNT_R)

            nb_, ni_ = len(base_rounds), len(bis_iters)
            bi = 0
            for ri, rfn in enumerate(base_rounds):
                rfn()
                if ri < 1:
                    continue
                want = int(np.ceil((ri - 1 + 1) * ni_ / max(nb_, 1)))
                while bi < min(want, ni_):
                    bis_iters[bi]()
                    bi += 1
            while bi < ni_:
                bis_iters[bi]()
                bi += 1

            if cfg.FIXUP:
                _fixup(em, cfg, s_buf, st_lo, st_hi, st_flo, st_th, st_d, ascr,
                       bscr, t1scr, qscr, cm8, fm8, bv9, oh9, iota9_s)
            else:
                em.op("V", lambda e: e.tensor_copy(st_th[:], st_hi[:]),
                      reads=[("sthi",)],
                      writes=[("stth", s_) for s_ in range(SUBS)])
            tie_pick()
            for sub in range(SUBS):
                em.op("V", lambda e, sub=sub: e.tensor_scalar(
                    st_nth[:, sub:sub + 1], st_th[:, sub:sub + 1], -1.0, None,
                    ALU.mult),
                    reads=[("stth", sub)], writes=[("stnth", sub)])

            def _ph2a_mask(igi, i0, iw):
                    for sub in range(SUBS):
                        em.op("V", lambda e, sub=sub, i0=i0, iw=iw: e.tensor_scalar(
                            mch_a[:, 0:iw],
                            s_buf[:, sub * I + i0: sub * I + i0 + iw],
                            st_th[:, sub:sub + 1], st_nth[:, sub:sub + 1],
                            ALU.min, op1=ALU.max),
                            reads=[("s", sub, igi), ("stth", sub),
                                   ("stnth", sub)],
                            writes=[("mcha",)])
                        em.op("V", lambda e, sub=sub, i0=i0, iw=iw: e.tensor_tensor(
                            mch_b[:, 0:iw],
                            s_buf[:, sub * I + i0: sub * I + i0 + iw],
                            mch_a[:, 0:iw], ALU.not_equal),
                            reads=[("s", sub, igi), ("mcha",)], writes=[("mchb",)])
                        em.op("V", lambda e, sub=sub, i0=i0, iw=iw: e.tensor_tensor(
                            mch_a[:, 0:iw], mch_b[:, 0:iw],
                            s_buf[:, sub * I + i0: sub * I + i0 + iw], ALU.mult),
                            reads=[("s", sub, igi), ("mchb",)], writes=[("mcha",)])
                        bank = sub + 4 * (igi % 2)
                        em.op("V", lambda e, sub=sub, iw=iw, bank=bank:
                              e.tensor_tensor(
                            mscr[:, 0:iw], mch_a[:, 0:iw], PB[bank][:, 0:iw],
                            ALU.mult),
                            reads=[("mcha",), ("PB", bank)], writes=[("OBK0",)])
                        nb = iw // 128
                        em.wait_only("T", reads=[("OBK0",), ("identF",)],
                                     writes=[("PB", bank)])
                        for j in range(nb):
                            em.raw_inc("T", lambda e, j=j, bank=bank:
                                       e.transpose(PB[bank][:, j * 128:(j + 1) * 128],
                                                   mscr[:, j * 128:(j + 1) * 128],
                                                   identF[:]),
                                       reads=[("OBK0",), ("identF",)],
                                       writes=[("PB", bank)],
                                       inc=(1 if j == nb - 1 else 0))
                        em.op("V", lambda e, sub=sub, i0=i0, iw=iw, bank=bank:
                              e.tensor_copy(
                                  s_buf[:, sub * I + i0: sub * I + i0 + iw],
                                  PB[bank][:, 0:iw]),
                              reads=[("PB", bank)], writes=[("s", sub, igi)])

            pend_mask = []
            for igi, (i0, iw) in enumerate(cfg.IGS):
                banks = [s_ + 4 * (igi % 2) for s_ in range(SUBS)]
                for cbi, (c0, nc8) in enumerate(cfg.CBS):
                    off = _off_ck(cfg, cfg.IGS, cfg.CBS, igi, cbi)
                    k = stream_tile(wup_h, off, 128 * nc8 * iw, nc8, iw)
                    em.wait_only("T", reads=[("AR", k)] + XNT_R,
                                 writes=([("PB", b) for b in banks]
                                         if cbi == 0 else []))
                    last_cb = cbi == len(cfg.CBS) - 1
                    for c8 in range(nc8):
                        c = c0 + c8
                        for sub in range(SUBS):
                            st = (cbi == 0 and c8 == 0)
                            sp = (last_cb and c8 == nc8 - 1)
                            em.raw_inc(
                                "T",
                                lambda e, c=c, sub=sub, k=k, c8=c8, iw=iw,
                                       nc8=nc8, st=st, sp=sp, bank=banks[sub]:
                                e.matmul(PB[bank][:, 0:iw],
                                         xnT[:, (c * SUBS + sub) * 128:
                                             (c * SUBS + sub) * 128 + 128],
                                         arslice(k, nc8, iw, c8),
                                         start=st, stop=sp),
                                reads=[("AR", k)],
                                writes=([("PB", banks[sub])] if sp else []),
                                inc=(1 if (c8 == nc8 - 1 and sub == SUBS - 1)
                                     or sp else 0))
                pend_mask.append((igi, i0, iw))
                if len(pend_mask) > 1:
                    _ph2a_mask(*pend_mask.pop(0))
            while pend_mask:
                _ph2a_mask(*pend_mask.pop(0))

            next_ln = _ln_sub_emitters(grp + 1) if grp + 1 < GROUPS else []
            nbb = len(cfg.BBS)
            for qi, (h0, hw_) in enumerate(cfg.HCH):
                banks = [s_ + 4 * (qi % 2) for s_ in range(SUBS)]
                for bsi, (b0, nb) in enumerate(cfg.BBS):
                    off = 0
                    for q2, (hh0, hhw) in enumerate(cfg.HCH):
                        for b2, (bb0, bnb) in enumerate(cfg.BBS):
                            if q2 == qi and b2 == bsi:
                                break
                            off += 128 * bnb * hhw
                        else:
                            continue
                        break
                    k = stream_tile(wdp_h, off, 128 * nb * hw_, nb, hw_)
                    em.wait_only("T", reads=[("AR", k)] +
                                 ([("s", s_, i_) for s_ in range(SUBS)
                                   for i_ in range(nIG)] if bsi == 0 else []),
                                 writes=([("PB", b) for b in banks]
                                         if bsi == 0 else []))
                    for jb in range(nb):
                        blk = b0 + jb
                        for sub in range(SUBS):
                            st = (bsi == 0 and jb == 0)
                            sp = (bsi == nbb - 1 and jb == nb - 1)
                            rds = [("AR", k)]
                            if sp:
                                rds = rds + [("s", sub, i_) for i_ in range(nIG)]
                            em.raw_inc(
                                "T",
                                lambda e, blk=blk, sub=sub, k=k, jb=jb, hw_=hw_,
                                       nb=nb, st=st, sp=sp, bank=banks[sub]:
                                e.matmul(PB[bank][:, 0:hw_],
                                         s_buf[:, sub * I + blk * 128:
                                               sub * I + blk * 128 + 128],
                                         arslice(k, nb, hw_, jb),
                                         start=st, stop=sp),
                                reads=rds,
                                writes=([("PB", banks[sub])] if sp else []),
                                inc=(1 if (jb == nb - 1 and sub == SUBS - 1)
                                     or sp else 0))
                em.dma("XQ", lambda e, h0=h0, hw_=hw_, t0=tok0:
                       e.dma_start(
                           out=xq[:, 0:SUBS * 512]
                           .rearrange("p (a b) -> p a b", a=SUBS)[:, :, 0:hw_],
                           in_=x_h[t0:t0 + SUBS * 128, h0:h0 + hw_]
                           .rearrange("(s p) h -> p s h", s=SUBS)),
                       writes=[("xq", s_) for s_ in range(SUBS)])
                for sub in range(SUBS):
                    bank = banks[sub]
                    r0 = tok0 + sub * 128
                    o = (qi * SUBS + sub) % 2
                    em.op("V", lambda e, sub=sub, bank=bank, h0=h0, hw_=hw_, o=o:
                          e.tensor_tensor(ob[o][:, 0:hw_], PB[bank][:, 0:hw_],
                                          g_sb[:, sub * H + h0:
                                               sub * H + h0 + hw_],
                                          ALU.mult),
                          reads=[("PB", bank), ("g", sub, qi)],
                          writes=[("OBK%d" % o,)])
                    em.op("V", lambda e, sub=sub, hw_=hw_, o=o:
                          e.tensor_tensor(ob[o][:, 0:hw_], ob[o][:, 0:hw_],
                                          xq[:, sub * 512: sub * 512 + hw_],
                                          ALU.add),
                          reads=[("OBK%d" % o,), ("xq", sub)],
                          writes=[("OBK%d" % o,)])
                    em.dma("OB%d" % o, lambda e, r0=r0, h0=h0, hw_=hw_, o=o:
                           e.dma_start(out=out_h[r0:r0 + 128, h0:h0 + hw_],
                                       in_=ob[o][:, 0:hw_]),
                           reads=[("OBK%d" % o,)])
                if grp + 1 < GROUPS and next_ln:
                    next_ln.pop(0)(bank_base=4 * (qi % 2))
            while grp + 1 < GROUPS and next_ln:
                next_ln.pop(0)(bank_base=6)

        em.prog["X"].append(("w", "OB0", em.cnt["OB0"]))
        em.prog["X"].append(("w", "OB1", em.cnt["OB1"]))

        sem_names = ["T", "V", "S"] + list(Em.DSEMS)
        with ExitStack() as sctx:
            sems = {n: sctx.enter_context(nc.semaphore("sem" + n))
                    for n in sem_names}
            block = sctx.enter_context(nc.Block())

            @block.tensor
            def _(e):
                em.replay("T", e, sems)

            @block.vector
            def _(e):
                em.replay("V", e, sems)

            @block.scalar
            def _(e):
                em.replay("S", e, sems)

            @block.gpsimd
            def _(e):
                em.replay("G", e, sems)

            @block.sync
            def _(e):
                em.replay("X", e, sems)

    return nc


def _base_rounds(em, cfg, stream_tile, arslice, wbp_h, PB, xnT, ones_sb,
                 brow_b, hwT, bflat, g_sb, XNT_R):
    SUBS, H, ER = cfg.SUBS, cfg.H, cfg.ER
    rounds = []
    for qi, (h0, hw_) in enumerate(cfg.HCH):
        def round_fn(qi=qi, h0=h0, hw_=hw_):
            banks = [s_ + 4 * (qi % 2) for s_ in range(SUBS)]
            for cbi, (c0, nc8) in enumerate(cfg.CBS):
                off = _off_ck(cfg, cfg.HCH, cfg.CBS, qi, cbi)
                k = stream_tile(wbp_h, off, 128 * nc8 * hw_, nc8, hw_)
                em.wait_only("T", reads=[("AR", k)] + XNT_R,
                             writes=([("PB", b) for b in banks]
                                     if cbi == 0 else []))
                for c8 in range(nc8):
                    c = c0 + c8
                    for sub in range(SUBS):
                        st = (cbi == 0 and c8 == 0)
                        em.raw_inc(
                            "T",
                            lambda e, c=c, sub=sub, k=k, c8=c8, nc8=nc8,
                                   hw_=hw_, st=st, bank=banks[sub]:
                            e.matmul(PB[bank][:, 0:hw_],
                                     xnT[:, (c * SUBS + sub) * 128:
                                         (c * SUBS + sub) * 128 + 128],
                                     arslice(k, nc8, hw_, c8),
                                     start=st, stop=False),
                            reads=[("AR", k)], writes=[],
                            inc=(1 if (c8 == nc8 - 1 and sub == SUBS - 1)
                                 else 0))
            em.wait_only("T", reads=[("browb",), ("ones",), ("bflat",)] +
                         [("hwT", s_) for s_ in range(SUBS)])
            for sub in range(SUBS):
                bank = banks[sub]
                em.raw_inc("T", lambda e, bank=bank, h0=h0, hw_=hw_:
                           e.matmul(PB[bank][:, 0:hw_], ones_sb[:],
                                    brow_b[:, h0:h0 + hw_],
                                    start=False, stop=False),
                           reads=[("browb",), ("ones",)], writes=[], inc=0)
                em.raw_inc("T", lambda e, bank=bank, sub=sub, h0=h0, hw_=hw_:
                           e.matmul(PB[bank][:, 0:hw_],
                                    hwT[:, sub * ER:(sub + 1) * ER],
                                    bflat[:, h0:h0 + hw_],
                                    start=False, stop=True),
                           reads=[("hwT", sub), ("bflat",)],
                           writes=[("PB", bank)], inc=1)
                em.op("S", lambda e, bank=bank, sub=sub, h0=h0, hw_=hw_:
                      e.activation(g_sb[:, sub * H + h0: sub * H + h0 + hw_],
                                   PB[bank][:, 0:hw_], AF.Silu),
                      reads=[("PB", bank)], writes=[("g", sub, qi)])
        rounds.append(round_fn)
    return rounds


def _bisect_program(em, cfg, s_buf, st, snkV, snkS):
    SUBS, I, K = cfg.SUBS, cfg.I, cfg.K
    nIG = len(cfg.IGS)
    NSUB = min(1024, I)
    KA = K * NSUB // I
    if SUBS == 4:
        v_passes = [(0, 0), (0, 1), (1, 0)]
        s_passes = [(1, 1), (2, 0), (2, 1), (3, 0), (3, 1)]
    else:
        v_passes = [(s_, sd) for s_ in range((SUBS + 1) // 2) for sd in (0, 1)]
        s_passes = [(s_, sd) for s_ in range((SUBS + 1) // 2, SUBS)
                    for sd in (0, 1)]
    vset = set(v_passes)
    trA_vals = [1.0 if (s_, 0) in vset else 0.5 for s_ in range(SUBS)]
    trB_vals = [1.0 if (s_, 1) in vset else 0.5 for s_ in range(SUBS)]
    trC_vals = [(0 if (s_, 0) in vset else 1) + (0 if (s_, 1) in vset else 1)
                for s_ in range(SUBS)]
    st["_tr_init"](trA_vals, trB_vals, trC_vals)
    trA, trB, trC = st["trA"], st["trB"], st["trC"]
    st_tr1, st_tr2 = st["tr1"], st["tr2"]
    A_ITERS = 12
    B_ITERS = cfg.B_ITERS
    lo, hi, mid, nmid = st["lo"], st["hi"], st["mid"], st["nmid"]
    cp, cn, cnt, msk, mski = st["cp"], st["cn"], st["cnt"], st["msk"], st["mski"]
    flo, fhi, hi0, w_, num, den = (st["flo"], st["fhi"], st["hi0"], st["w"],
                                   st["num"], st["den"])
    floi, fhii, pmsk = st["floi"], st["fhii"], st["pmsk"]

    nred_of = lambda width: min((width + 511) // 512, nIG)

    def count_round(width, kthr, track_f, upd=True):
        em.op("V", lambda e: e.tensor_scalar(nmid[:], mid[:], -1.0, None,
                                             ALU.mult),
              reads=[("stmid",)], writes=[("stnmid",)])
        for (sub, side) in v_passes:
            srd = [("s", sub, i_) for i_ in range(nred_of(width))]
            dst = cp if side == 0 else cn
            thr = mid if side == 0 else nmid
            opx = ALU.is_gt if side == 0 else ALU.is_lt
            tk = ("stmid",) if side == 0 else ("stnmid",)
            ck = ("stcp", sub) if side == 0 else ("stcn", sub)
            em.op("V", lambda e, sub=sub, width=width, dst=dst, thr=thr,
                  opx=opx: e.tensor_scalar(
                snkV[:, 0:1].broadcast_to([128, width]),
                s_buf[:, sub * I: sub * I + width],
                thr[:, sub:sub + 1], None, opx, op1=ALU.add,
                accum_out=dst[:, sub:sub + 1]),
                reads=[tk] + srd, writes=[("snkV",), ck])
        for (sub, side) in s_passes:
            srd = [("s", sub, i_) for i_ in range(nred_of(width))]
            dst = cp if side == 0 else cn
            sc = 1.0 if side == 0 else -1.0
            ck = ("stcp", sub) if side == 0 else ("stcn", sub)
            em.op("S", lambda e, sub=sub, width=width, dst=dst, sc=sc:
                  e.activation(
                snkS[:, 0:1].broadcast_to([128, width]),
                s_buf[:, sub * I: sub * I + width], AF.Sign,
                bias=nmid[:, sub:sub + 1], scale=sc,
                accum_out=dst[:, sub:sub + 1]),
                reads=[("stnmid",)] + srd, writes=[("snkS",), ck])
        em.op("V", lambda e: e.tensor_tensor(st_tr1[:], cp[:], trA[:],
                                             ALU.mult),
              reads=[("stcp", s_) for s_ in range(SUBS)] + [("trA",)],
              writes=[("sttr1",)])
        em.op("V", lambda e: e.tensor_tensor(st_tr2[:], cn[:], trB[:],
                                             ALU.mult),
              reads=[("stcn", s_) for s_ in range(SUBS)] + [("trB",)],
              writes=[("sttr2",)])
        em.op("V", lambda e: e.tensor_tensor(cnt[:], st_tr1[:], st_tr2[:],
                                             ALU.add),
              reads=[("sttr1",), ("sttr2",)], writes=[("stcnt",)])
        em.op("V", lambda e, width=width: e.scalar_tensor_tensor(
            cnt[:], trC[:], 0.5 * width, cnt[:], ALU.mult, ALU.add),
            reads=[("stcnt",), ("trC",)], writes=[("stcnt",)])
        if not upd:
            return
        em.op("V", lambda e, kthr=kthr: e.tensor_scalar(
            msk[:], cnt[:], kthr, None, ALU.is_gt),
            reads=[("stcnt",)], writes=[("stmsk",)])
        em.op("V", lambda e, kthr=kthr: e.tensor_scalar(
            mski[:], cnt[:], kthr, None, ALU.is_le),
            reads=[("stcnt",)], writes=[("stmski",)])
        em.op("V", lambda e: e.select(lo[:], msk[:], mid[:], lo[:],
                                      add_drain=True),
              reads=[("stmsk",), ("stmid",), ("stlo",)], writes=[("stlo",)])
        em.op("V", lambda e: e.select(hi[:], mski[:], mid[:], hi[:],
                                      add_drain=True),
              reads=[("stmski",), ("stmid",), ("sthi",)], writes=[("sthi",)])
        if track_f:
            em.op("V", lambda e: e.select(flo[:], msk[:], cnt[:], flo[:],
                                          add_drain=True),
                  reads=[("stmsk",), ("stcnt",), ("stflo",)],
                  writes=[("stflo",)])
            em.op("V", lambda e: e.select(fhi[:], mski[:], cnt[:], fhi[:],
                                          add_drain=True),
                  reads=[("stmski",), ("stcnt",), ("stfhi",)],
                  writes=[("stfhi",)])
            em.op("V", lambda e: e.select(floi[:], msk[:], cnt[:], floi[:],
                                          add_drain=True),
                  reads=[("stmsk",), ("stcnt",), ("stfloi",)],
                  writes=[("stfloi",)])
            em.op("V", lambda e: e.select(fhii[:], mski[:], cnt[:], fhii[:],
                                          add_drain=True),
                  reads=[("stmski",), ("stcnt",), ("stfhii",)],
                  writes=[("stfhii",)])
            em.op("V", lambda e: e.tensor_tensor(st["tr1"][:], msk[:], pmsk[:],
                                                 ALU.mult),
                  reads=[("stmsk",), ("stpmsk",)], writes=[("sttr1",)])
            em.op("V", lambda e: e.tensor_scalar(
                st["msk2"][:], st["tr1"][:], 0.5, None, ALU.is_gt),
                reads=[("sttr1",)], writes=[("stmsk2",)])
            em.op("V", lambda e, K2=None: e.tensor_scalar(
                st["tr2"][:], fhii[:], 0.5, 0.5 * (cfg.K + 0.5), ALU.mult,
                op1=ALU.add),
                reads=[("stfhii",)], writes=[("sttr2",)])
            em.op("V", lambda e: e.select(fhii[:], st["msk2"][:], st["tr2"][:],
                                          fhii[:], add_drain=True),
                  reads=[("stmsk2",), ("sttr2",), ("stfhii",)],
                  writes=[("stfhii",)])
            em.op("V", lambda e: e.tensor_tensor(
                st["tr1"][:], mski[:], st["pmski"][:], ALU.mult),
                reads=[("stmski",), ("stpmski",)], writes=[("sttr1",)])
            em.op("V", lambda e: e.tensor_scalar(
                st["msk2"][:], st["tr1"][:], 0.5, None, ALU.is_gt),
                reads=[("sttr1",)], writes=[("stmsk2",)])
            em.op("V", lambda e: e.tensor_scalar(
                st["tr2"][:], floi[:], 0.5, 0.5 * (cfg.K + 0.5), ALU.mult,
                op1=ALU.add),
                reads=[("stfloi",)], writes=[("sttr2",)])
            em.op("V", lambda e: e.select(floi[:], st["msk2"][:], st["tr2"][:],
                                          floi[:], add_drain=True),
                  reads=[("stmsk2",), ("sttr2",), ("stfloi",)],
                  writes=[("stfloi",)])
            em.op("V", lambda e: e.tensor_copy(pmsk[:], msk[:]),
                  reads=[("stmsk",)], writes=[("stpmsk",)])
            em.op("V", lambda e: e.tensor_copy(st["pmski"][:], mski[:]),
                  reads=[("stmski",)], writes=[("stpmski",)])

    a_list = []
    iters = []
    for it in range(A_ITERS):
        def a_iter(it=it):
            em.op("V", lambda e: e.tensor_tensor(mid[:], lo[:], hi[:], ALU.add),
                  reads=[("stlo",), ("sthi",)], writes=[("stmid",)])
            em.op("V", lambda e: e.tensor_scalar(mid[:], mid[:], 0.5, None,
                                                 ALU.mult),
                  reads=[("stmid",)], writes=[("stmid",)])
            count_round(NSUB, KA + 0.5, False)
        a_list.append(a_iter)

    def b_init():
        em.op("V", lambda e: e.tensor_scalar(
            den[:], hi0[:], 0.014, None, ALU.mult),
            reads=[("sthi0",)], writes=[("stden",)])
        em.op("V", lambda e: e.tensor_tensor(lo[:], lo[:], den[:],
                                             ALU.subtract),
              reads=[("stlo",), ("stden",)], writes=[("stlo",)])
        em.op("V", lambda e: e.tensor_scalar(lo[:], lo[:], 0.0, None, ALU.max),
              reads=[("stlo",)], writes=[("stlo",)])
        em.op("V", lambda e: e.tensor_tensor(hi[:], hi[:], den[:], ALU.add),
              reads=[("sthi",), ("stden",)], writes=[("sthi",)])
        em.op("V", lambda e, v=float(cfg.K + 1300): e.memset(flo[:], v),
              writes=[("stflo",)])
        em.op("V", lambda e, v=float(cfg.K - 1300): e.memset(fhi[:], v),
              writes=[("stfhi",)])
        em.op("V", lambda e, v=float(cfg.K + 1300): e.memset(floi[:], v),
              writes=[("stfloi",)])
        em.op("V", lambda e, v=float(cfg.K - 1300): e.memset(fhii[:], v),
              writes=[("stfhii",)])
        em.op("V", lambda e: e.memset(pmsk[:], 0.0), writes=[("stpmsk",)])
        em.op("V", lambda e: e.memset(st["pmski"][:], 0.0),
              writes=[("stpmski",)])
    iters.append(b_init)

    for it in range(B_ITERS):
        def b_iter(it=it):
            if it == 0:
                em.op("V", lambda e: e.tensor_tensor(mid[:], lo[:], hi[:],
                                                     ALU.add),
                      reads=[("stlo",), ("sthi",)], writes=[("stmid",)])
                em.op("V", lambda e: e.tensor_scalar(mid[:], mid[:], 0.5, None,
                                                     ALU.mult),
                      reads=[("stmid",)], writes=[("stmid",)])
            else:
                em.op("V", lambda e: e.tensor_tensor(w_[:], hi[:], lo[:],
                                                     ALU.subtract),
                      reads=[("sthi",), ("stlo",)], writes=[("stw",)])
                em.op("V", lambda e: e.tensor_scalar(
                    num[:], floi[:], -(cfg.K + 0.5), None, ALU.add),
                    reads=[("stfloi",)], writes=[("stnum",)])
                em.op("V", lambda e: e.tensor_tensor(den[:], floi[:], fhii[:],
                                                     ALU.subtract),
                      reads=[("stfloi",), ("stfhii",)], writes=[("stden",)])
                em.op("V", lambda e: e.tensor_scalar(den[:], den[:], 1.0, None,
                                                     ALU.max),
                      reads=[("stden",)], writes=[("stden",)])
                em.op("V", lambda e: e.reciprocal(den[:], den[:]),
                      reads=[("stden",)], writes=[("stden",)])
                em.op("V", lambda e: e.tensor_tensor(num[:], num[:], den[:],
                                                     ALU.mult),
                      reads=[("stnum",), ("stden",)], writes=[("stnum",)])
                em.op("V", lambda e: e.tensor_scalar(
                    num[:], num[:], 1.0 / 64, 1.0 - 1.0 / 64, ALU.max,
                    op1=ALU.min),
                    reads=[("stnum",)], writes=[("stnum",)])
                em.op("V", lambda e: e.tensor_tensor(num[:], num[:], w_[:],
                                                     ALU.mult),
                      reads=[("stnum",), ("stw",)], writes=[("stnum",)])
                em.op("V", lambda e: e.tensor_tensor(mid[:], lo[:], num[:],
                                                     ALU.add),
                      reads=[("stlo",), ("stnum",)], writes=[("stmid",)])
            count_round(I, cfg.K + 0.5, True)
        iters.append(b_iter)

    def tie_pick():
        em.op("V", lambda e: e.tensor_copy(mid[:], st["th"][:]),
              reads=[("stth", s_) for s_ in range(SUBS)], writes=[("stmid",)])
        count_round(I, cfg.K + 0.5, False, upd=False)
        em.op("V", lambda e: e.tensor_scalar(
            num[:], cnt[:], float(-cfg.K), None, ALU.add),
            reads=[("stcnt",)], writes=[("stnum",)])
        em.op("V", lambda e: e.scalar_tensor_tensor(
            den[:], num[:], -1.0, num[:], ALU.mult, ALU.max),
            reads=[("stnum",)], writes=[("stden",)])
        em.op("V", lambda e: e.tensor_scalar(
            w_[:], st["flo"][:], float(-cfg.K), None, ALU.add),
            reads=[("stflo",)], writes=[("stw",)])
        em.op("V", lambda e: e.tensor_tensor(
            st["msk2"][:], den[:], w_[:], ALU.is_le),
            reads=[("stden",), ("stw",)], writes=[("stmsk2",)])
        em.op("V", lambda e: e.select(st["th"][:], st["msk2"][:], mid[:],
                                      lo[:], add_drain=True),
              reads=[("stmsk2",), ("stmid",), ("stlo",)],
              writes=[("stth", s_) for s_ in range(SUBS)])
    return a_list, iters, tie_pick


def _fixup(em, cfg, s_buf, st_lo, st_hi, st_flo, st_th, st_d, ascr, bscr,
           t1scr, qscr, cm8, fm8, bv9, oh9, iota9_s):
    SUBS, I, K = cfg.SUBS, cfg.I, cfg.K
    nIG = len(cfg.IGS)
    for sub in range(SUBS):
        for igi, (i0, iw) in enumerate(cfg.IGS):
            em.op("S", lambda e, sub=sub, i0=i0, iw=iw: e.activation(
                ascr[:, 0:iw], s_buf[:, sub * I + i0: sub * I + i0 + iw],
                AF.Abs),
                reads=[("s", sub, igi)], writes=[("mcha",)])
            em.op("V", lambda e, sub=sub, iw=iw: e.tensor_scalar(
                bscr[:, 0:iw], ascr[:, 0:iw], st_lo[:, sub:sub + 1], 0.5,
                ALU.is_le, op1=ALU.subtract),
                reads=[("mcha",), ("stlo",)], writes=[("mchb",)])
            em.op("V", lambda e, sub=sub, iw=iw: e.scalar_tensor_tensor(
                qscr[:, 0:iw], ascr[:, 0:iw], st_hi[:, sub:sub + 1],
                bscr[:, 0:iw], ALU.subtract, ALU.mult),
                reads=[("mcha",), ("mchb",), ("sthi",)], writes=[("OBK1",)])
            em.op("V", lambda e, igi=igi, iw=iw: e.max(
                out=cm8[:, igi * 8:(igi + 1) * 8], in_=qscr[:, 0:iw]),
                reads=[("OBK1",)], writes=[("cm8", igi)])
        em.op("V", lambda e: e.max(out=fm8[:], in_=cm8[:]),
              reads=[("cm8", i_) for i_ in range(nIG)], writes=[("fm8",)])
        em.op("V", lambda e, sub=sub: e.tensor_copy(bv9[:, 0:1],
                                                    st_lo[:, sub:sub + 1]),
              reads=[("stlo",)], writes=[("bv9",)])
        em.op("V", lambda e, sub=sub: e.tensor_scalar(
            bv9[:, 1:9], fm8[:], -2.0, st_hi[:, sub:sub + 1], ALU.mult,
            op1=ALU.add),
            reads=[("fm8",), ("sthi",)], writes=[("bv9",)])
        em.op("V", lambda e, sub=sub: e.tensor_scalar(
            st_d[:, sub:sub + 1], st_flo[:, sub:sub + 1], float(-K), 8.0,
            ALU.add, op1=ALU.min),
            reads=[("stflo",)], writes=[("std", sub)])
        em.op("V", lambda e, sub=sub: e.tensor_scalar(
            st_d[:, sub:sub + 1], st_d[:, sub:sub + 1], 0.0, None, ALU.max),
            reads=[("std", sub)], writes=[("std", sub)])
        em.op("V", lambda e, sub=sub: e.scalar_tensor_tensor(
            oh9[:, 0:9], iota9_s[:, 0:9], st_d[:, sub:sub + 1], bv9[:, 0:9],
            ALU.is_equal, ALU.mult, accum_out=st_th[:, sub:sub + 1]),
            reads=[("iota9",), ("std", sub), ("bv9",)],
            writes=[("oh9",), ("stth", sub)])


def _sparsemax(em, cfg, lg, lgs, zc, zc2, sup, supn, kz, kzr, tau, wrt, srt_t,
               iotaE_s):
    SUBS, E = cfg.SUBS, cfg.E
    W = SUBS * E

    def v3(t):
        return t[:, 0:W].rearrange("p (a b) -> p a b", a=SUBS)

    em.op("V", lambda e: e.tensor_copy(lgs[:, 0:W], lg[:, 0:W]),
          reads=[("lg", s_) for s_ in range(SUBS)], writes=[("lgs",)])
    for rnd in range(E):
        if rnd % 2 == 0:
            a0, b0, npair = 0, 1, E // 2
        else:
            a0, b0, npair = 1, 2, (E - 1) // 2
        em.op("V", lambda e, a0=a0, b0=b0, npair=npair: e.tensor_tensor(
            srt_t[:, 0:SUBS * npair].rearrange("p (x y) -> p x y", x=SUBS),
            v3(lgs)[:, :, a0:a0 + 2 * npair - 1:2],
            v3(lgs)[:, :, b0:b0 + 2 * npair - 1:2], ALU.max),
            reads=[("lgs",)], writes=[("srtt",)])
        em.op("V", lambda e, a0=a0, b0=b0, npair=npair: e.tensor_tensor(
            v3(lgs)[:, :, b0:b0 + 2 * npair - 1:2],
            v3(lgs)[:, :, a0:a0 + 2 * npair - 1:2],
            v3(lgs)[:, :, b0:b0 + 2 * npair - 1:2], ALU.min),
            reads=[("lgs",)], writes=[("lgs",)])
        em.op("V", lambda e, a0=a0, npair=npair: e.tensor_copy(
            v3(lgs)[:, :, a0:a0 + 2 * npair - 1:2],
            srt_t[:, 0:SUBS * npair].rearrange("p (x y) -> p x y", x=SUBS)),
            reads=[("srtt",)], writes=[("lgs",)])
    em.op("V", lambda e: e.tensor_copy(zc[:, 0:W], lgs[:, 0:W]),
          reads=[("lgs",)], writes=[("zc",)])
    cur, oth, curk, othk = zc, zc2, "zc", "zc2"
    sh = 1
    while sh < E:
        em.op("V", lambda e, sh=sh, cur=cur, oth=oth: e.tensor_tensor(
            v3(oth)[:, :, sh:E], v3(cur)[:, :, sh:E], v3(cur)[:, :, 0:E - sh],
            ALU.add),
            reads=[(curk,)], writes=[(othk,)])
        em.op("V", lambda e, sh=sh, cur=cur, oth=oth: e.tensor_copy(
            v3(oth)[:, :, 0:sh], v3(cur)[:, :, 0:sh]),
            reads=[(curk,)], writes=[(othk,)])
        cur, oth, curk, othk = oth, cur, othk, curk
        sh *= 2
    em.op("V", lambda e: e.tensor_tensor(
        sup[:, 0:W], lgs[:, 0:W], iotaE_s[:, 0:W], ALU.mult),
        reads=[("lgs",), ("iotaE",)], writes=[("sup",)])
    em.op("V", lambda e: e.tensor_scalar(
        sup[:, 0:W], sup[:, 0:W], 1.0, None, ALU.add),
        reads=[("sup",)], writes=[("sup",)])
    em.op("V", lambda e, cur=cur: e.tensor_tensor(
        sup[:, 0:W], sup[:, 0:W], cur[:, 0:W], ALU.is_gt),
        reads=[("sup",), (curk,)], writes=[("sup",)])
    for sub in range(SUBS):
        em.op("V", lambda e, sub=sub: e.reduce_sum(
            kz[:, sub:sub + 1], sup[:, sub * E:(sub + 1) * E], AX),
            reads=[("sup",)], writes=[("kz",)])
    em.op("V", lambda e: e.memset(supn[:, 0:W], 0.0), writes=[("supn",)])
    em.op("V", lambda e: e.tensor_copy(
        v3(supn)[:, :, 0:E - 1], v3(sup)[:, :, 1:E]),
        reads=[("sup",)], writes=[("supn",)])
    em.op("V", lambda e: e.tensor_tensor(
        supn[:, 0:W], sup[:, 0:W], supn[:, 0:W], ALU.subtract),
        reads=[("sup",), ("supn",)], writes=[("supn",)])
    em.op("V", lambda e, cur=cur: e.tensor_tensor(
        supn[:, 0:W], supn[:, 0:W], cur[:, 0:W], ALU.mult),
        reads=[("supn",), (curk,)], writes=[("supn",)])
    for sub in range(SUBS):
        em.op("V", lambda e, sub=sub: e.reduce_sum(
            tau[:, sub:sub + 1], supn[:, sub * E:(sub + 1) * E], AX),
            reads=[("supn",)], writes=[("tau",)])
    em.op("V", lambda e: e.tensor_scalar(
        tau[:, 0:SUBS], tau[:, 0:SUBS], -1.0, None, ALU.add),
        reads=[("tau",)], writes=[("tau",)])
    em.op("V", lambda e: e.reciprocal(kzr[:, 0:SUBS], kz[:, 0:SUBS]),
          reads=[("kz",)], writes=[("kzr",)])
    em.op("V", lambda e: e.tensor_tensor(
        tau[:, 0:SUBS], tau[:, 0:SUBS], kzr[:, 0:SUBS], ALU.mult),
        reads=[("tau",), ("kzr",)], writes=[("tau",)])
    em.op("V", lambda e: e.tensor_tensor(
        v3(wrt), v3(lg),
        tau[:, 0:SUBS][:, :, None].broadcast_to([128, SUBS, E]),
        ALU.subtract),
        reads=[("lg", s_) for s_ in range(SUBS)] + [("tau",)],
        writes=[("wrt",)])
    em.op("V", lambda e: e.tensor_scalar(
        wrt[:, 0:W], wrt[:, 0:W], 0.0, None, ALU.max),
        reads=[("wrt",)], writes=[("wrt",)])



def kernel(L0_attn_x, L0_attn_w_gate, L0_attn_w_up, L0_attn_w_down,
           L0_ffn_W_base, L0_ffn_b_base, L0_ffn_W_router, L0_ffn_b_router,
           L0_ffn_lora_A, L0_ffn_lora_B):
    cfg = Cfg()
    xin = np.asarray(L0_attn_x)
    x = np.ascontiguousarray(xin.astype(np.float32, copy=False).reshape(-1, cfg.H))
    packed = pack_weights(
        cfg,
        np.asarray(L0_attn_w_gate, np.float32),
        np.asarray(L0_attn_w_up, np.float32),
        np.asarray(L0_attn_w_down, np.float32),
        np.asarray(L0_ffn_W_base, np.float32),
        np.asarray(L0_ffn_b_base, np.float32),
        np.asarray(L0_ffn_W_router, np.float32),
        np.asarray(L0_ffn_b_router, np.float32),
        np.asarray(L0_ffn_lora_A, np.float32),
        np.asarray(L0_ffn_lora_B, np.float32))
    nc = build(cfg)
    n = cfg.N_CORES
    per = x.shape[0] // n
    in_maps = []
    for c in range(n):
        m = dict(packed)
        m["x"] = np.ascontiguousarray(x[c * per:(c + 1) * per])
        in_maps.append(m)
    res = run_bass_kernel_spmd(nc, in_maps, core_ids=list(range(n)))
    out = np.concatenate([r["out"] for r in res.results], axis=0)
    return out.reshape(xin.shape).astype(np.float32)



# version 18 (final)
# speedup vs baseline: 1.1240x; 1.0211x over previous
import numpy as np
import concourse.bass as bass
import concourse.mybir as mybir
from concourse.bass_utils import run_bass_kernel_spmd
from contextlib import ExitStack

F32 = mybir.dt.float32
BF16 = mybir.dt.bfloat16
F16 = mybir.dt.float16
AF = mybir.ActivationFunctionType
ALU = mybir.AluOpType
AX = mybir.AxisListType.X

NEG_PUSH = -1000.0
BIG_F = 1.0e9


def chunks(total, size):
    out, o = [], 0
    while o < total:
        w = min(size, total - o)
        out.append((o, w))
        o += w
    return out


class Cfg:
    def __init__(self, H=4096, I=11008, TOK=1024, TG=512, E=8, R=16,
                 B_ITERS=8, FIXUP=True, N_CORES=8):
        self.H, self.I, self.TOK, self.TG, self.E, self.R = H, I, TOK, TG, E, R
        self.B_ITERS, self.FIXUP, self.N_CORES = B_ITERS, FIXUP, N_CORES
        self.HC = H // 128
        self.GROUPS = TOK // TG
        self.SUBS = TG // 128
        self.K = I // 2
        self.IGS = chunks(I, 512)
        self.NBLK = I // 128
        self.HCH = chunks(H, 512)
        self.ER = E * R
        self.F0W = E + self.ER
        self.CBS = chunks(self.HC, 4)
        self.F0CB = chunks(self.HC, 8)
        self.BBS = chunks(self.NBLK, 4)
        self.LNH = chunks(H, 1024)



def pack_weights(cfg, w_gate, w_up, w_down, W_base, b_base, W_router, b_router,
                 lora_A, lora_B):
    H, I = cfg.H, cfg.I

    def pack_ck(wT, colchunks, cbs):
        w3 = wT.reshape(cfg.HC, 128, wT.shape[1])
        parts = []
        for (i0, iw) in colchunks:
            for (c0, nc) in cbs:
                t = w3[c0:c0 + nc, :, i0:i0 + iw]
                parts.append(np.ascontiguousarray(t.transpose(1, 0, 2)).ravel())
        return np.concatenate(parts)

    wgT = np.ascontiguousarray(w_gate.T)
    wuT = np.ascontiguousarray(w_up.T)
    wgp = pack_ck(wgT, cfg.IGS, cfg.CBS)
    wup = pack_ck(wuT, cfg.IGS, cfg.CBS)

    wdT = np.ascontiguousarray(w_down.T)
    w3 = wdT.reshape(cfg.NBLK, 128, H)
    parts = []
    for (h0, hw) in cfg.HCH:
        for (b0, nb) in cfg.BBS:
            t = w3[b0:b0 + nb, :, h0:h0 + hw]
            parts.append(np.ascontiguousarray(t.transpose(1, 0, 2)).ravel())
    wdp = np.concatenate(parts)

    A_r = np.ascontiguousarray(lora_A.transpose(1, 0, 2)).reshape(H, cfg.ER)
    wra = np.concatenate([np.ascontiguousarray(W_router.T), A_r], axis=1)
    wrap_ = pack_ck(np.ascontiguousarray(wra), [(0, cfg.F0W)], cfg.F0CB)

    wbp = pack_ck(np.ascontiguousarray(W_base.T), cfg.HCH, cfg.CBS)

    bflat = np.ascontiguousarray(lora_B.reshape(cfg.ER, H))
    brow_wra = np.concatenate([b_router, np.zeros(cfg.ER, np.float32)])[None, :]
    brow_wb = np.ascontiguousarray(b_base[None, :])
    ident = np.eye(128, dtype=np.float32)
    iota9 = np.tile(np.arange(9, dtype=np.float32)[None, :], (128, 1))
    iotaE = np.tile(np.arange(1, cfg.E + 1, dtype=np.float32)[None, :],
                    (128, cfg.SUBS))
    onesrow = np.ones((1, 128), dtype=np.float32)
    return dict(wgp=wgp, wup=wup, wdp=wdp, wrap=wrap_, wbp=wbp, bflat=bflat,
                brow_wra=np.ascontiguousarray(brow_wra, np.float32),
                brow_wb=brow_wb.astype(np.float32),
                ident=ident, iota9=iota9, iotaE=iotaE, onesrow=onesrow)


def _off_ck(cfg, colchunks, cbs, ci, cbi):
    off = 0
    for i2, (i0, iw) in enumerate(colchunks):
        for c2, (c0, nc) in enumerate(cbs):
            if i2 == ci and c2 == cbi:
                return off
            off += 128 * nc * iw
    raise KeyError



class Em:
    ENGS = ("T", "V", "S", "G", "X")
    DSEMS = {"AR0": "G", "AR1": "G", "AR2": "G", "AR3": "G", "CG": "G",
             "XZ": "X", "XZ2": "X", "XQ": "X", "OB0": "X", "OB1": "X",
             "CX": "X"}

    def __init__(self):
        self.prog = {e: [] for e in self.ENGS}
        self.cnt = {e: 0 for e in self.ENGS}
        for s in self.DSEMS:
            self.cnt[s] = 0
        self.waited = {e: {} for e in self.ENGS}
        self.lastw = {}
        self.readers = {}

    def _emit_waits(self, eng, evts):
        need = {}
        for (se, v) in evts:
            if self.DSEMS.get(se) == eng:
                continue
            if v > self.waited[eng].get(se, 0):
                need[se] = max(need.get(se, 0), v)
        for se, v in need.items():
            self.waited[eng][se] = v
            self.prog[eng].append(("w", se, v))

    def _collect(self, reads, writes):
        evts = []
        for r in reads:
            ev = self.lastw.get(r)
            if ev is not None:
                evts.append(ev)
        for w in writes:
            ev = self.lastw.get(w)
            if ev is not None:
                evts.append(ev)
            evts.extend(self.readers.get(w, ()))
        return evts

    def _register(self, ev, reads, writes):
        for r in reads:
            self.readers.setdefault(r, []).append(ev)
        for w in writes:
            self.lastw[w] = ev
            self.readers[w] = []

    def op(self, eng, fn, reads=(), writes=(), inc=1):
        self._emit_waits(eng, self._collect(reads, writes))
        self.cnt[eng] += inc
        self.prog[eng].append(("i", fn, inc, eng))
        ev = (eng, self.cnt[eng])
        self._register(ev, reads, writes)
        return ev

    def dma(self, sem, fn, reads=(), writes=()):
        qeng = self.DSEMS[sem]
        self._emit_waits(qeng, self._collect(reads, writes))
        self.cnt[sem] += 16
        self.prog[qeng].append(("i", fn, 16, sem))
        ev = (sem, self.cnt[sem])
        self._register(ev, reads, writes)
        return ev

    def wait_only(self, eng, reads=(), writes=()):
        self._emit_waits(eng, self._collect(reads, writes))

    def raw_inc(self, eng, fn, reads=(), writes=(), inc=1):
        self.cnt[eng] += inc
        self.prog[eng].append(("i", fn, inc, eng))
        ev = (eng, self.cnt[eng])
        self._register(ev, reads, writes)
        return ev

    def force_event(self, region, ev):
        self.lastw[region] = ev
        self.readers[region] = []

    def replay(self, eng_name, eng_obj, sems):
        for item in self.prog[eng_name]:
            if item[0] == "w":
                eng_obj.wait_ge(sems[item[1]], item[2])
            else:
                inst = item[1](eng_obj)
                if item[2]:
                    inst.then_inc(sems[item[3]], item[2])



def build(cfg):
    nc = bass.Bass("TRN2", target_bir_lowering=False)
    H, I, E, R, ER, F0W = cfg.H, cfg.I, cfg.E, cfg.R, cfg.ER, cfg.F0W
    SUBS, GROUPS, K = cfg.SUBS, cfg.GROUPS, cfg.K
    nIG = len(cfg.IGS)
    nh = len(cfg.LNH)

    def din(name, shape):
        return nc.dram_tensor(name, list(shape), F32, kind="ExternalInput").ap()

    x_h = din("x", (cfg.TOK, H))
    wgp_h = din("wgp", (H * I,))
    wup_h = din("wup", (H * I,))
    wdp_h = din("wdp", (I * H,))
    wrap_h = din("wrap", (H * F0W,))
    wbp_h = din("wbp", (H * H,))
    bflat_h = din("bflat", (ER, H))
    browa_h = din("brow_wra", (1, F0W))
    browb_h = din("brow_wb", (1, H))
    ident_h = din("ident", (128, 128))
    iota9_h = din("iota9", (128, 9))
    iotaE_h = din("iotaE", (128, E * SUBS))
    ones_h = din("onesrow", (1, 128))
    out_h = nc.dram_tensor("out", [cfg.TOK, H], F32, kind="ExternalOutput").ap()

    em = Em()
    ctx = ExitStack()
    with ctx:
        def sbuf(name, shape, dt):
            return ctx.enter_context(nc.sbuf_tensor(name, list(shape), dt))

        s_buf = sbuf("s_buf", (128, SUBS * I), F16)
        xnT = sbuf("xnT", (128, cfg.HC * 128 * SUBS), F16)
        g_sb = sbuf("g_sb", (128, SUBS * H), F16)
        xz = sbuf("xz", (128, 1024), F32)
        ar = [sbuf("ar%d" % i_, (128, 2048), F16) for i_ in range(4)]
        bflat = sbuf("bflat_sb", (128, H), F16)
        mch_a = sbuf("mch_a", (128, 512), F16)
        mch_b = sbuf("mch_b", (128, 512), F16)
        mscr = sbuf("mscr", (128, 512), F32)
        ascr = mch_a
        bscr = mch_b
        qscr = sbuf("qscr", (128, 512), F32)
        ob = [mscr, qscr]
        cm8 = sbuf("cm8", (128, 8 * nIG), F32)
        fm8 = sbuf("fm8", (128, 8), F32)
        bv9 = sbuf("bv9", (128, 9), F32)
        oh9 = sbuf("oh9", (128, 9), F32)
        st_lo = sbuf("st_lo", (128, SUBS), F32)
        st_hi = sbuf("st_hi", (128, SUBS), F32)
        st_mid = sbuf("st_mid", (128, SUBS), F32)
        st_nmid = sbuf("st_nmid", (128, SUBS), F32)
        st_cp = sbuf("st_cp", (128, SUBS), F32)
        st_cn = sbuf("st_cn", (128, SUBS), F32)
        st_cnt = sbuf("st_cnt", (128, SUBS), F32)
        st_msk = sbuf("st_msk", (128, SUBS), mybir.dt.uint8)
        st_mski = sbuf("st_mski", (128, SUBS), mybir.dt.uint8)
        st_flo = sbuf("st_flo", (128, SUBS), F32)
        st_th = sbuf("st_th", (128, SUBS), F32)
        st_nth = sbuf("st_nth", (128, SUBS), F32)
        st_tmp = sbuf("st_tmp", (128, SUBS), F32)
        st_d = sbuf("st_d", (128, SUBS), F32)
        st_hi0 = sbuf("st_hi0", (128, SUBS), F32)
        st_fhi = sbuf("st_fhi", (128, SUBS), F32)
        st_w = sbuf("st_w", (128, SUBS), F32)
        st_num = sbuf("st_num", (128, SUBS), F32)
        st_den = sbuf("st_den", (128, SUBS), F32)
        st_trA = sbuf("st_trA", (128, SUBS), F32)
        st_trB = sbuf("st_trB", (128, SUBS), F32)
        st_trC = sbuf("st_trC", (128, SUBS), F32)
        st_tr1 = sbuf("st_tr1", (128, SUBS), F32)
        st_tr2 = sbuf("st_tr2", (128, SUBS), F32)
        st_floi = sbuf("st_floi", (128, SUBS), F32)
        st_fhii = sbuf("st_fhii", (128, SUBS), F32)
        st_pmsk = sbuf("st_pmsk", (128, SUBS), F32)
        st_pmski = sbuf("st_pmski", (128, SUBS), F32)
        st_msk2 = sbuf("st_msk2", (128, SUBS), mybir.dt.uint8)
        rmscr = sbuf("rmscr", (128, SUBS * nIG), F32)
        snkV = sbuf("snkV", (128, 1), F16)
        snkS = sbuf("snkS", (128, 1), F16)
        ln_sm = sbuf("ln_sm", (128, nh), F32)
        ln_sq = sbuf("ln_sq", (128, nh), F32)
        ln_mean = sbuf("ln_mean", (128, 1), F32)
        ln_var = sbuf("ln_var", (128, 1), F32)
        ln_rstd = sbuf("ln_rstd", (128, SUBS), F32)
        ln_nb = sbuf("ln_nb", (128, SUBS), F32)
        lg = sbuf("lg", (128, SUBS * E), F32)
        lgs = sbuf("lgs", (128, SUBS * E), F32)
        zc = sbuf("zc", (128, SUBS * E), F32)
        zc2 = sbuf("zc2", (128, SUBS * E), F32)
        sup = sbuf("sup", (128, SUBS * E), F32)
        supn = sbuf("supn", (128, SUBS * E), F32)
        kz = sbuf("kz", (128, SUBS), F32)
        kzr = sbuf("kzr", (128, SUBS), F32)
        tau = sbuf("tau", (128, SUBS), F32)
        wrt = sbuf("wrt", (128, SUBS * E), F32)
        srt_t = sbuf("srt_t", (128, SUBS * 4), F32)
        h_sb = sbuf("h_sb", (128, SUBS * ER), F16)
        hw_scr = sbuf("hw_scr", (128, ER), F32)
        hwT = sbuf("hwT", (128, SUBS * ER), F16)
        identF = sbuf("identF", (128, 128), F32)
        iota9_s = sbuf("iota9_s", (128, 9), F32)
        iotaE_s = sbuf("iotaE_s", (128, E * SUBS), F32)
        ones_sb = sbuf("ones_sb", (1, 128), F16)
        brow_a = sbuf("brow_a", (1, F0W), F16)
        brow_b = sbuf("brow_b", (1, H), F16)
        xq = sbuf("xq", (128, SUBS * 512), F32)
        t1scr = xq
        eps_ap = sbuf("eps_ap", (128, 1), F32)
        nhalf_ap = sbuf("nhalf_ap", (128, 1), F32)

        PB = [ctx.enter_context(nc.psum_tensor(f"PB{i}", [128, 512], F32))
              for i in range(8)]

        ALLS = [("s", s_, i_) for s_ in range(SUBS) for i_ in range(nIG)]

        def _tr_init(av, bv, cv):
            for c_, v_ in enumerate(av):
                em.op("V", lambda e, c_=c_, v_=v_: e.memset(
                    st_trA[:, c_:c_ + 1], v_), writes=[("trA",)])
            for c_, v_ in enumerate(bv):
                em.op("V", lambda e, c_=c_, v_=v_: e.memset(
                    st_trB[:, c_:c_ + 1], v_), writes=[("trB",)])
            for c_, v_ in enumerate(cv):
                em.op("V", lambda e, c_=c_, v_=v_: e.memset(
                    st_trC[:, c_:c_ + 1], float(v_)), writes=[("trC",)])
        bst = dict(lo=st_lo, hi=st_hi, mid=st_mid, nmid=st_nmid, cp=st_cp,
                  cn=st_cn, cnt=st_cnt, msk=st_msk, mski=st_mski,
                  flo=st_flo, fhi=st_fhi, hi0=st_hi0, w=st_w, num=st_num,
                  den=st_den, trA=st_trA, trB=st_trB, trC=st_trC,
                  tr1=st_tr1, tr2=st_tr2, floi=st_floi, fhii=st_fhii,
                  pmsk=st_pmsk, pmski=st_pmski, msk2=st_msk2,
                  th=st_th, _tr_init=_tr_init)

        em.dma("CX", lambda e: e.dma_start(out=identF[:], in_=ident_h))
        em.dma("CX", lambda e: e.dma_start(out=iota9_s[:], in_=iota9_h))
        em.dma("CX", lambda e: e.dma_start(out=iotaE_s[:], in_=iotaE_h))
        em.dma("CG", lambda e: e.dma_start(out=ones_sb[:], in_=ones_h))
        em.dma("CG", lambda e: e.dma_start(out=brow_a[:], in_=browa_h))
        em.dma("CG", lambda e: e.dma_start(out=brow_b[:], in_=browb_h))
        em.dma("CG", lambda e: e.dma_start(out=bflat[:], in_=bflat_h))
        for reg in [("identF",), ("iota9",), ("iotaE",)]:
            em.force_event(reg, ("CX", em.cnt["CX"]))
        for reg in [("ones",), ("browa",), ("browb",), ("bflat",)]:
            em.force_event(reg, ("CG", em.cnt["CG"]))

        em.op("V", lambda e: e.memset(eps_ap[:], 1e-5), writes=[("epsap",)])
        em.op("V", lambda e: e.memset(nhalf_ap[:], -0.5), writes=[("nhalfap",)])

        wslot = {"k": 0}

        def stream_tile(src_1d, off, nelem, d1, d2):
            k = wslot["k"] % 4
            wslot["k"] += 1
            dst = ar[k][:, 0:d1 * d2].rearrange("p (a b) -> p a b", a=d1)
            src = src_1d[off:off + nelem].rearrange("(p a b) -> p a b",
                                                    p=128, a=d1)
            em.dma("AR%d" % k, lambda e, d=dst, s=src: e.dma_start(out=d, in_=s),
                   writes=[("AR", k)])
            return k

        def arslice(k, d1, d2, j):
            return ar[k][:, 0:d1 * d2].rearrange("p (a b) -> p a b", a=d1)[:, j, :]

        def _ln_sub_emitters(grp2, bank_base=6, bufs=None):
            tok02 = grp2 * cfg.TG
            if bufs is None:
                bufs = [(xz, (("XZ",),), "XZ"), (xz, (("XZ",),), "XZ")]
            fns = []
            for sub in range(SUBS):
                def fn_a(sub=sub, **kw):
                    r0 = tok02 + sub * 128
                    for hi_, (h0, hw_) in enumerate(cfg.LNH):
                        buf, bkeys, bsem = bufs[hi_ % len(bufs)]
                        em.dma(bsem, lambda e, h0=h0, hw_=hw_, r0=r0, buf=buf:
                               e.dma_start(out=buf[:, 0:hw_],
                                           in_=x_h[r0:r0 + 128, h0:h0 + hw_]),
                               writes=list(bkeys))
                        em.op("V", lambda e, hw_=hw_, c=hi_, buf=buf:
                              e.reduce_sum(ln_sm[:, c:c + 1], buf[:, 0:hw_], AX),
                              reads=list(bkeys), writes=[("lnsm", hi_)])
                        em.op("S", lambda e, hw_=hw_, c=hi_, buf=buf:
                              e.activation(snkS[:, 0:1].broadcast_to([128, hw_]),
                                           buf[:, 0:hw_], AF.Square,
                                           accum_out=ln_sq[:, c:c + 1]),
                              reads=list(bkeys), writes=[("lnsq", hi_), ("snkS",)])
                    em.op("V", lambda e: e.reduce_sum(ln_mean[:, 0:1],
                                                      ln_sm[:, 0:nh], AX),
                          reads=[("lnsm", i_) for i_ in range(nh)],
                          writes=[("lnmean",)])
                    em.op("V", lambda e: e.tensor_scalar(
                        ln_mean[:, 0:1], ln_mean[:, 0:1], 1.0 / H, None, ALU.mult),
                        reads=[("lnmean",)], writes=[("lnmean",)])
                    em.op("V", lambda e: e.reduce_sum(ln_var[:, 0:1],
                                                      ln_sq[:, 0:nh], AX),
                          reads=[("lnsq", i_) for i_ in range(nh)],
                          writes=[("lnvar",)])
                    em.op("V", lambda e: e.tensor_scalar(
                        ln_var[:, 0:1], ln_var[:, 0:1], 1.0 / H, None, ALU.mult),
                        reads=[("lnvar",)], writes=[("lnvar",)])
                    em.op("V", lambda e: e.tensor_tensor(
                        st_tmp[:, 0:1], ln_mean[:, 0:1], ln_mean[:, 0:1], ALU.mult),
                        reads=[("lnmean",)], writes=[("sttmp",)])
                    em.op("V", lambda e: e.tensor_tensor(
                        ln_var[:, 0:1], ln_var[:, 0:1], st_tmp[:, 0:1], ALU.subtract),
                        reads=[("lnvar",), ("sttmp",)], writes=[("lnvar",)])
                    em.op("S", lambda e: e.activation(
                        ln_rstd[:, 0:1], ln_var[:, 0:1], AF.Ln, bias=eps_ap[:, 0:1]),
                        reads=[("lnvar",), ("epsap",)], writes=[("lnr0",)])
                    em.op("S", lambda e, sub=sub: e.activation(
                        ln_rstd[:, sub:sub + 1], ln_rstd[:, 0:1], AF.Exp,
                        scale=nhalf_ap[:, 0:1]),
                        reads=[("lnr0",), ("nhalfap",)],
                        writes=[("lnrstd", sub), ("lnr0",)])
                    em.op("V", lambda e, sub=sub: e.tensor_tensor(
                        ln_nb[:, sub:sub + 1], ln_mean[:, 0:1],
                        ln_rstd[:, sub:sub + 1], ALU.mult),
                        reads=[("lnmean",), ("lnrstd", sub)], writes=[("lnnb", sub)])
                    em.op("V", lambda e, sub=sub: e.tensor_scalar(
                        ln_nb[:, sub:sub + 1], ln_nb[:, sub:sub + 1], -1.0, None,
                        ALU.mult),
                        reads=[("lnnb", sub)], writes=[("lnnb", sub)])
                def fn_b(sub=sub, bank_base=bank_base, **kw):
                    bank_base = kw.get("bank_base", bank_base)
                    r0 = tok02 + sub * 128
                    tp_par = 0
                    for hi_, (h0, hw_) in enumerate(cfg.LNH):
                        buf, bkeys, bsem = bufs[hi_ % len(bufs)]
                        em.dma(bsem, lambda e, h0=h0, hw_=hw_, r0=r0, buf=buf:
                               e.dma_start(out=buf[:, 0:hw_],
                                           in_=x_h[r0:r0 + 128, h0:h0 + hw_]),
                               writes=list(bkeys))
                        em.op("V", lambda e, hw_=hw_, sub=sub, buf=buf:
                              e.tensor_scalar(
                            buf[:, 0:hw_], buf[:, 0:hw_], ln_rstd[:, sub:sub + 1],
                            ln_nb[:, sub:sub + 1], ALU.mult, op1=ALU.add),
                            reads=list(bkeys) + [("lnnb", sub), ("lnrstd", sub)],
                            writes=list(bkeys))
                        for (b0, nb) in chunks(hw_ // 128, 4):
                            bank = bank_base + (tp_par % 2)
                            tp_par += 1
                            em.wait_only("T", reads=list(bkeys) + [("identF",)],
                                         writes=[("PB", bank)])
                            for j in range(nb):
                                em.raw_inc("T", lambda e, b=b0 + j, j=j,
                                           bank=bank, buf=buf:
                                           e.transpose(
                                               PB[bank][:, j * 128:(j + 1) * 128],
                                               buf[:, b * 128:(b + 1) * 128],
                                               identF[:]),
                                           reads=list(bkeys) + [("identF",)],
                                           writes=[("PB", bank)],
                                           inc=(1 if j == nb - 1 else 0))
                            cbase = h0 // 128 + b0
                            em.op("V", lambda e, cbase=cbase, nb=nb, sub=sub,
                                  bank=bank:
                                  e.tensor_copy(
                                      xnT[:, (cbase * SUBS + sub) * 128:]
                                      .rearrange("p (a b) -> p a b", b=128)
                                      [:, 0:(nb - 1) * SUBS + 1:SUBS, :],
                                      PB[bank][:, 0:nb * 128]
                                      .rearrange("p (a b) -> p a b", b=128)),
                                  reads=[("PB", bank)], writes=[("xnT", sub)])

                fns.append(fn_a)
                fns.append(fn_b)
            return fns

        for grp in range(GROUPS):
            tok0 = grp * cfg.TG

            if grp == 0:
                if SUBS * 512 >= 2048:
                    g0bufs = [(xq[:, 0:1024], (("xq", 0), ("xq", 1)),
                               "XZ"),
                              (xq[:, 1024:2048], (("xq", 2), ("xq", 3)),
                               "XZ2")]
                else:
                    g0bufs = None
                for f_ in _ln_sub_emitters(0, bufs=g0bufs):
                    f_()
            XNT_R = [("xnT", s_) for s_ in range(SUBS)]

            a_iters, bis_iters, tie_pick = _bisect_program(em, cfg, s_buf,
                                                           bst, snkV, snkS)

            a_pend = []
            for igi, (i0, iw) in enumerate(cfg.IGS):
                banks = [s_ + 4 * (igi % 2) for s_ in range(SUBS)]
                for cbi, (c0, nc8) in enumerate(cfg.CBS):
                    off = _off_ck(cfg, cfg.IGS, cfg.CBS, igi, cbi)
                    k = stream_tile(wgp_h, off, 128 * nc8 * iw, nc8, iw)
                    em.wait_only("T", reads=[("AR", k)] + XNT_R,
                                 writes=([("PB", b) for b in banks]
                                         if cbi == 0 else []))
                    last_cb = cbi == len(cfg.CBS) - 1
                    for c8 in range(nc8):
                        c = c0 + c8
                        for sub in range(SUBS):
                            st = (cbi == 0 and c8 == 0)
                            sp = (last_cb and c8 == nc8 - 1)
                            em.raw_inc(
                                "T",
                                lambda e, c=c, sub=sub, k=k, c8=c8, iw=iw,
                                       nc8=nc8, st=st, sp=sp, bank=banks[sub]:
                                e.matmul(PB[bank][:, 0:iw],
                                         xnT[:, (c * SUBS + sub) * 128:
                                             (c * SUBS + sub) * 128 + 128],
                                         arslice(k, nc8, iw, c8),
                                         start=st, stop=sp),
                                reads=[("AR", k)],
                                writes=([("PB", banks[sub])] if sp else []),
                                inc=(1 if (c8 == nc8 - 1 and sub == SUBS - 1)
                                     or sp else 0))
                for sub in range(SUBS):
                    bank = banks[sub]
                    em.op("S", lambda e, bank=bank, iw=iw, sub=sub, i0=i0:
                          e.activation(s_buf[:, sub * I + i0: sub * I + i0 + iw],
                                       PB[bank][:, 0:iw], AF.Silu),
                          reads=[("PB", bank)], writes=[("s", sub, igi)])
                    if igi < 2:
                        em.op("V", lambda e, sub=sub, i0=i0, iw=iw, igi=igi:
                              e.tensor_reduce(
                                  rmscr[:, sub * nIG + igi:
                                        sub * nIG + igi + 1],
                                  s_buf[:, sub * I + i0: sub * I + i0 + iw],
                                  AX, ALU.max, apply_absolute_value=True),
                              reads=[("s", sub, igi)],
                              writes=[("rm", sub, igi)])
                if igi == 1:
                    for sub in range(SUBS):
                        em.op("V", lambda e, sub=sub: e.tensor_reduce(
                            st_hi[:, sub:sub + 1],
                            rmscr[:, sub * nIG: sub * nIG + 2], AX, ALU.max),
                            reads=[("rm", sub, 0), ("rm", sub, 1)],
                            writes=[("sthi",)])
                    em.op("V", lambda e: e.tensor_copy(st_hi0[:], st_hi[:]),
                          reads=[("sthi",)], writes=[("sthi0",)])
                    em.op("V", lambda e: e.memset(st_lo[:], 0.0),
                          writes=[("stlo",)])
                    em.op("V", lambda e: e.memset(st_flo[:], BIG_F),
                          writes=[("stflo",)])
                    a_pend.extend(a_iters)
                if igi >= 2 and a_pend:
                    a_pend.pop(0)()

            while a_pend:
                a_pend.pop(0)()

            for fi, (c0, nc16) in enumerate(cfg.F0CB):
                off = _off_ck(cfg, [(0, F0W)], cfg.F0CB, 0, fi)
                k = stream_tile(wrap_h, off, 128 * nc16 * F0W, nc16, F0W)
                em.wait_only("T", reads=[("AR", k), ("browa",), ("ones",)] + XNT_R,
                             writes=([("PB", s_) for s_ in range(SUBS)]
                                     if fi == 0 else []))
                for c16 in range(nc16):
                    c = c0 + c16
                    for sub in range(SUBS):
                        st = (fi == 0 and c16 == 0)
                        em.raw_inc("T",
                                   lambda e, c=c, sub=sub, k=k, c16=c16,
                                          nc16=nc16, st=st:
                                   e.matmul(PB[sub][:, 0:F0W],
                                            xnT[:, (c * SUBS + sub) * 128:
                                                (c * SUBS + sub) * 128 + 128],
                                            arslice(k, nc16, F0W, c16),
                                            start=st, stop=False),
                                   reads=[("AR", k)], writes=[],
                                   inc=(1 if (c16 == nc16 - 1 and
                                              sub == SUBS - 1) else 0))
            for sub in range(SUBS):
                em.raw_inc("T", lambda e, sub=sub:
                           e.matmul(PB[sub][:, 0:F0W], ones_sb[:], brow_a[:],
                                    start=False, stop=True),
                           reads=[("browa",), ("ones",)],
                           writes=[("PB", sub)], inc=1)
                em.op("V", lambda e, sub=sub: e.tensor_copy(
                    lg[:, sub * E:(sub + 1) * E], PB[sub][:, 0:E]),
                    reads=[("PB", sub)], writes=[("lg", sub)])
                em.op("V", lambda e, sub=sub: e.tensor_copy(
                    h_sb[:, sub * ER:(sub + 1) * ER], PB[sub][:, E:E + ER]),
                    reads=[("PB", sub)], writes=[("hsb", sub)])

            _sparsemax(em, cfg, lg, lgs, zc, zc2, sup, supn, kz, kzr, tau, wrt,
                       srt_t, iotaE_s)

            for sub in range(SUBS):
                em.op("V", lambda e, sub=sub: e.tensor_tensor(
                    hw_scr[:, 0:ER].rearrange("p (a b) -> p a b", a=E),
                    h_sb[:, sub * ER:(sub + 1) * ER]
                    .rearrange("p (a b) -> p a b", a=E),
                    wrt[:, sub * E:(sub + 1) * E][:, :, None]
                    .broadcast_to([128, E, R]),
                    ALU.mult),
                    reads=[("hsb", sub), ("wrt",)], writes=[("hwscr",)])
                bank = 4 + (sub % 2)
                em.op("T", lambda e, bank=bank: e.transpose(
                    PB[bank][:, 0:128], hw_scr[:, 0:128], identF[:]),
                    reads=[("hwscr",), ("identF",)], writes=[("PB", bank)])
                em.op("V", lambda e, sub=sub, bank=bank: e.tensor_copy(
                    hwT[:, sub * ER:(sub + 1) * ER], PB[bank][:, 0:ER]),
                    reads=[("PB", bank)], writes=[("hwT", sub)])

            base_rounds = _base_rounds(em, cfg, stream_tile, arslice, wbp_h, PB,
                                       xnT, ones_sb, brow_b, hwT, bflat, g_sb,
                                       XNT_R)

            nb_, ni_ = len(base_rounds), len(bis_iters)
            bi = 0
            for ri, rfn in enumerate(base_rounds):
                rfn()
                if ri < 1:
                    continue
                want = int(np.ceil((ri - 1 + 1) * ni_ / max(nb_, 1)))
                while bi < min(want, ni_):
                    bis_iters[bi]()
                    bi += 1
            while bi < ni_:
                bis_iters[bi]()
                bi += 1

            if cfg.FIXUP:
                _fixup(em, cfg, s_buf, st_lo, st_hi, st_flo, st_th, st_d, ascr,
                       bscr, t1scr, qscr, cm8, fm8, bv9, oh9, iota9_s)
            else:
                em.op("V", lambda e: e.tensor_copy(st_th[:], st_hi[:]),
                      reads=[("sthi",)],
                      writes=[("stth", s_) for s_ in range(SUBS)])
            tie_pick()
            for sub in range(SUBS):
                em.op("V", lambda e, sub=sub: e.tensor_scalar(
                    st_nth[:, sub:sub + 1], st_th[:, sub:sub + 1], -1.0, None,
                    ALU.mult),
                    reads=[("stth", sub)], writes=[("stnth", sub)])

            def _ph2a_mask(igi, i0, iw):
                    for sub in range(SUBS):
                        em.op("V", lambda e, sub=sub, i0=i0, iw=iw: e.tensor_scalar(
                            mch_a[:, 0:iw],
                            s_buf[:, sub * I + i0: sub * I + i0 + iw],
                            st_th[:, sub:sub + 1], st_nth[:, sub:sub + 1],
                            ALU.min, op1=ALU.max),
                            reads=[("s", sub, igi), ("stth", sub),
                                   ("stnth", sub)],
                            writes=[("mcha",)])
                        em.op("V", lambda e, sub=sub, i0=i0, iw=iw: e.tensor_tensor(
                            mch_b[:, 0:iw],
                            s_buf[:, sub * I + i0: sub * I + i0 + iw],
                            mch_a[:, 0:iw], ALU.not_equal),
                            reads=[("s", sub, igi), ("mcha",)], writes=[("mchb",)])
                        em.op("V", lambda e, sub=sub, i0=i0, iw=iw: e.tensor_tensor(
                            mch_a[:, 0:iw], mch_b[:, 0:iw],
                            s_buf[:, sub * I + i0: sub * I + i0 + iw], ALU.mult),
                            reads=[("s", sub, igi), ("mchb",)], writes=[("mcha",)])
                        bank = sub + 4 * (igi % 2)
                        em.op("V", lambda e, sub=sub, iw=iw, bank=bank:
                              e.tensor_tensor(
                            mscr[:, 0:iw], mch_a[:, 0:iw], PB[bank][:, 0:iw],
                            ALU.mult),
                            reads=[("mcha",), ("PB", bank)], writes=[("OBK0",)])
                        nb = iw // 128
                        em.wait_only("T", reads=[("OBK0",), ("identF",)],
                                     writes=[("PB", bank)])
                        for j in range(nb):
                            em.raw_inc("T", lambda e, j=j, bank=bank:
                                       e.transpose(PB[bank][:, j * 128:(j + 1) * 128],
                                                   mscr[:, j * 128:(j + 1) * 128],
                                                   identF[:]),
                                       reads=[("OBK0",), ("identF",)],
                                       writes=[("PB", bank)],
                                       inc=(1 if j == nb - 1 else 0))
                        em.op("V", lambda e, sub=sub, i0=i0, iw=iw, bank=bank:
                              e.tensor_copy(
                                  s_buf[:, sub * I + i0: sub * I + i0 + iw],
                                  PB[bank][:, 0:iw]),
                              reads=[("PB", bank)], writes=[("s", sub, igi)])

            pend_mask = []
            for igi, (i0, iw) in enumerate(cfg.IGS):
                banks = [s_ + 4 * (igi % 2) for s_ in range(SUBS)]
                for cbi, (c0, nc8) in enumerate(cfg.CBS):
                    off = _off_ck(cfg, cfg.IGS, cfg.CBS, igi, cbi)
                    k = stream_tile(wup_h, off, 128 * nc8 * iw, nc8, iw)
                    em.wait_only("T", reads=[("AR", k)] + XNT_R,
                                 writes=([("PB", b) for b in banks]
                                         if cbi == 0 else []))
                    last_cb = cbi == len(cfg.CBS) - 1
                    for c8 in range(nc8):
                        c = c0 + c8
                        for sub in range(SUBS):
                            st = (cbi == 0 and c8 == 0)
                            sp = (last_cb and c8 == nc8 - 1)
                            em.raw_inc(
                                "T",
                                lambda e, c=c, sub=sub, k=k, c8=c8, iw=iw,
                                       nc8=nc8, st=st, sp=sp, bank=banks[sub]:
                                e.matmul(PB[bank][:, 0:iw],
                                         xnT[:, (c * SUBS + sub) * 128:
                                             (c * SUBS + sub) * 128 + 128],
                                         arslice(k, nc8, iw, c8),
                                         start=st, stop=sp),
                                reads=[("AR", k)],
                                writes=([("PB", banks[sub])] if sp else []),
                                inc=(1 if (c8 == nc8 - 1 and sub == SUBS - 1)
                                     or sp else 0))
                pend_mask.append((igi, i0, iw))
                if len(pend_mask) > 1:
                    _ph2a_mask(*pend_mask.pop(0))
            while pend_mask:
                _ph2a_mask(*pend_mask.pop(0))

            next_ln = _ln_sub_emitters(grp + 1) if grp + 1 < GROUPS else []
            nbb = len(cfg.BBS)
            for qi, (h0, hw_) in enumerate(cfg.HCH):
                banks = [s_ + 4 * (qi % 2) for s_ in range(SUBS)]
                for bsi, (b0, nb) in enumerate(cfg.BBS):
                    off = 0
                    for q2, (hh0, hhw) in enumerate(cfg.HCH):
                        for b2, (bb0, bnb) in enumerate(cfg.BBS):
                            if q2 == qi and b2 == bsi:
                                break
                            off += 128 * bnb * hhw
                        else:
                            continue
                        break
                    k = stream_tile(wdp_h, off, 128 * nb * hw_, nb, hw_)
                    em.wait_only("T", reads=[("AR", k)] +
                                 ([("s", s_, i_) for s_ in range(SUBS)
                                   for i_ in range(nIG)] if bsi == 0 else []),
                                 writes=([("PB", b) for b in banks]
                                         if bsi == 0 else []))
                    for jb in range(nb):
                        blk = b0 + jb
                        for sub in range(SUBS):
                            st = (bsi == 0 and jb == 0)
                            sp = (bsi == nbb - 1 and jb == nb - 1)
                            rds = [("AR", k)]
                            if sp:
                                rds = rds + [("s", sub, i_) for i_ in range(nIG)]
                            em.raw_inc(
                                "T",
                                lambda e, blk=blk, sub=sub, k=k, jb=jb, hw_=hw_,
                                       nb=nb, st=st, sp=sp, bank=banks[sub]:
                                e.matmul(PB[bank][:, 0:hw_],
                                         s_buf[:, sub * I + blk * 128:
                                               sub * I + blk * 128 + 128],
                                         arslice(k, nb, hw_, jb),
                                         start=st, stop=sp),
                                reads=rds,
                                writes=([("PB", banks[sub])] if sp else []),
                                inc=(1 if (jb == nb - 1 and sub == SUBS - 1)
                                     or sp else 0))
                em.dma("XQ", lambda e, h0=h0, hw_=hw_, t0=tok0:
                       e.dma_start(
                           out=xq[:, 0:SUBS * 512]
                           .rearrange("p (a b) -> p a b", a=SUBS)[:, :, 0:hw_],
                           in_=x_h[t0:t0 + SUBS * 128, h0:h0 + hw_]
                           .rearrange("(s p) h -> p s h", s=SUBS)),
                       writes=[("xq", s_) for s_ in range(SUBS)])
                for sub in range(SUBS):
                    bank = banks[sub]
                    r0 = tok0 + sub * 128
                    o = (qi * SUBS + sub) % 2
                    em.op("V", lambda e, sub=sub, bank=bank, h0=h0, hw_=hw_, o=o:
                          e.tensor_tensor(ob[o][:, 0:hw_], PB[bank][:, 0:hw_],
                                          g_sb[:, sub * H + h0:
                                               sub * H + h0 + hw_],
                                          ALU.mult),
                          reads=[("PB", bank), ("g", sub, qi)],
                          writes=[("OBK%d" % o,)])
                    em.op("V", lambda e, sub=sub, hw_=hw_, o=o:
                          e.tensor_tensor(ob[o][:, 0:hw_], ob[o][:, 0:hw_],
                                          xq[:, sub * 512: sub * 512 + hw_],
                                          ALU.add),
                          reads=[("OBK%d" % o,), ("xq", sub)],
                          writes=[("OBK%d" % o,)])
                    em.dma("OB%d" % o, lambda e, r0=r0, h0=h0, hw_=hw_, o=o:
                           e.dma_start(out=out_h[r0:r0 + 128, h0:h0 + hw_],
                                       in_=ob[o][:, 0:hw_]),
                           reads=[("OBK%d" % o,)])
                if grp + 1 < GROUPS and next_ln:
                    next_ln.pop(0)(bank_base=4 * (qi % 2))
            while grp + 1 < GROUPS and next_ln:
                next_ln.pop(0)(bank_base=6)

        em.prog["X"].append(("w", "OB0", em.cnt["OB0"]))
        em.prog["X"].append(("w", "OB1", em.cnt["OB1"]))

        sem_names = ["T", "V", "S"] + list(Em.DSEMS)
        with ExitStack() as sctx:
            sems = {n: sctx.enter_context(nc.semaphore("sem" + n))
                    for n in sem_names}
            block = sctx.enter_context(nc.Block())

            @block.tensor
            def _(e):
                em.replay("T", e, sems)

            @block.vector
            def _(e):
                em.replay("V", e, sems)

            @block.scalar
            def _(e):
                em.replay("S", e, sems)

            @block.gpsimd
            def _(e):
                em.replay("G", e, sems)

            @block.sync
            def _(e):
                em.replay("X", e, sems)

    return nc


def _base_rounds(em, cfg, stream_tile, arslice, wbp_h, PB, xnT, ones_sb,
                 brow_b, hwT, bflat, g_sb, XNT_R):
    SUBS, H, ER = cfg.SUBS, cfg.H, cfg.ER
    rounds = []
    for qi, (h0, hw_) in enumerate(cfg.HCH):
        def round_fn(qi=qi, h0=h0, hw_=hw_):
            banks = [s_ + 4 * (qi % 2) for s_ in range(SUBS)]
            for cbi, (c0, nc8) in enumerate(cfg.CBS):
                off = _off_ck(cfg, cfg.HCH, cfg.CBS, qi, cbi)
                k = stream_tile(wbp_h, off, 128 * nc8 * hw_, nc8, hw_)
                em.wait_only("T", reads=[("AR", k)] + XNT_R,
                             writes=([("PB", b) for b in banks]
                                     if cbi == 0 else []))
                for c8 in range(nc8):
                    c = c0 + c8
                    for sub in range(SUBS):
                        st = (cbi == 0 and c8 == 0)
                        em.raw_inc(
                            "T",
                            lambda e, c=c, sub=sub, k=k, c8=c8, nc8=nc8,
                                   hw_=hw_, st=st, bank=banks[sub]:
                            e.matmul(PB[bank][:, 0:hw_],
                                     xnT[:, (c * SUBS + sub) * 128:
                                         (c * SUBS + sub) * 128 + 128],
                                     arslice(k, nc8, hw_, c8),
                                     start=st, stop=False),
                            reads=[("AR", k)], writes=[],
                            inc=(1 if (c8 == nc8 - 1 and sub == SUBS - 1)
                                 else 0))
            em.wait_only("T", reads=[("browb",), ("ones",), ("bflat",)] +
                         [("hwT", s_) for s_ in range(SUBS)])
            for sub in range(SUBS):
                bank = banks[sub]
                em.raw_inc("T", lambda e, bank=bank, h0=h0, hw_=hw_:
                           e.matmul(PB[bank][:, 0:hw_], ones_sb[:],
                                    brow_b[:, h0:h0 + hw_],
                                    start=False, stop=False),
                           reads=[("browb",), ("ones",)], writes=[], inc=0)
                em.raw_inc("T", lambda e, bank=bank, sub=sub, h0=h0, hw_=hw_:
                           e.matmul(PB[bank][:, 0:hw_],
                                    hwT[:, sub * ER:(sub + 1) * ER],
                                    bflat[:, h0:h0 + hw_],
                                    start=False, stop=True),
                           reads=[("hwT", sub), ("bflat",)],
                           writes=[("PB", bank)], inc=1)
                em.op("S", lambda e, bank=bank, sub=sub, h0=h0, hw_=hw_:
                      e.activation(g_sb[:, sub * H + h0: sub * H + h0 + hw_],
                                   PB[bank][:, 0:hw_], AF.Silu),
                      reads=[("PB", bank)], writes=[("g", sub, qi)])
        rounds.append(round_fn)
    return rounds


def _bisect_program(em, cfg, s_buf, st, snkV, snkS):
    SUBS, I, K = cfg.SUBS, cfg.I, cfg.K
    nIG = len(cfg.IGS)
    NSUB = min(1024, I)
    KA = K * NSUB // I
    if SUBS == 4:
        v_passes = [(0, 0), (0, 1), (1, 0)]
        s_passes = [(1, 1), (2, 0), (2, 1), (3, 0), (3, 1)]
    else:
        v_passes = [(s_, sd) for s_ in range((SUBS + 1) // 2) for sd in (0, 1)]
        s_passes = [(s_, sd) for s_ in range((SUBS + 1) // 2, SUBS)
                    for sd in (0, 1)]
    vset = set(v_passes)
    trA_vals = [1.0 if (s_, 0) in vset else 0.5 for s_ in range(SUBS)]
    trB_vals = [1.0 if (s_, 1) in vset else 0.5 for s_ in range(SUBS)]
    trC_vals = [(0 if (s_, 0) in vset else 1) + (0 if (s_, 1) in vset else 1)
                for s_ in range(SUBS)]
    st["_tr_init"](trA_vals, trB_vals, trC_vals)
    trA, trB, trC = st["trA"], st["trB"], st["trC"]
    st_tr1, st_tr2 = st["tr1"], st["tr2"]
    A_ITERS = 12
    B_ITERS = cfg.B_ITERS
    lo, hi, mid, nmid = st["lo"], st["hi"], st["mid"], st["nmid"]
    cp, cn, cnt, msk, mski = st["cp"], st["cn"], st["cnt"], st["msk"], st["mski"]
    flo, fhi, hi0, w_, num, den = (st["flo"], st["fhi"], st["hi0"], st["w"],
                                   st["num"], st["den"])
    floi, fhii, pmsk = st["floi"], st["fhii"], st["pmsk"]

    nred_of = lambda width: min((width + 511) // 512, nIG)

    def count_round(width, kthr, track_f, upd=True):
        em.op("V", lambda e: e.tensor_scalar(nmid[:], mid[:], -1.0, None,
                                             ALU.mult),
              reads=[("stmid",)], writes=[("stnmid",)])
        for (sub, side) in v_passes:
            srd = [("s", sub, i_) for i_ in range(nred_of(width))]
            dst = cp if side == 0 else cn
            thr = mid if side == 0 else nmid
            opx = ALU.is_gt if side == 0 else ALU.is_lt
            tk = ("stmid",) if side == 0 else ("stnmid",)
            ck = ("stcp", sub) if side == 0 else ("stcn", sub)
            em.op("V", lambda e, sub=sub, width=width, dst=dst, thr=thr,
                  opx=opx: e.tensor_scalar(
                snkV[:, 0:1].broadcast_to([128, width]),
                s_buf[:, sub * I: sub * I + width],
                thr[:, sub:sub + 1], None, opx, op1=ALU.add,
                accum_out=dst[:, sub:sub + 1]),
                reads=[tk] + srd, writes=[("snkV",), ck])
        for (sub, side) in s_passes:
            srd = [("s", sub, i_) for i_ in range(nred_of(width))]
            dst = cp if side == 0 else cn
            sc = 1.0 if side == 0 else -1.0
            ck = ("stcp", sub) if side == 0 else ("stcn", sub)
            em.op("S", lambda e, sub=sub, width=width, dst=dst, sc=sc:
                  e.activation(
                snkS[:, 0:1].broadcast_to([128, width]),
                s_buf[:, sub * I: sub * I + width], AF.Sign,
                bias=nmid[:, sub:sub + 1], scale=sc,
                accum_out=dst[:, sub:sub + 1]),
                reads=[("stnmid",)] + srd, writes=[("snkS",), ck])
        em.op("V", lambda e: e.tensor_tensor(st_tr1[:], cp[:], trA[:],
                                             ALU.mult),
              reads=[("stcp", s_) for s_ in range(SUBS)] + [("trA",)],
              writes=[("sttr1",)])
        em.op("V", lambda e: e.tensor_tensor(st_tr2[:], cn[:], trB[:],
                                             ALU.mult),
              reads=[("stcn", s_) for s_ in range(SUBS)] + [("trB",)],
              writes=[("sttr2",)])
        em.op("V", lambda e: e.tensor_tensor(cnt[:], st_tr1[:], st_tr2[:],
                                             ALU.add),
              reads=[("sttr1",), ("sttr2",)], writes=[("stcnt",)])
        em.op("V", lambda e, width=width: e.scalar_tensor_tensor(
            cnt[:], trC[:], 0.5 * width, cnt[:], ALU.mult, ALU.add),
            reads=[("stcnt",), ("trC",)], writes=[("stcnt",)])
        if not upd:
            return
        em.op("V", lambda e, kthr=kthr: e.tensor_scalar(
            msk[:], cnt[:], kthr, None, ALU.is_gt),
            reads=[("stcnt",)], writes=[("stmsk",)])
        em.op("V", lambda e, kthr=kthr: e.tensor_scalar(
            mski[:], cnt[:], kthr, None, ALU.is_le),
            reads=[("stcnt",)], writes=[("stmski",)])
        em.op("V", lambda e: e.select(lo[:], msk[:], mid[:], lo[:],
                                      add_drain=True),
              reads=[("stmsk",), ("stmid",), ("stlo",)], writes=[("stlo",)])
        em.op("V", lambda e: e.select(hi[:], mski[:], mid[:], hi[:],
                                      add_drain=True),
              reads=[("stmski",), ("stmid",), ("sthi",)], writes=[("sthi",)])
        if track_f:
            em.op("V", lambda e: e.select(flo[:], msk[:], cnt[:], flo[:],
                                          add_drain=True),
                  reads=[("stmsk",), ("stcnt",), ("stflo",)],
                  writes=[("stflo",)])
            em.op("V", lambda e: e.select(fhi[:], mski[:], cnt[:], fhi[:],
                                          add_drain=True),
                  reads=[("stmski",), ("stcnt",), ("stfhi",)],
                  writes=[("stfhi",)])
            em.op("V", lambda e: e.select(floi[:], msk[:], cnt[:], floi[:],
                                          add_drain=True),
                  reads=[("stmsk",), ("stcnt",), ("stfloi",)],
                  writes=[("stfloi",)])
            em.op("V", lambda e: e.select(fhii[:], mski[:], cnt[:], fhii[:],
                                          add_drain=True),
                  reads=[("stmski",), ("stcnt",), ("stfhii",)],
                  writes=[("stfhii",)])
            em.op("V", lambda e: e.tensor_tensor(st["tr1"][:], msk[:], pmsk[:],
                                                 ALU.mult),
                  reads=[("stmsk",), ("stpmsk",)], writes=[("sttr1",)])
            em.op("V", lambda e: e.tensor_scalar(
                st["msk2"][:], st["tr1"][:], 0.5, None, ALU.is_gt),
                reads=[("sttr1",)], writes=[("stmsk2",)])
            em.op("V", lambda e, K2=None: e.tensor_scalar(
                st["tr2"][:], fhii[:], 0.5, 0.5 * (cfg.K + 0.5), ALU.mult,
                op1=ALU.add),
                reads=[("stfhii",)], writes=[("sttr2",)])
            em.op("V", lambda e: e.select(fhii[:], st["msk2"][:], st["tr2"][:],
                                          fhii[:], add_drain=True),
                  reads=[("stmsk2",), ("sttr2",), ("stfhii",)],
                  writes=[("stfhii",)])
            em.op("V", lambda e: e.tensor_tensor(
                st["tr1"][:], mski[:], st["pmski"][:], ALU.mult),
                reads=[("stmski",), ("stpmski",)], writes=[("sttr1",)])
            em.op("V", lambda e: e.tensor_scalar(
                st["msk2"][:], st["tr1"][:], 0.5, None, ALU.is_gt),
                reads=[("sttr1",)], writes=[("stmsk2",)])
            em.op("V", lambda e: e.tensor_scalar(
                st["tr2"][:], floi[:], 0.5, 0.5 * (cfg.K + 0.5), ALU.mult,
                op1=ALU.add),
                reads=[("stfloi",)], writes=[("sttr2",)])
            em.op("V", lambda e: e.select(floi[:], st["msk2"][:], st["tr2"][:],
                                          floi[:], add_drain=True),
                  reads=[("stmsk2",), ("sttr2",), ("stfloi",)],
                  writes=[("stfloi",)])
            em.op("V", lambda e: e.tensor_copy(pmsk[:], msk[:]),
                  reads=[("stmsk",)], writes=[("stpmsk",)])
            em.op("V", lambda e: e.tensor_copy(st["pmski"][:], mski[:]),
                  reads=[("stmski",)], writes=[("stpmski",)])

    a_list = []
    iters = []
    for it in range(A_ITERS):
        def a_iter(it=it):
            em.op("V", lambda e: e.tensor_tensor(mid[:], lo[:], hi[:], ALU.add),
                  reads=[("stlo",), ("sthi",)], writes=[("stmid",)])
            em.op("V", lambda e: e.tensor_scalar(mid[:], mid[:], 0.5, None,
                                                 ALU.mult),
                  reads=[("stmid",)], writes=[("stmid",)])
            count_round(NSUB, KA + 0.5, False)
        a_list.append(a_iter)

    def b_init():
        em.op("V", lambda e: e.tensor_scalar(
            den[:], hi0[:], 0.014, None, ALU.mult),
            reads=[("sthi0",)], writes=[("stden",)])
        em.op("V", lambda e: e.tensor_tensor(lo[:], lo[:], den[:],
                                             ALU.subtract),
              reads=[("stlo",), ("stden",)], writes=[("stlo",)])
        em.op("V", lambda e: e.tensor_scalar(lo[:], lo[:], 0.0, None, ALU.max),
              reads=[("stlo",)], writes=[("stlo",)])
        em.op("V", lambda e: e.tensor_tensor(hi[:], hi[:], den[:], ALU.add),
              reads=[("sthi",), ("stden",)], writes=[("sthi",)])
        em.op("V", lambda e, v=float(cfg.K + 1300): e.memset(flo[:], v),
              writes=[("stflo",)])
        em.op("V", lambda e, v=float(cfg.K - 1300): e.memset(fhi[:], v),
              writes=[("stfhi",)])
        em.op("V", lambda e, v=float(cfg.K + 1300): e.memset(floi[:], v),
              writes=[("stfloi",)])
        em.op("V", lambda e, v=float(cfg.K - 1300): e.memset(fhii[:], v),
              writes=[("stfhii",)])
        em.op("V", lambda e: e.memset(pmsk[:], 0.0), writes=[("stpmsk",)])
        em.op("V", lambda e: e.memset(st["pmski"][:], 0.0),
              writes=[("stpmski",)])
    iters.append(b_init)

    for it in range(B_ITERS):
        def b_iter(it=it):
            if it == 0:
                em.op("V", lambda e: e.tensor_tensor(mid[:], lo[:], hi[:],
                                                     ALU.add),
                      reads=[("stlo",), ("sthi",)], writes=[("stmid",)])
                em.op("V", lambda e: e.tensor_scalar(mid[:], mid[:], 0.5, None,
                                                     ALU.mult),
                      reads=[("stmid",)], writes=[("stmid",)])
            else:
                em.op("V", lambda e: e.tensor_tensor(w_[:], hi[:], lo[:],
                                                     ALU.subtract),
                      reads=[("sthi",), ("stlo",)], writes=[("stw",)])
                em.op("V", lambda e: e.tensor_scalar(
                    num[:], floi[:], -(cfg.K + 0.5), None, ALU.add),
                    reads=[("stfloi",)], writes=[("stnum",)])
                em.op("V", lambda e: e.tensor_tensor(den[:], floi[:], fhii[:],
                                                     ALU.subtract),
                      reads=[("stfloi",), ("stfhii",)], writes=[("stden",)])
                em.op("V", lambda e: e.tensor_scalar(den[:], den[:], 1.0, None,
                                                     ALU.max),
                      reads=[("stden",)], writes=[("stden",)])
                em.op("V", lambda e: e.reciprocal(den[:], den[:]),
                      reads=[("stden",)], writes=[("stden",)])
                em.op("V", lambda e: e.tensor_tensor(num[:], num[:], den[:],
                                                     ALU.mult),
                      reads=[("stnum",), ("stden",)], writes=[("stnum",)])
                em.op("V", lambda e: e.tensor_scalar(
                    num[:], num[:], 1.0 / 64, 1.0 - 1.0 / 64, ALU.max,
                    op1=ALU.min),
                    reads=[("stnum",)], writes=[("stnum",)])
                em.op("V", lambda e: e.tensor_tensor(num[:], num[:], w_[:],
                                                     ALU.mult),
                      reads=[("stnum",), ("stw",)], writes=[("stnum",)])
                em.op("V", lambda e: e.tensor_tensor(mid[:], lo[:], num[:],
                                                     ALU.add),
                      reads=[("stlo",), ("stnum",)], writes=[("stmid",)])
            count_round(I, cfg.K + 0.5, True)
        iters.append(b_iter)

    def tie_pick():
        em.op("V", lambda e: e.tensor_copy(mid[:], st["th"][:]),
              reads=[("stth", s_) for s_ in range(SUBS)], writes=[("stmid",)])
        count_round(I, cfg.K + 0.5, False, upd=False)
        em.op("V", lambda e: e.tensor_scalar(
            num[:], cnt[:], float(-cfg.K), None, ALU.add),
            reads=[("stcnt",)], writes=[("stnum",)])
        em.op("V", lambda e: e.scalar_tensor_tensor(
            den[:], num[:], -1.0, num[:], ALU.mult, ALU.max),
            reads=[("stnum",)], writes=[("stden",)])
        em.op("V", lambda e: e.tensor_scalar(
            w_[:], st["flo"][:], float(-cfg.K), None, ALU.add),
            reads=[("stflo",)], writes=[("stw",)])
        em.op("V", lambda e: e.tensor_tensor(
            st["msk2"][:], den[:], w_[:], ALU.is_le),
            reads=[("stden",), ("stw",)], writes=[("stmsk2",)])
        em.op("V", lambda e: e.select(st["th"][:], st["msk2"][:], mid[:],
                                      lo[:], add_drain=True),
              reads=[("stmsk2",), ("stmid",), ("stlo",)],
              writes=[("stth", s_) for s_ in range(SUBS)])
    return a_list, iters, tie_pick


def _fixup(em, cfg, s_buf, st_lo, st_hi, st_flo, st_th, st_d, ascr, bscr,
           t1scr, qscr, cm8, fm8, bv9, oh9, iota9_s):
    SUBS, I, K = cfg.SUBS, cfg.I, cfg.K
    nIG = len(cfg.IGS)
    for sub in range(SUBS):
        for igi, (i0, iw) in enumerate(cfg.IGS):
            em.op("S", lambda e, sub=sub, i0=i0, iw=iw: e.activation(
                ascr[:, 0:iw], s_buf[:, sub * I + i0: sub * I + i0 + iw],
                AF.Abs),
                reads=[("s", sub, igi)], writes=[("mcha",)])
            em.op("V", lambda e, sub=sub, iw=iw: e.tensor_scalar(
                bscr[:, 0:iw], ascr[:, 0:iw], st_lo[:, sub:sub + 1], 0.5,
                ALU.is_le, op1=ALU.subtract),
                reads=[("mcha",), ("stlo",)], writes=[("mchb",)])
            em.op("V", lambda e, sub=sub, iw=iw: e.scalar_tensor_tensor(
                qscr[:, 0:iw], ascr[:, 0:iw], st_hi[:, sub:sub + 1],
                bscr[:, 0:iw], ALU.subtract, ALU.mult),
                reads=[("mcha",), ("mchb",), ("sthi",)], writes=[("OBK1",)])
            em.op("V", lambda e, igi=igi, iw=iw: e.max(
                out=cm8[:, igi * 8:(igi + 1) * 8], in_=qscr[:, 0:iw]),
                reads=[("OBK1",)], writes=[("cm8", igi)])
        em.op("V", lambda e: e.max(out=fm8[:], in_=cm8[:]),
              reads=[("cm8", i_) for i_ in range(nIG)], writes=[("fm8",)])
        em.op("V", lambda e, sub=sub: e.tensor_copy(bv9[:, 0:1],
                                                    st_lo[:, sub:sub + 1]),
              reads=[("stlo",)], writes=[("bv9",)])
        em.op("V", lambda e, sub=sub: e.tensor_scalar(
            bv9[:, 1:9], fm8[:], -2.0, st_hi[:, sub:sub + 1], ALU.mult,
            op1=ALU.add),
            reads=[("fm8",), ("sthi",)], writes=[("bv9",)])
        em.op("V", lambda e, sub=sub: e.tensor_scalar(
            st_d[:, sub:sub + 1], st_flo[:, sub:sub + 1], float(-K), 8.0,
            ALU.add, op1=ALU.min),
            reads=[("stflo",)], writes=[("std", sub)])
        em.op("V", lambda e, sub=sub: e.tensor_scalar(
            st_d[:, sub:sub + 1], st_d[:, sub:sub + 1], 0.0, None, ALU.max),
            reads=[("std", sub)], writes=[("std", sub)])
        em.op("V", lambda e, sub=sub: e.scalar_tensor_tensor(
            oh9[:, 0:9], iota9_s[:, 0:9], st_d[:, sub:sub + 1], bv9[:, 0:9],
            ALU.is_equal, ALU.mult, accum_out=st_th[:, sub:sub + 1]),
            reads=[("iota9",), ("std", sub), ("bv9",)],
            writes=[("oh9",), ("stth", sub)])


def _sparsemax(em, cfg, lg, lgs, zc, zc2, sup, supn, kz, kzr, tau, wrt, srt_t,
               iotaE_s):
    SUBS, E = cfg.SUBS, cfg.E
    W = SUBS * E

    def v3(t):
        return t[:, 0:W].rearrange("p (a b) -> p a b", a=SUBS)

    em.op("V", lambda e: e.tensor_copy(lgs[:, 0:W], lg[:, 0:W]),
          reads=[("lg", s_) for s_ in range(SUBS)], writes=[("lgs",)])
    for rnd in range(E):
        if rnd % 2 == 0:
            a0, b0, npair = 0, 1, E // 2
        else:
            a0, b0, npair = 1, 2, (E - 1) // 2
        em.op("V", lambda e, a0=a0, b0=b0, npair=npair: e.tensor_tensor(
            srt_t[:, 0:SUBS * npair].rearrange("p (x y) -> p x y", x=SUBS),
            v3(lgs)[:, :, a0:a0 + 2 * npair - 1:2],
            v3(lgs)[:, :, b0:b0 + 2 * npair - 1:2], ALU.max),
            reads=[("lgs",)], writes=[("srtt",)])
        em.op("V", lambda e, a0=a0, b0=b0, npair=npair: e.tensor_tensor(
            v3(lgs)[:, :, b0:b0 + 2 * npair - 1:2],
            v3(lgs)[:, :, a0:a0 + 2 * npair - 1:2],
            v3(lgs)[:, :, b0:b0 + 2 * npair - 1:2], ALU.min),
            reads=[("lgs",)], writes=[("lgs",)])
        em.op("V", lambda e, a0=a0, npair=npair: e.tensor_copy(
            v3(lgs)[:, :, a0:a0 + 2 * npair - 1:2],
            srt_t[:, 0:SUBS * npair].rearrange("p (x y) -> p x y", x=SUBS)),
            reads=[("srtt",)], writes=[("lgs",)])
    em.op("V", lambda e: e.tensor_copy(zc[:, 0:W], lgs[:, 0:W]),
          reads=[("lgs",)], writes=[("zc",)])
    cur, oth, curk, othk = zc, zc2, "zc", "zc2"
    sh = 1
    while sh < E:
        em.op("V", lambda e, sh=sh, cur=cur, oth=oth: e.tensor_tensor(
            v3(oth)[:, :, sh:E], v3(cur)[:, :, sh:E], v3(cur)[:, :, 0:E - sh],
            ALU.add),
            reads=[(curk,)], writes=[(othk,)])
        em.op("V", lambda e, sh=sh, cur=cur, oth=oth: e.tensor_copy(
            v3(oth)[:, :, 0:sh], v3(cur)[:, :, 0:sh]),
            reads=[(curk,)], writes=[(othk,)])
        cur, oth, curk, othk = oth, cur, othk, curk
        sh *= 2
    em.op("V", lambda e: e.tensor_tensor(
        sup[:, 0:W], lgs[:, 0:W], iotaE_s[:, 0:W], ALU.mult),
        reads=[("lgs",), ("iotaE",)], writes=[("sup",)])
    em.op("V", lambda e: e.tensor_scalar(
        sup[:, 0:W], sup[:, 0:W], 1.0, None, ALU.add),
        reads=[("sup",)], writes=[("sup",)])
    em.op("V", lambda e, cur=cur: e.tensor_tensor(
        sup[:, 0:W], sup[:, 0:W], cur[:, 0:W], ALU.is_gt),
        reads=[("sup",), (curk,)], writes=[("sup",)])
    for sub in range(SUBS):
        em.op("V", lambda e, sub=sub: e.reduce_sum(
            kz[:, sub:sub + 1], sup[:, sub * E:(sub + 1) * E], AX),
            reads=[("sup",)], writes=[("kz",)])
    em.op("V", lambda e: e.memset(supn[:, 0:W], 0.0), writes=[("supn",)])
    em.op("V", lambda e: e.tensor_copy(
        v3(supn)[:, :, 0:E - 1], v3(sup)[:, :, 1:E]),
        reads=[("sup",)], writes=[("supn",)])
    em.op("V", lambda e: e.tensor_tensor(
        supn[:, 0:W], sup[:, 0:W], supn[:, 0:W], ALU.subtract),
        reads=[("sup",), ("supn",)], writes=[("supn",)])
    em.op("V", lambda e, cur=cur: e.tensor_tensor(
        supn[:, 0:W], supn[:, 0:W], cur[:, 0:W], ALU.mult),
        reads=[("supn",), (curk,)], writes=[("supn",)])
    for sub in range(SUBS):
        em.op("V", lambda e, sub=sub: e.reduce_sum(
            tau[:, sub:sub + 1], supn[:, sub * E:(sub + 1) * E], AX),
            reads=[("supn",)], writes=[("tau",)])
    em.op("V", lambda e: e.tensor_scalar(
        tau[:, 0:SUBS], tau[:, 0:SUBS], -1.0, None, ALU.add),
        reads=[("tau",)], writes=[("tau",)])
    em.op("V", lambda e: e.reciprocal(kzr[:, 0:SUBS], kz[:, 0:SUBS]),
          reads=[("kz",)], writes=[("kzr",)])
    em.op("V", lambda e: e.tensor_tensor(
        tau[:, 0:SUBS], tau[:, 0:SUBS], kzr[:, 0:SUBS], ALU.mult),
        reads=[("tau",), ("kzr",)], writes=[("tau",)])
    em.op("V", lambda e: e.tensor_tensor(
        v3(wrt), v3(lg),
        tau[:, 0:SUBS][:, :, None].broadcast_to([128, SUBS, E]),
        ALU.subtract),
        reads=[("lg", s_) for s_ in range(SUBS)] + [("tau",)],
        writes=[("wrt",)])
    em.op("V", lambda e: e.tensor_scalar(
        wrt[:, 0:W], wrt[:, 0:W], 0.0, None, ALU.max),
        reads=[("wrt",)], writes=[("wrt",)])



def kernel(L0_attn_x, L0_attn_w_gate, L0_attn_w_up, L0_attn_w_down,
           L0_ffn_W_base, L0_ffn_b_base, L0_ffn_W_router, L0_ffn_b_router,
           L0_ffn_lora_A, L0_ffn_lora_B):
    cfg = Cfg()
    xin = np.asarray(L0_attn_x)
    x = np.ascontiguousarray(xin.astype(np.float32, copy=False).reshape(-1, cfg.H))
    packed = pack_weights(
        cfg,
        np.asarray(L0_attn_w_gate, np.float32),
        np.asarray(L0_attn_w_up, np.float32),
        np.asarray(L0_attn_w_down, np.float32),
        np.asarray(L0_ffn_W_base, np.float32),
        np.asarray(L0_ffn_b_base, np.float32),
        np.asarray(L0_ffn_W_router, np.float32),
        np.asarray(L0_ffn_b_router, np.float32),
        np.asarray(L0_ffn_lora_A, np.float32),
        np.asarray(L0_ffn_lora_B, np.float32))
    nc = build(cfg)
    n = cfg.N_CORES
    per = x.shape[0] // n
    in_maps = []
    for c in range(n):
        m = dict(packed)
        m["x"] = np.ascontiguousarray(x[c * per:(c + 1) * per])
        in_maps.append(m)
    res = run_bass_kernel_spmd(nc, in_maps, core_ids=list(range(n)))
    out = np.concatenate([r["out"] for r in res.results], axis=0)
    return out.reshape(xin.shape).astype(np.float32)

